# version 21
# baseline (speedup 1.0000x reference)
import numpy as np
import ml_dtypes
from contextlib import ExitStack

import concourse.bass as bass
import concourse.mybir as mybir
from concourse.bass_utils import run_bass_kernel_spmd

F32 = mybir.dt.float32
BF16 = mybir.dt.bfloat16
FP8 = mybir.dt.float8e4
AX = mybir.AxisListType.X
ALU = mybir.AluOpType
ACT = mybir.ActivationFunctionType

S = 4096
D = 1024
H = 8
DK = 128
DR = 32
DC = 32
HID = 4096
NT = S // 128
EPS = 1e-5
SCALE = 1.0 / float(np.sqrt(DK + DR))

ENGINES = ("sync", "tensor", "vector", "scalar", "gpsimd")

WNAMES = ["W_DQ", "b_DQ", "W_UQ", "b_UQ", "W_DKV", "b_DKV", "W_UK", "b_UK",
          "W_UV", "b_UV", "W_QR", "b_QR", "W_KR", "b_KR", "W_O", "b_O",
          "ln1_g", "ln1_b", "w1", "b1", "ln2_g", "ln2_b", "w2", "b2"]


class Prog:

    def __init__(self, sems):
        self.items = {e: [] for e in ENGINES}
        self.sems = sems
        self.cnt = {k: 0 for k in sems}
        self.nbar = 0

    def wait(self, eng, sem, val=None):
        v = self.cnt[sem] if val is None else val
        h = self.sems[sem]
        self.items[eng].append(lambda e, h=h, v=v: e.wait_ge(h, v))

    AUTOSEM = {"vector": "v", "scalar": "s"}

    def op(self, eng, fn, inc=None, n=1):
        auto = self.AUTOSEM.get(eng)
        if auto is not None:
            assert inc in (None, auto), (eng, inc)
            self.wait(eng, auto)
            inc = auto
        if inc is None:
            self.items[eng].append(fn)
            return None
        h = self.sems[inc]
        self.cnt[inc] += n

        def run(e, fn=fn, h=h, n=n):
            fn(e).then_inc(h, n)
        self.items[eng].append(run)
        return self.cnt[inc]

    def dma(self, eng, out, in_, sem):
        return self.op(eng, lambda e, o=out, i=in_: e.dma_start(out=o, in_=i),
                       inc=sem, n=16)

    def barrier(self):
        for s in ("dl", "dx0", "dx1", "dw0", "dw1"):
            self.wait("sync", s)
        for s in ("dg", "ds0", "ds1"):
            self.wait("gpsimd", s)
        self.wait("tensor", "t")
        self.wait("vector", "v")
        self.wait("scalar", "s")
        self.nbar += 1
        tgt = 5 * self.nbar
        h = self.sems["bar"]
        self.cnt["bar"] = tgt
        for e in ENGINES:
            self.items[e].append(lambda eng, h=h: eng.sem_inc(h, 1))
            self.items[e].append(lambda eng, h=h, t=tgt: eng.wait_ge(h, t))


def build_program(dbg=False):
    nc = bass.Bass("TRN2", target_bir_lowering=False)

    d = {"x": nc.dram_tensor("x", [S, D], F32, kind="ExternalInput").ap()}
    shapes = {"W_DQ": [D, DC], "b_DQ": [DC], "W_UQ": [DC, H * DK],
              "b_UQ": [H * DK], "W_DKV": [D, DC], "b_DKV": [DC],
              "W_UK": [DC, H * DK], "b_UK": [H * DK], "W_UV": [DC, H * DK],
              "b_UV": [H * DK], "W_QR": [DC, H * DR], "b_QR": [H * DR],
              "W_KR": [D, DR], "b_KR": [DR], "W_O": [H * DK, D], "b_O": [D],
              "ln1_g": [D], "ln1_b": [D], "w1": [HID, D], "b1": [HID],
              "ln2_g": [HID], "ln2_b": [HID], "w2": [D, HID], "b2": [D]}
    for nm in WNAMES:
        d[nm] = nc.dram_tensor(nm, shapes[nm], F32, kind="ExternalInput").ap()
    d["idtb"] = nc.dram_tensor("idtb", [128, 128], BF16, kind="ExternalInput").ap()
    d["idtf"] = nc.dram_tensor("idtf", [128, 128], F32, kind="ExternalInput").ap()
    d["y"] = nc.dram_tensor("y", [S, D], F32, kind="ExternalOutput").ap()
    d["x2s"] = nc.dram_tensor("x2s", [S, D], F32).ap()
    dbg_t = {}
    if dbg:
        for nm, shp, dt in [
            ("o_cq", [33, S], BF16), ("o_ckv", [33, S], BF16),
            ("o_krh", [33, S], BF16), ("o_qc", [128, S], BF16),
            ("o_kc", [128, S], BF16), ("o_vt", [128, S], BF16),
            ("o_qrh", [33, S], BF16), ("o_oT", [128, H * S], BF16),
            ("o_x2", [S, D], F32), ("o_w1t", [128, 16384], BF16),
            ("o_w2t", [128, 16384], BF16), ("o_h2", [128, 32 * 256], BF16),
            ("o_ppx", [128, 8], F32), ("o_b1t", [128, 32], F32),
            ("o_zt", [128, 2048], BF16), ("o_gg", [128, 32 * 256], BF16),
        ]:
            dbg_t[nm] = nc.dram_tensor(nm, shp, dt, kind="ExternalOutput").ap()

    es = ExitStack()
    with es:
        def sb(name, shape, dt):
            return es.enter_context(nc.sbuf_tensor(name, shape, dt))

        bigA = sb("bigA", [128, 32768], BF16)
        bigB = sb("bigB", [128, 32768], BF16)
        bigC = sb("bigC", [128, 8192], BF16)
        stg = sb("stg", [128, 2048], F32)
        srow_f = sb("srow_f", [128, 1024], F32)
        xbuf = sb("xbuf", [128, 2048], F32)
        x2w2 = sb("x2w2", [128, 2048], F32)
        xm = sb("xm", [128, 1024], F32)
        stbuf = sb("stbuf", [128, 16], F32)
        outr = sb("outr", [128, 2048], F32)
        xlnb = sb("xlnb", [128, 1024], BF16)
        bcat = sb("bcat", [96, 1], F32)
        WUQ = sb("WUQ", [128, 1024], BF16)
        WUK = sb("WUK", [128, 1024], BF16)
        WUV = sb("WUV", [128, 1024], BF16)
        idtb = sb("idtb_s", [128, 128], BF16)
        idtf = sb("idtf_s", [128, 128], F32)
        onesb = sb("onesb", [128, 128], BF16)
        onesf = sb("onesf", [128, 128], F32)
        b1p = sb("b1p", [128, 32], F32)
        g2p = sb("g2p", [128, 32], F32)
        b2p = sb("b2p", [128, 32], F32)
        ppx = sb("ppx", [128, 8], F32)
        wabs = sb("wabs", [128, 64], F32)
        g1pp = sb("g1pp", [128, 8], F32)
        b1spp = sb("b1spp", [128, 8], F32)
        b1sbf = sb("b1sbf", [128, 8], BF16)
        b1tot = sb("b1tot", [128, 32], F32)
        b2b = sb("b2b", [128, 1024], F32)
        g1b = sb("g1b", [128, 1024], F32)
        zTw = sb("zTw", [128, 1024], BF16)
        gsq = sb("gsq", [128, 1024], BF16)

        Wcat = bigB[:, 31744:32512]
        WQR = bigB[:, 32512:32768]
        nhalf = srow_f[0:1, 0:512]
        Qc = bigB[:, 0:4096]
        Kc = bigB[:, 4096:8192]
        Vt = bigB[:, 8192:12288]
        Qrh = bigB[:, 12288:16384]
        cq = bigB[:, 16384:20480]
        ckv = bigB[:, 20480:24576]
        krh = bigB[:, 24576:28672]
        PT = bigB[:, 28672:30720]
        q2s = bigB[:, 30720:31744]
        WO = bigC[:, 0:8192]
        bO_b = x2w2[:, 1024:2048]
        mu2b = stg[:, 1024:1280].bitcast(BF16)
        rs2b = stg[:, 1280:1536].bitcast(BF16)

        psA = es.enter_context(nc.psum_tensor("psA", [128, 1024], F32))
        psB = es.enter_context(nc.psum_tensor("psB", [128, 1024], F32))
        psC = es.enter_context(nc.psum_tensor("psC", [128, 1024], F32))
        psD = es.enter_context(nc.psum_tensor("psD", [128, 1024], F32))

        sems = {k: es.enter_context(nc.semaphore(k))
                for k in ("dl", "dg", "dx0", "dx1", "ds0", "ds1",
                          "dw0", "dw1", "t", "v", "s", "bar")}
        p = Prog(sems)

        st_mu = stbuf[:, 0:1]
        st_u = stbuf[:, 1:2]
        st_s0 = stbuf[:, 2:3]
        st_r0 = stbuf[:, 3:4]
        st_t1 = stbuf[:, 4:5]
        st_rs = stbuf[:, 5:6]
        st_ss = stbuf[:, 6:7]

        def rsqrt_chain(u_ap, s0_ap, r0_ap, t1_ap, rs_ap):
            p.wait("scalar", "v")
            p.op("scalar", lambda e: e.activation(s0_ap, u_ap, ACT.Sqrt), inc="s")
            p.wait("vector", "s")
            p.op("vector", lambda e: e.reciprocal(rs_ap, s0_ap), inc="v")

        def ln_stats(xin_ap, ncols, dump=None):
            if dump is None:
                dump = stg[:, 0:ncols]
            p.op("vector", lambda e: e.reduce_sum(st_mu, xin_ap, AX))
            p.op("vector", lambda e: e.tensor_scalar(
                st_mu, st_mu, 1.0 / ncols, None, ALU.mult))
            p.op("vector", lambda e: e.tensor_scalar(
                xm[:, 0:ncols], xin_ap, st_mu, None, ALU.subtract), inc="v")
            p.wait("scalar", "v")
            p.op("scalar", lambda e, dump=dump: e.activation(
                dump, xm[:, 0:ncols], ACT.Square,
                accum_out=st_ss), inc="s")
            rt = p.op("scalar", lambda e, ncols=ncols: e.activation(
                st_s0, st_ss, ACT.Sqrt, scale=1.0 / ncols, bias=wabs[:, 63:64]), inc="s")
            p.wait("vector", "s", rt)
            p.op("vector", lambda e: e.reciprocal(st_rs, st_s0), inc="v")

        p.dma("sync", idtb[:, :], d["idtb"][:, :], "dl")
        p.dma("sync", idtf[:, :], d["idtf"][:, :], "dl")
        p.op("vector", lambda e: e.memset(ppx[:, :], 0.0))
        p.op("vector", lambda e: e.memset(wabs[:, 63:64], EPS))
        p.op("vector", lambda e: e.memset(bigB[32:64, 12288:28672], 0.0))
        p.op("vector", lambda e: e.memset(bigB[64:128, 12288:28672], 0.0))
        p.op("vector", lambda e: e.memset(bigB[32:64, 31232:31744], 0.0))
        p.op("vector", lambda e: e.memset(bigB[64:128, 31232:31744], 0.0))
        p.op("vector", lambda e: e.memset(bigB[32:64, 32512:32768], 0.0))
        p.op("vector", lambda e: e.memset(bigB[64:128, 32512:32768], 0.0))
        p.op("vector", lambda e: e.memset(WUQ[32:64, :], 0.0))
        p.op("vector", lambda e: e.memset(WUQ[64:128, :], 0.0))
        p.op("vector", lambda e: e.memset(WUK[32:64, :], 0.0))
        p.op("vector", lambda e: e.memset(WUK[64:128, :], 0.0))
        p.op("vector", lambda e: e.memset(WUV[32:64, :], 0.0))
        p.op("vector", lambda e: e.memset(WUV[64:128, :], 0.0))
        p.op("vector", lambda e: e.memset(srow_f[:, :], 0.0))
        p.op("vector", lambda e: e.memset(onesb[:, :], 1.0))
        p.op("vector", lambda e: e.memset(onesf[:, :], 1.0))
        p.op("vector", lambda e: e.memset(nhalf[:, :], -0.5))
        p.op("vector", lambda e: e.memset(cq[32:33, :], 1.0))
        p.op("vector", lambda e: e.memset(ckv[32:33, :], 1.0))
        p.op("vector", lambda e: e.memset(krh[32:33, :], 1.0), inc="v")

        for ci in range(8):
            p.dma("gpsimd", stg[:, ci * 32:(ci + 1) * 32],
                  d["W_DQ"][ci * 128:(ci + 1) * 128, :], "dg")
            p.dma("gpsimd", stg[:, 256 + ci * 32:256 + (ci + 1) * 32],
                  d["W_DKV"][ci * 128:(ci + 1) * 128, :], "dg")
            p.dma("gpsimd", stg[:, 512 + ci * 32:512 + (ci + 1) * 32],
                  d["W_KR"][ci * 128:(ci + 1) * 128, :], "dg")
        p.dma("gpsimd", bcat[0:32, :], d["b_DQ"].rearrange("(a b) -> a b", b=1), "dg")
        p.dma("gpsimd", bcat[32:64, :], d["b_DKV"].rearrange("(a b) -> a b", b=1), "dg")
        p.dma("gpsimd", bcat[64:96, :], d["b_KR"].rearrange("(a b) -> a b", b=1), "dg")
        p.wait("vector", "dg")
        for ci in range(8):
            for k, base in ((0, 0), (1, 256), (2, 512)):
                p.op("vector", lambda e, ci=ci, k=k, base=base: e.tensor_copy(
                    Wcat[:, ci * 96 + k * 32:ci * 96 + (k + 1) * 32],
                    stg[:, base + ci * 32:base + (ci + 1) * 32]))
        vfree = p.op("vector", lambda e: e.memset(stbuf[:, 15:16], 0.0),
                     inc="v")

        for wnm, bnm, dst, w in [("W_UQ", "b_UQ", WUQ, 1024),
                                 ("W_UK", "b_UK", WUK, 1024),
                                 ("W_UV", "b_UV", WUV, 1024),
                                 ("W_QR", "b_QR", WQR, 256)]:
            p.wait("gpsimd", "v", vfree)
            p.dma("gpsimd", stg[0:32, 0:w], d[wnm][:, :], "dg")
            p.dma("gpsimd", stg[32:33, 0:w],
                  d[bnm].rearrange("(a b) -> b a", b=1), "dg")
            p.wait("vector", "dg")
            vfree = p.op("vector", lambda e, dst=dst, w=w: e.tensor_copy(
                dst[0:33, 0:w], stg[0:33, 0:w]), inc="v")

        p.wait("tensor", "dl")
        for src, dst, nr in [("b1", b1p, 32), ("ln2_g", g2p, 32),
                             ("ln2_b", b2p, 32), ("ln1_g", g1pp, 8),
                             ("ln1_b", b1spp, 8)]:
            p.wait("gpsimd", "v", vfree)
            p.dma("gpsimd", stg[0:nr, 0:128],
                  d[src].rearrange("(a b) -> a b", b=128), "dg")
            p.wait("tensor", "dg")
            p.op("tensor", lambda e, nr=nr: e.matmul(
                psD[:, 0:nr], stg[0:nr, 0:128], idtf[0:nr, 0:nr],
                start=True, stop=True), inc="t")
            p.wait("vector", "t")
            vfree = p.op("vector", lambda e, dst=dst, nr=nr: e.tensor_copy(
                dst[:, 0:nr], psD[:, 0:nr]), inc="v")

        p.wait("tensor", "dl")
        xlT = bigC[:, 0:4096]
        xm_p = [xm[:, 0:1024], x2w2[:, 1024:2048]]
        dump_p = [stg[:, 0:1024], stg[:, 1024:2048]]
        xlnb_p = [xlnb[:, 0:1024], zTw[:, 0:1024].bitcast(BF16)]
        stb = [stbuf[:, 0:8], stbuf[:, 8:16]]
        v_xfree = {}
        v_sub = {}
        v_u = {}
        v_cp = {}
        s_sq = {}
        s_rt = {}
        s_vx = {}
        t_tr = {}
        t_dn = 0
        for i in range(NT + 1):
            j = i - 1
            if i < NT:
                par = i % 2
                dxs = "dx0" if par == 0 else "dx1"
                if i >= 2:
                    p.wait("sync", "v", v_xfree[i - 2])
                p.dma("sync", xbuf[:, par * 1024:(par + 1) * 1024],
                      d["x"][i * 128:(i + 1) * 128, :], dxs)
            if i < NT:
                par = i % 2
                sb_i = stb[par]
                dxs = "dx0" if par == 0 else "dx1"
                p.wait("vector", dxs)
                if i >= 2 and (i - 2) in s_vx:
                    p.wait("vector", "s", s_vx[i - 2])
                p.op("vector", lambda e, par=par, sb_i=sb_i: e.reduce_sum(
                    sb_i[:, 0:1], xbuf[:, par * 1024:(par + 1) * 1024], AX))
                p.op("vector", lambda e, sb_i=sb_i: e.tensor_scalar(
                    sb_i[:, 0:1], sb_i[:, 0:1], 1.0 / 1024, None, ALU.mult))
                v_sub[i] = p.op("vector", lambda e, par=par, sb_i=sb_i:
                                e.tensor_scalar(
                                    xm_p[par], xbuf[:, par * 1024:(par + 1) * 1024],
                                    sb_i[:, 0:1], None, ALU.subtract), inc="v")
                v_xfree[i] = v_sub[i]
                p.wait("scalar", "v", v_sub[i])
                p.op("scalar", lambda e, par=par, sb_i=sb_i:
                     e.activation(dump_p[par], xm_p[par], ACT.Square,
                                  accum_out=sb_i[:, 6:7]), inc="s")
                s_rt[i] = p.op("scalar", lambda e, sb_i=sb_i: e.activation(
                    sb_i[:, 2:3], sb_i[:, 6:7], ACT.Sqrt,
                    scale=1.0 / 1024, bias=wabs[:, 63:64]), inc="s")
            if j >= 0:
                pj = j % 2
                sb_j = stb[pj]
                p.wait("vector", "s", s_rt[j])
                vrj = p.op("vector", lambda e, sb_j=sb_j: e.reciprocal(
                    sb_j[:, 5:6], sb_j[:, 2:3]), inc="v")
                p.wait("scalar", "v", vrj)
                vx = p.op("scalar", lambda e, pj=pj, sb_j=sb_j: e.mul(
                    xlnb_p[pj], xm_p[pj], sb_j[:, 5:6]), inc="s")
                s_vx[j] = vx
                p.wait("tensor", "s", vx)
                if j >= 1:
                    p.wait("tensor", "v", v_cp[j - 1])
                if j % 4 == 0 and j >= 4:
                    p.wait("tensor", "t", t_dn)
                for c in range(8):
                    t_tr[j] = p.op("tensor", lambda e, c=c, pj=pj: e.matmul(
                        psD[:, c * 128:(c + 1) * 128],
                        xlnb_p[pj][:, c * 128:(c + 1) * 128], idtb[:, :],
                        start=True, stop=True),
                        inc="t" if c == 7 else None)
                p.wait("scalar", "t", t_tr[j])
                if j % 4 == 0 and j >= 4:
                    p.wait("scalar", "t", t_dn)
                jw = j % 4
                v_cp[j] = p.op("scalar", lambda e, jw=jw: e.copy(
                    bass.AP(bigC, jw * 128, [[8192, 128], [512, 8], [1, 128]]),
                    psD[:, :].rearrange("p (c w) -> p c w", c=8)), inc="s")
                if j % 4 == 3:
                    w = j // 4
                    p.wait("tensor", "s", v_cp[j])
                    for c in range(8):
                        t_dn = p.op("tensor", lambda e, c=c: e.matmul(
                            psC[0:96, 0:512], Wcat[:, c * 96:(c + 1) * 96],
                            xlT[:, c * 512:(c + 1) * 512],
                            start=(c == 0), stop=(c == 7)),
                            inc="t" if c == 7 else None)
                    p.wait("vector", "t", t_dn)
                    for r0, dst in ((0, cq), (32, ckv), (64, krh)):
                        p.op("vector", lambda e, r0=r0, dst=dst, w=w:
                             e.tensor_scalar(
                                 dst[0:32, w * 512:(w + 1) * 512],
                                 psC[r0:r0 + 32, 0:512], bcat[r0:r0 + 32, :],
                                 None, ALU.add), inc="v")
        if dbg:
            p.barrier()
            for nm, src in [("o_cq", cq), ("o_ckv", ckv), ("o_krh", krh)]:
                p.dma("gpsimd", dbg_t[nm][:, :], src[0:33, :], "dg")
        p.barrier()

        p1_vq = {}

        def emit_pass1(js):
            for j in js:
                sl = (j % 2) * 1024
                if j >= 2:
                    p.wait("sync", "v", p1_vq[j - 2])
                if j < 32:
                    srcap = d["w1"][j * 128:(j + 1) * 128, :]
                    col = j
                else:
                    i2, q2 = (j - 32) // 4, (j - 32) % 4
                    srcap = d["w2"][i2 * 128:(i2 + 1) * 128,
                                    q2 * 1024:(q2 + 1) * 1024]
                    col = 32 + (j - 32) // 2
                dxs = "dx0" if j % 2 == 0 else "dx1"
                p.dma("sync", stg[:, sl:sl + 1024], srcap, dxs)
                p.wait("vector", dxs)
                p.op("vector", lambda e, j=j, sl=sl, col=col: e.reduce_sum(
                    wabs[:, col:col + 1] if j < 32 else stbuf[:, 8:9],
                    stg[:, sl:sl + 1024], AX,
                    apply_absolute_value=True), inc="v")
                if j < 32:
                    p1_vq[j] = p.cnt["v"]
                elif (j - 32) % 2 == 0:
                    p1_vq[j] = p.op("vector", lambda e, col=col: e.tensor_copy(
                        wabs[:, col:col + 1], stbuf[:, 8:9]), inc="v")
                else:
                    p1_vq[j] = p.op("vector", lambda e, col=col: e.tensor_tensor(
                        wabs[:, col:col + 1], wabs[:, col:col + 1],
                        stbuf[:, 8:9], ALU.add), inc="v")

        gkc = 0
        smark = {}
        expmark = {}
        pvmark = {}
        evacmark = 0

        def emit_pv(kc_g, kc):
            p.wait("tensor", "s", expmark[kc_g])
            first = (kc == 0)
            last = (kc == 31)
            if first:
                p.wait("tensor", "v", evacmark)
            par = (kc_g % 2) * 1024
            for hf in range(2):
                p.op("tensor", lambda e, kc=kc, hf=hf, par=par, first=first,
                     last=last: e.matmul(
                         psC[:, hf * 512:(hf + 1) * 512],
                         Vt[:, kc * 128:(kc + 1) * 128],
                         PT[:, par + hf * 512:par + (hf + 1) * 512],
                         start=first, stop=last))
            for hf in range(2):
                pvmark[kc_g] = p.op(
                    "tensor", lambda e, hf=hf, par=par, first=first, last=last:
                    e.matmul(psD[:, hf * 512:(hf + 1) * 512], onesb[:, :],
                             PT[:, par + hf * 512:par + (hf + 1) * 512],
                             start=first, stop=last),
                    inc="t" if hf == 1 else None)

        deferred_norm = []
        p1_queue = list(range(64))
        for h in range(H):
            if gkc > 0:
                p.wait("tensor", "s", expmark[gkc - 1])
                p.wait("tensor", "v", evacmark)
            vm = {}
            sm = {}
            for di, (dst, wt, src) in enumerate(
                    ((Qc, WUQ, cq), (Kc, WUK, ckv))):
                for g4 in range(4):
                    k = di * 4 + g4
                    ps = psA if k % 2 == 0 else psB
                    if k >= 2:
                        p.wait("tensor", "v", vm.get(k - 2, 0))
                        p.wait("tensor", "s", sm.get(k - 2, 0))
                    for half in range(2):
                        tm = p.op("tensor", lambda e, wt=wt, src=src, g4=g4,
                                  ps=ps, h=h, half=half: e.matmul(
                                      ps[:, half * 512:(half + 1) * 512],
                                      wt[:, h * 128:(h + 1) * 128],
                                      src[:, g4 * 1024 + half * 512:
                                          g4 * 1024 + (half + 1) * 512],
                                      start=True, stop=True),
                                  inc="t" if half == 1 else None)
                    p.wait("vector", "t", tm)
                    vm[k] = p.op("vector", lambda e, dst=dst, g4=g4, ps=ps:
                                 e.tensor_copy(
                                     dst[:, g4 * 1024:g4 * 1024 + 512],
                                     ps[:, 0:512]), inc="v")
                    p.wait("scalar", "t", tm)
                    sm[k] = p.op("scalar", lambda e, dst=dst, g4=g4, ps=ps:
                                 e.copy(dst[:, g4 * 1024 + 512:(g4 + 1) * 1024],
                                        ps[:, 512:1024]), inc="s")
            for g8 in range(8):
                k = 8 + g8
                ps = psA if k % 2 == 0 else psB
                p.wait("tensor", "v", vm.get(k - 2, 0))
                p.wait("tensor", "s", sm.get(k - 2, 0))
                tm = p.op("tensor", lambda e, g8=g8, ps=ps, h=h: e.matmul(
                    ps[0:32, 0:512], WQR[:, h * 32:(h + 1) * 32],
                    cq[:, g8 * 512:(g8 + 1) * 512],
                    start=True, stop=True), inc="t")
                p.wait("vector", "t", tm)
                vm[k] = p.op("vector", lambda e, g8=g8, ps=ps: e.tensor_copy(
                    Qrh[0:32, g8 * 512:(g8 + 1) * 512], ps[0:32, 0:512]),
                    inc="v")
                sm[k] = sm.get(k - 1, 0)
            vmv = {}
            for vt in range(NT):
                slot = vt % 8
                ps = psA if slot < 4 else psB
                off = 512 + (slot % 4) * 128
                if vt >= 8 and vt % 4 == 0:
                    p.wait("tensor", "v", vmv[vt - 8])
                tm = p.op("tensor", lambda e, vt=vt, ps=ps, off=off, h=h:
                    e.matmul(ps[:, off:off + 128],
                             ckv[:, vt * 128:(vt + 1) * 128],
                             WUV[:, h * 128:(h + 1) * 128],
                             start=True, stop=True),
                    inc="t" if vt % 4 == 3 else None)
                if vt % 4 == 3:
                    p.wait("vector", "t", tm)
                    vmv[vt - 3] = p.op(
                        "vector", lambda e, vt=vt, ps=ps: e.tensor_copy(
                            Vt[:, (vt - 3) * 128:(vt + 1) * 128],
                            ps[:, 512:1024]), inc="v")
            for g8 in range(8):
                vmk = p.op("vector", lambda e, g8=g8: e.tensor_mul(
                    q2s[:, 0:512], Kc[:, g8 * 512:(g8 + 1) * 512],
                    Kc[:, g8 * 512:(g8 + 1) * 512]))
                vmk = p.op("vector", lambda e, g8=g8: e.tensor_mul(
                    q2s[0:32, 512:1024], krh[0:32, g8 * 512:(g8 + 1) * 512],
                    krh[0:32, g8 * 512:(g8 + 1) * 512]), inc="v")
                p.wait("tensor", "v", vmk)
                tm = p.op("tensor", lambda e: e.matmul(
                    psD[0:1, 0:512], onesb[:, 0:1], q2s[:, 0:512],
                    start=True, stop=False))
                tm = p.op("tensor", lambda e: e.matmul(
                    psD[0:1, 0:512], onesb[:, 0:1], q2s[:, 512:1024],
                    start=False, stop=True), inc="t")
                p.wait("vector", "t", tm)
                p.op("vector", lambda e, g8=g8: e.reduce_max(
                    stbuf[0:1, 8 + g8:9 + g8], psD[0:1, 0:512], AX), inc="v")
            p.op("vector", lambda e: e.reduce_max(
                stbuf[0:1, 7:8], stbuf[0:1, 8:16], AX), inc="v")
            for g8 in range(8):
                vmk = p.op("vector", lambda e, g8=g8: e.tensor_mul(
                    q2s[:, 0:512], Qc[:, g8 * 512:(g8 + 1) * 512],
                    Qc[:, g8 * 512:(g8 + 1) * 512]))
                vmk = p.op("vector", lambda e, g8=g8: e.tensor_mul(
                    q2s[0:32, 512:1024], Qrh[0:32, g8 * 512:(g8 + 1) * 512],
                    Qrh[0:32, g8 * 512:(g8 + 1) * 512]), inc="v")
                p.wait("tensor", "v", vmk)
                tm = p.op("tensor", lambda e: e.matmul(
                    psD[0:1, 0:512], onesb[:, 0:1], q2s[:, 0:512],
                    start=True, stop=False))
                tm = p.op("tensor", lambda e: e.matmul(
                    psD[0:1, 0:512], onesb[:, 0:1], q2s[:, 512:1024],
                    start=False, stop=True), inc="t")
                p.wait("vector", "t", tm)
                p.op("vector", lambda e, g8=g8: e.scalar_tensor_tensor(
                    Qrh[32:33, g8 * 512:(g8 + 1) * 512],
                    psD[0:1, 0:512], stbuf[0:1, 7:8], nhalf[:, 0:512],
                    ALU.add, ALU.mult), inc="v")
            prep_v = p.cnt["v"]
            if dbg and h == 0:
                p.barrier()
                for nm, src in [("o_qc", Qc), ("o_kc", Kc), ("o_vt", Vt),
                                ("o_qrh", Qrh[0:33, :])]:
                    p.dma("gpsimd", dbg_t[nm][:, :], src, "dg")
                p.barrier()

            for g in range(4):
                q0 = g * 1024
                for kc in range(32):
                    if deferred_norm and kc in (2, 3):
                        deferred_norm.pop(0)()
                    if p1_queue and kc in (6, 10, 14, 18, 22, 26):
                        emit_pass1([p1_queue.pop(0)])
                    if gkc >= 2:
                        p.wait("tensor", "s", expmark[gkc - 2])
                    if kc == 0 and g == 0:
                        p.wait("tensor", "v", prep_v)
                    ps = psA if (gkc % 2 == 0) else psB
                    for hf in range(2):
                        p.op("tensor", lambda e, kc=kc, hf=hf, ps=ps, q0=q0:
                             e.matmul(ps[:, hf * 512:(hf + 1) * 512],
                                      Kc[:, kc * 128:(kc + 1) * 128],
                                      Qc[:, q0 + hf * 512:q0 + (hf + 1) * 512],
                                      start=True, stop=False))
                        smark[gkc] = p.op(
                            "tensor", lambda e, kc=kc, hf=hf, ps=ps, q0=q0:
                            e.matmul(ps[:, hf * 512:(hf + 1) * 512],
                                     krh[:, kc * 128:(kc + 1) * 128],
                                     Qrh[:, q0 + hf * 512:q0 + (hf + 1) * 512],
                                     start=False, stop=True),
                            inc="t" if hf == 1 else None)
                    p.wait("scalar", "t", smark[gkc])
                    if gkc >= 2:
                        p.wait("scalar", "t", pvmark[gkc - 2])
                    par = (gkc % 2) * 1024
                    expmark[gkc] = p.op(
                        "scalar", lambda e, ps=ps, par=par: e.activation(
                            PT[:, par:par + 1024], ps[:, :], ACT.Exp,
                            scale=SCALE), inc="s")
                    if kc >= 1:
                        emit_pv(gkc - 1, kc - 1)
                    gkc += 1
                emit_pv(gkc - 1, 31)
                p.wait("vector", "t", pvmark[gkc - 1])
                p.op("vector", lambda e: e.tensor_copy(x2w2[:, 0:1024],
                                                       psD[:, :]))
                evacmark = p.op("vector", lambda e: e.tensor_copy(
                    outr[:, 0:1024], psC[:, :]), inc="v")

                def _dn1():
                    p.op("vector", lambda e: e.reciprocal(
                        x2w2[:, 1024:2048], x2w2[:, 0:1024]))

                def _dn2(h=h, q0=q0):
                    p.op("vector", lambda e, h=h, q0=q0: e.tensor_tensor(
                        bigA[:, h * 4096 + q0:h * 4096 + q0 + 1024],
                        outr[:, 0:1024], x2w2[:, 1024:2048], ALU.mult),
                        inc="v")
                deferred_norm += [_dn1, _dn2]
        for fn in deferred_norm:
            fn()
        deferred_norm = []
        if p1_queue:
            emit_pass1(list(p1_queue))
            p1_queue = []
        if dbg:
            p.barrier()
            p.dma("gpsimd", dbg_t["o_oT"][:, :], bigA[:, :], "dg")
        p.barrier()

        vfree = p.cnt["v"]
        for ci in range(4):
            if ci >= 1:
                p.wait("sync", "v", vfree)
            p.dma("sync", stg[:, 0:1024],
                  d["W_O"][(2 * ci) * 128:(2 * ci + 1) * 128, :], "dl")
            p.dma("sync", stg[:, 1024:2048],
                  d["W_O"][(2 * ci + 1) * 128:(2 * ci + 2) * 128, :], "dl")
            p.wait("vector", "dl")
            vfree = p.op("vector", lambda e, ci=ci: e.tensor_copy(
                WO[:, (2 * ci) * 1024:(2 * ci + 2) * 1024], stg[:, :]), inc="v")
        p.wait("sync", "v", vfree)
        p.dma("sync", srow_f[0:1, 0:1024],
              d["b_O"].rearrange("(a b) -> b a", b=1), "dl")
        p.wait("tensor", "dl")
        p.wait("tensor", "v", vfree)
        for hf in range(2):
            tm = p.op("tensor", lambda e, hf=hf: e.matmul(
                psC[:, hf * 512:(hf + 1) * 512], onesf[:, :],
                srow_f[:, hf * 512:(hf + 1) * 512], start=True, stop=True),
                inc="t" if hf == 1 else None)
        p.wait("vector", "t", tm)
        p.op("vector", lambda e: e.tensor_copy(bO_b[:, 0:1024], psC[:, :]),
             inc="v")
        for k, (c0, c1, n_el) in enumerate([(0, 32, HID * D), (32, 48, D * HID)]):
            p.op("vector", lambda e, c0=c0, c1=c1: e.reduce_sum(
                stbuf[:, 0:1], wabs[:, c0:c1], AX), inc="v")
            p.wait("tensor", "v")
            tm = p.op("tensor", lambda e: e.matmul(
                psB[0:1, 0:128], stbuf[:, 0:1], idtf[:, :],
                start=True, stop=True), inc="t")
            p.wait("vector", "t", tm)
            p.op("vector", lambda e: e.reduce_sum(
                srow_f[0:1, 0:1], psB[0:1, 0:128], AX))
            p.op("vector", lambda e, n_el=n_el: e.tensor_scalar(
                srow_f[0:1, 0:1], srow_f[0:1, 0:1], 1.0 / n_el, 1e-5,
                ALU.mult, ALU.max), inc="v")
            p.wait("tensor", "v")
            tm = p.op("tensor", lambda e: e.matmul(
                psB[:, 128:129], onesf[:, :], srow_f[:, 0:1],
                start=True, stop=True), inc="t")
            p.wait("vector", "t", tm)
            p.op("vector", lambda e, k=k: e.tensor_copy(
                ppx[:, 3 * k:3 * k + 1], psB[:, 128:129]))
            p.op("vector", lambda e, k=k: e.tensor_scalar(
                ppx[:, 3 * k + 1:3 * k + 2], ppx[:, 3 * k:3 * k + 1], 0.5,
                None, ALU.mult))
            p.op("vector", lambda e, k=k: e.tensor_scalar(
                ppx[:, 3 * k + 2:3 * k + 3], ppx[:, 3 * k:3 * k + 1], -0.5,
                None, ALU.mult), inc="v")
        p.wait("sync", "v")
        p.dma("sync", srow_f[0:1, 0:1024],
              d["ln1_g"].rearrange("(a b) -> b a", b=1), "dl")
        p.wait("tensor", "dl")
        for hf in range(2):
            tm = p.op("tensor", lambda e, hf=hf: e.matmul(
                psB[:, hf * 512:(hf + 1) * 512], onesf[:, :],
                srow_f[:, hf * 512:(hf + 1) * 512],
                start=True, stop=True), inc="t" if hf == 1 else None)
        p.wait("vector", "t", tm)
        p.op("vector", lambda e: e.tensor_scalar(
            g1b[:, :], psB[:, :], ppx[:, 0:1], None, ALU.mult), inc="v")
        p.op("vector", lambda e: e.tensor_scalar(
            g2p[:, :], g2p[:, :], ppx[:, 3:4], None, ALU.mult))
        p.op("vector", lambda e: e.tensor_scalar(
            b2p[:, :], b2p[:, :], ppx[:, 3:4], None, ALU.mult))
        p.op("vector", lambda e: e.tensor_scalar(
            b1spp[:, :], b1spp[:, :], ppx[:, 0:1], None, ALU.mult), inc="v")
        p.wait("sync", "v")
        p.dma("sync", srow_f[0:1, 0:1024],
              d["b2"].rearrange("(a b) -> b a", b=1), "dl")
        p.wait("tensor", "dl")
        for hf in range(2):
            tm = p.op("tensor", lambda e, hf=hf: e.matmul(
                psB[:, hf * 512:(hf + 1) * 512], onesf[:, :],
                srow_f[:, hf * 512:(hf + 1) * 512],
                start=True, stop=True), inc="t" if hf == 1 else None)
        p.wait("vector", "t", tm)
        p.op("vector", lambda e: e.tensor_copy(b2b[:, :], psB[:, :]), inc="v")
        p.barrier()

        p.op("vector", lambda e: e.reduce_sum(wabs[:, 62:63], bO_b[:, 0:1024],
                                              AX))
        p.op("vector", lambda e: e.tensor_scalar(
            wabs[:, 62:63], wabs[:, 62:63], 1.0 / 1024, None, ALU.mult),
            inc="v")

        def emit_w2_tile(jj):
            i2, q2 = jj // 4, jj % 4
            dws = "dw0" if jj % 2 == 0 else "dw1"
            sl = 0
            if jj >= 1:
                p.wait("sync", "v", w2q[jj - 1])
            p.dma("sync", stg[:, sl:sl + 1024],
                  d["w2"][i2 * 128:(i2 + 1) * 128,
                          q2 * 1024:(q2 + 1) * 1024], dws)
            p.wait("vector", dws)
            if jj >= 1:
                p.wait("vector", "t", w2t[jj - 1])
            p.op("vector", lambda e, sl=sl: e.tensor_scalar(
                xlnb[:, :], stg[:, sl:sl + 1024], ppx[:, 5:6], None,
                ALU.is_lt))
            zsl = 0
            w2q[jj] = p.op("vector", lambda e, sl=sl, zsl=zsl:
                           e.scalar_tensor_tensor(
                               zTw[:, zsl:zsl + 1024], stg[:, sl:sl + 1024],
                               ppx[:, 4:5], xlnb[:, :],
                               ALU.is_gt, ALU.subtract), inc="v")
            p.wait("tensor", "v", w2q[jj])
            if jj >= 1:
                p.wait("tensor", "s", w2c[jj - 1])
            for c in range(8):
                w2t[jj] = p.op("tensor", lambda e, c=c, zsl=zsl: e.matmul(
                    psD[:, c * 128:(c + 1) * 128],
                    zTw[:, zsl + c * 128:zsl + (c + 1) * 128], idtb[:, :],
                    start=True, stop=True), inc="t" if c == 7 else None)
            p.wait("scalar", "t", w2t[jj])
            dst = bass.AP(bigB, (q2 * 8) * 1024 + i2 * 128,
                          [[32768, 128], [1024, 8], [1, 128]])
            w2c[jj] = p.op("scalar", lambda e, dst=dst: e.copy(
                dst, psD[:, :].rearrange("p (c w) -> p c w", c=8)), inc="s")

        w2q, w2t, w2c = {}, {}, {}
        v_res = {}
        v_att = {}
        s_rt2 = {}
        xmC = [xm[:, 0:1024], stg[:, 1024:2048]]
        stbC = [stbuf[:, 0:8], stbuf[:, 8:16]]
        for i in range(NT + 1):
            j = i - 1
            if i < NT:
                par = i % 2
                dxs = "dx0" if par == 0 else "dx1"
                if i >= 2:
                    p.wait("sync", "v", v_res.get(i - 2, 0))
                p.dma("sync", xbuf[:, par * 1024:(par + 1) * 1024],
                      d["x"][i * 128:(i + 1) * 128, :], dxs)
                if i >= 1:
                    p.wait("tensor", "v", v_att[i - 1])
                for hf in range(2):
                    for c in range(8):
                        tm = p.op("tensor", lambda e, i=i, hf=hf, c=c: e.matmul(
                            psA[:, hf * 512:(hf + 1) * 512],
                            bigA[:, c * 4096 + i * 128:c * 4096 + (i + 1) * 128],
                            WO[:, c * 1024 + hf * 512:c * 1024 + (hf + 1) * 512],
                            start=(c == 0), stop=(c == 7)),
                            inc="t" if (hf == 1 and c == 7) else None)
                t_oproj = tm
            if j >= 0:
                pj = j % 2
                sbj = stbC[pj]
                dxsj = "dx0" if pj == 0 else "dx1"
                dssj = "ds0" if pj == 0 else "ds1"
                p.wait("vector", "s", s_rt2[j])
                p.op("vector", lambda e, sbj=sbj: e.reciprocal(
                    sbj[:, 5:6], sbj[:, 2:3]))
                p.wait("vector", dxsj)
                p.wait("vector", dssj)
                osl = pj * 1024
                v_res[j] = p.op("vector", lambda e, pj=pj, osl=osl, sbj=sbj:
                                e.scalar_tensor_tensor(
                                    outr[:, osl:osl + 1024], xmC[pj],
                                    sbj[:, 5:6],
                                    xbuf[:, pj * 1024:(pj + 1) * 1024],
                                    ALU.mult, ALU.add),
                                inc="v")
                p.wait("gpsimd", "v", v_res[j])
                p.dma("gpsimd", d["x2s"][j * 128:(j + 1) * 128, :],
                      outr[:, osl:osl + 1024], dssj)
            if i < NT:
                par = i % 2
                sbi = stbC[par]
                p.wait("vector", "t", t_oproj)
                p.op("vector", lambda e, sbi=sbi: e.reduce_sum(
                    sbi[:, 0:1], psA[:, :], AX))
                p.op("vector", lambda e, sbi=sbi: e.scalar_tensor_tensor(
                    sbi[:, 0:1], sbi[:, 0:1], 1.0 / 1024, stbuf[:, 9:10],
                    ALU.mult, ALU.add) if False else e.scalar_tensor_tensor(
                    sbi[:, 0:1], sbi[:, 0:1], 1.0 / 1024, wabs[:, 62:63],
                    ALU.mult, ALU.add))
                v_att[i] = p.op("vector", lambda e, par=par, sbi=sbi:
                                e.scalar_tensor_tensor(
                                    xmC[par], psA[:, :], sbi[:, 0:1],
                                    bO_b[:, 0:1024], ALU.subtract, ALU.add),
                                inc="v")
                p.wait("scalar", "v", v_att[i])
                p.op("scalar", lambda e, par=par, sbi=sbi: e.activation(
                    psB[:, 0:1024], xmC[par], ACT.Square,
                    accum_out=sbi[:, 6:7]), inc="s")
                s_rt2[i] = p.op("scalar", lambda e, sbi=sbi: e.activation(
                    sbi[:, 2:3], sbi[:, 6:7], ACT.Sqrt,
                    scale=1.0 / 1024, bias=wabs[:, 63:64]), inc="s")
            if j >= 2:
                emit_w2_tile(j - 2)
        emit_w2_tile(30)
        emit_w2_tile(31)
        ds_x2 = (p.cnt["ds0"], p.cnt["ds1"])
        if dbg:
            p.barrier()
            p.dma("gpsimd", dbg_t["o_x2"][:, :], d["x2s"][:, :], "dg")
        p.barrier()

        p.op("vector", lambda e: e.tensor_copy(b1sbf[:, :], b1spp[:, :]),
             inc="v")
        for hc in range(32):
            dstc = bass.AP(bigA.bitcast(FP8), 32768 + hc * 1024,
                           [[65536, 128], [1, 1024]])
            if hc % 2 == 0:
                p.op("vector", lambda e, hc=hc, dstc=dstc: e.tensor_copy(
                    dstc, bigB[:, hc * 1024:(hc + 1) * 1024]), inc="v")
            else:
                p.op("scalar", lambda e, hc=hc, dstc=dstc: e.copy(
                    dstc, bigB[:, hc * 1024:(hc + 1) * 1024]), inc="s")
        if dbg:
            p.barrier()
            p.dma("gpsimd", dbg_t["o_w2t"][:, :], bigA[:, 16384:32768], "dg")
            p.dma("gpsimd", dbg_t["o_ppx"][:, :], ppx[:, :], "dg")
        w2cv = p.cnt["v"]
        w2cs = p.cnt["s"]

        p.wait("sync", "ds0", ds_x2[0])
        p.wait("sync", "ds1", ds_x2[1])
        NWIN = 8
        TPW = 4
        w1T8 = bigA.bitcast(FP8)
        gbuf = bigB[:, 0:16384]
        zTbase = 16384
        xwodd = bigB.bitcast(F32)
        v_win = {}
        v_out = {-1: 0, 0: 0}
        v_zt = {}
        gelc = {}
        t_mm1 = {}
        v_mm2evac = []

        def xw_tile(w, tt):
            if w % 2 == 0:
                return (xbuf if tt < 2 else x2w2)[:, (tt % 2) * 1024:
                                                  (tt % 2) * 1024 + 1024]
            return xwodd[:, 12288 + tt * 1024:12288 + (tt + 1) * 1024]

        def emit_pre_steps(w):
            dxs = "dx0" if w % 2 == 0 else "dx1"
            zsl = zTbase + (w % 2) * 4096
            sb8 = stbuf[:, 8:16]
            steps = []
            m = {}

            def s_load():
                if w == 1:
                    p.wait("sync", "v", w2cv)
                    p.wait("sync", "s", w2cs)
                if w >= 2:
                    p.wait("sync", "v", v_win[w - 2])
                for tt in range(TPW):
                    p.dma("sync", xw_tile(w, tt),
                          d["x2s"][w * 512 + tt * 128:
                                   w * 512 + (tt + 1) * 128, :], dxs)
            steps.append(s_load)

            def mk_ln_steps(get_in, tt, second):
                def s1():
                    if not second and tt == 0:
                        p.wait("vector", dxs)
                    p.op("vector", lambda e: e.reduce_sum(
                        sb8[:, 0:1], get_in(), AX))
                    p.op("vector", lambda e: e.tensor_scalar(
                        sb8[:, 0:1], sb8[:, 0:1], 1.0 / 1024, None, ALU.mult))
                    m["vv"] = p.op("vector", lambda e: e.tensor_scalar(
                        xm[:, 0:1024], get_in(), sb8[:, 0:1], None,
                        ALU.subtract), inc="v")

                def s2():
                    p.wait("scalar", "v", m["vv"])
                    p.op("scalar", lambda e: e.activation(
                        psA[:, :], xm[:, 0:1024], ACT.Square,
                        accum_out=sb8[:, 6:7]), inc="s")
                    m["rt"] = p.op("scalar", lambda e: e.activation(
                        sb8[:, 2:3], sb8[:, 6:7], ACT.Sqrt,
                        scale=1.0 / 1024, bias=wabs[:, 63:64]), inc="s")

                def s5():
                    p.wait("vector", "s", m["rt"])
                    p.op("vector", lambda e: e.reciprocal(
                        sb8[:, 5:6], sb8[:, 2:3]))
                    if not second:
                        p.op("vector", lambda e: e.tensor_scalar(
                            xm[:, 0:1024], xm[:, 0:1024], sb8[:, 5:6], None,
                            ALU.mult), inc="v")
                    else:
                        m["vz"] = p.op("vector", lambda e: e.scalar_tensor_tensor(
                            xlnb[:, :], xm[:, 0:1024], sb8[:, 5:6], g1b[:, :],
                            ALU.mult, ALU.mult), inc="v")
                return [s1, s2, s5]

            for tt in range(TPW):
                xa_fn = (lambda w=w, tt=tt: xw_tile(w, tt))
                xm_fn = (lambda: xm[:, 0:1024])
                steps += mk_ln_steps(xa_fn, tt, False)
                steps += mk_ln_steps(xm_fn, tt, True)

                def s_tr(tt=tt):
                    p.wait("tensor", "v", m["vz"])
                    for c in range(8):
                        tz = p.op("tensor", lambda e, c=c: e.matmul(
                            psD[:, c * 128:(c + 1) * 128],
                            xlnb[:, c * 128:(c + 1) * 128], idtb[:, :],
                            start=True, stop=True),
                            inc="t" if c == 7 else None)
                    m["tz"] = tz

                def s_cp(tt=tt, zsl=zsl):
                    p.wait("vector", "t", m["tz"])
                    v_zt[w] = p.op("vector", lambda e, tt=tt: e.tensor_copy(
                        bass.AP(bigB, zsl + tt * 128,
                                [[32768, 128], [512, 8], [1, 128]]),
                        psD[:, :].rearrange("p (c w) -> p c w", c=8)),
                        inc="v")
                steps.append(s_tr)
                steps.append(s_cp)
            return steps

        w1ev = {}
        b1ct = {}
        vq1 = {}
        tq1 = {}

        def emit_w1_tile(j):
            sl = (j % 2) * 1024
            dws = "dw0" if j % 2 == 0 else "dw1"
            if j >= 2:
                p.wait("sync", "v", vq1[j - 2])
            else:
                p.wait("sync", "v", w2q[31])
            p.dma("sync", stg[:, sl:sl + 1024],
                  d["w1"][j * 128:(j + 1) * 128, :], dws)
            p.wait("vector", dws)
            if j >= 1:
                p.wait("vector", "t", tq1[j - 1])
            p.op("vector", lambda e, sl=sl: e.tensor_scalar(
                xm[:, 0:1024], stg[:, sl:sl + 1024], ppx[:, 2:3], None,
                ALU.is_lt))
            vq1[j] = p.op("vector", lambda e, sl=sl: e.scalar_tensor_tensor(
                xlnb[:, :], stg[:, sl:sl + 1024], ppx[:, 1:2], xm[:, 0:1024],
                ALU.is_gt, ALU.subtract), inc="v")
            p.wait("tensor", "v", vq1[j])
            if j >= 1:
                p.wait("tensor", "s", w1ev[j - 1])
            for c in range(8):
                tq1[j] = p.op("tensor", lambda e, c=c: e.matmul(
                    psD[:, c * 128:(c + 1) * 128],
                    xlnb[:, c * 128:(c + 1) * 128], idtb[:, :],
                    start=True, stop=True), inc="t" if c == 7 else None)
            p.wait("scalar", "t", tq1[j])
            dst = bass.AP(bigA.bitcast(FP8), j * 128,
                          [[65536, 128], [4096, 8], [1, 128]])
            w1ev[j] = p.op("scalar", lambda e, dst=dst: e.copy(
                dst, psD[:, :].rearrange("p (c w) -> p c w", c=8)), inc="s")
            p.wait("tensor", "s", w1ev[j])
            for dc in range(8):
                tb = p.op("tensor", lambda e, j=j, dc=dc: e.matmul(
                    psC[:, j:j + 1],
                    w1T8[:, dc * 4096 + j * 128:dc * 4096 + (j + 1) * 128],
                    b1sbf[:, dc:dc + 1],
                    start=(dc == 0), stop=(dc == 7)),
                    inc="t" if dc == 7 else None)
            p.wait("vector", "t", tb)
            b1ct[j] = p.op("vector", lambda e, j=j: e.tensor_tensor(
                b1tot[:, j:j + 1], psC[:, j:j + 1], b1p[:, j:j + 1],
                ALU.add), inc="v")

        for fn in emit_pre_steps(0):
            fn()
        emit_w1_tile(0)
        emit_w1_tile(1)
        for w in range(NWIN):
            zsl = zTbase + (w % 2) * 4096
            gel = gelc
            vgsq = {}
            tsum = {}
            pre_steps = emit_pre_steps(w + 1) if w + 1 < NWIN else []
            p.wait("tensor", "v", v_zt[w])
            p.wait("tensor", "s")
            for hc in range(33):
                if pre_steps and len(pre_steps) == 33:
                    pre_steps.pop(0)()
                if w == 0 and hc + 2 < 32:
                    emit_w1_tile(hc + 2)
                if hc < 32:
                    off = (hc % 2) * 512
                    if w == 0:
                        p.wait("tensor", "s", w1ev[hc])
                    km2 = (w, hc - 2) if hc >= 2 else (w - 1, 30 + hc)
                    if km2 in gel:
                        p.wait("tensor", "s", gel[km2])
                    for c in range(8):
                        tmm = p.op("tensor", lambda e, hc=hc, c=c, off=off,
                                   zsl=zsl: e.matmul(
                                       psA[:, off:off + 512],
                                       w1T8[:, c * 4096 + hc * 128:
                                            c * 4096 + (hc + 1) * 128],
                                       bigB[:, zsl + c * 512:zsl + (c + 1) * 512],
                                       start=(c == 0), stop=(c == 7)),
                                   inc="t" if c == 7 else None)
                    t_mm1[(w, hc)] = tmm
                    p.wait("scalar", "t", tmm)
                    if w == 0:
                        p.wait("scalar", "v", b1ct[hc])
                    gel[(w, hc)] = p.op(
                        "scalar", lambda e, hc=hc, off=off: e.activation(
                            gbuf[:, hc * 512:(hc + 1) * 512],
                            psA[:, off:off + 512],
                            ACT.Gelu, bias=b1tot[:, hc:hc + 1]), inc="s")
                    p.wait("vector", "s", gel[(w, hc)])
                    if hc >= 2:
                        p.wait("vector", "t", tsum[hc - 2])
                    vgsq[hc] = p.op("vector", lambda e, hc=hc: e.tensor_mul(
                        gsq[:, (hc % 2) * 512:(hc % 2) * 512 + 512],
                        gbuf[:, hc * 512:(hc + 1) * 512],
                        gbuf[:, hc * 512:(hc + 1) * 512]), inc="v")
                if hc >= 1:
                    hp = hc - 1
                    p.wait("tensor", "s", gel[(w, hp)])
                    p.op("tensor", lambda e, hp=hp: e.matmul(
                        psB[0:1, 0:512], onesb[:, 0:1],
                        gbuf[:, hp * 512:(hp + 1) * 512],
                        start=(hp == 0), stop=(hp == 31)))
                    p.wait("tensor", "v", vgsq[hp])
                    tsum[hp] = p.op("tensor", lambda e, hp=hp: e.matmul(
                        psB[0:1, 512:1024], onesb[:, 0:1],
                        gsq[:, (hp % 2) * 512:(hp % 2) * 512 + 512],
                        start=(hp == 0), stop=(hp == 31)),
                        inc="t")
            p.wait("vector", "t", tsum[31])
            p.op("vector", lambda e: e.tensor_scalar(
                srow_f[0:1, 0:512], psB[0:1, 0:512], 1.0 / HID, None,
                ALU.mult))
            p.op("vector", lambda e: e.tensor_mul(
                srow_f[0:1, 512:1024], srow_f[0:1, 0:512],
                srow_f[0:1, 0:512]))
            vv = p.op("vector", lambda e: e.scalar_tensor_tensor(
                srow_f[0:1, 512:1024], psB[0:1, 512:1024], 1.0 / HID,
                srow_f[0:1, 512:1024], ALU.mult, ALU.subtract), inc="v")
            p.wait("scalar", "v", vv)
            rt = p.op("scalar", lambda e: e.activation(
                srow_f[0:1, 512:1024], srow_f[0:1, 512:1024], ACT.Sqrt,
                bias=wabs[0:1, 63:64]), inc="s")
            p.wait("vector", "s", rt)
            p.op("vector", lambda e: e.reciprocal(
                srow_f[0:1, 512:1024], srow_f[0:1, 512:1024]))
            vrc = p.op("vector", lambda e: e.tensor_mul(
                srow_f[0:1, 0:512], srow_f[0:1, 0:512],
                srow_f[0:1, 512:1024]), inc="v")
            p.wait("tensor", "v", vrc)
            p.op("tensor", lambda e: e.matmul(
                psB[:, 0:512], onesf[:, :], srow_f[:, 0:512],
                start=True, stop=True))
            tm = p.op("tensor", lambda e: e.matmul(
                psB[:, 512:1024], onesf[:, :], srow_f[:, 512:1024],
                start=True, stop=True), inc="t")
            p.wait("vector", "t", tm)
            p.op("vector", lambda e: e.tensor_copy(mu2b[:, :], psB[:, 0:512]))
            p.op("vector", lambda e: e.tensor_copy(rs2b[:, :], psB[:, 512:1024]),
                 inc="v")
            vap = {}
            for half in range(2):
                h0 = half * 16
                gb3 = gbuf[:, h0 * 512:(h0 + 16) * 512].rearrange(
                    "p (c w) -> p c w", c=16)
                m2r_bc = bass.AP(stg.bitcast(BF16), 2048,
                                 [[4096, 128], [0, 16], [1, 512]])
                rs2_bc = bass.AP(stg.bitcast(BF16), 2560,
                                 [[4096, 128], [0, 16], [1, 512]])
                p.op("vector", lambda e, gb3=gb3, rs2_bc=rs2_bc:
                     e.tensor_tensor(gb3, gb3, rs2_bc, ALU.mult))
                p.op("vector", lambda e, gb3=gb3, m2r_bc=m2r_bc:
                     e.tensor_tensor(gb3, gb3, m2r_bc, ALU.subtract), inc="v")
                for hc in range(h0, h0 + 16):
                    vap[hc] = p.op("vector", lambda e, hc=hc: e.tensor_scalar(
                        gbuf[:, hc * 512:(hc + 1) * 512],
                        gbuf[:, hc * 512:(hc + 1) * 512],
                        g2p[:, hc:hc + 1], b2p[:, hc:hc + 1],
                        ALU.mult, ALU.add), inc="v")
            if dbg and w == 0:
                p.barrier()
                p.dma("gpsimd", dbg_t["o_h2"][:, :], gbuf[:, 0:8192], "dg")
                p.dma("gpsimd", dbg_t["o_w1t"][:, :], bigA[:, 0:16384], "dg")
                p.dma("gpsimd", dbg_t["o_b1t"][:, :], b1tot[:, :], "dg")
                p.barrier()
            def emit_final(w, tt):
                osl = (tt % 2) * 1024
                sb0 = stbuf[:, 0:8]
                p.op("vector", lambda e, osl=osl, sb0=sb0: e.reduce_sum(
                    sb0[:, 0:1], outr[:, osl:osl + 1024], AX))
                p.op("vector", lambda e, sb0=sb0: e.tensor_scalar(
                    sb0[:, 0:1], sb0[:, 0:1], 1.0 / 1024, None, ALU.mult))
                vv = p.op("vector", lambda e, osl=osl, sb0=sb0: e.tensor_scalar(
                    outr[:, osl:osl + 1024], outr[:, osl:osl + 1024],
                    sb0[:, 0:1], None, ALU.subtract), inc="v")
                p.wait("scalar", "v", vv)
                p.op("scalar", lambda e, osl=osl, sb0=sb0: e.activation(
                    psA[:, :], outr[:, osl:osl + 1024], ACT.Square,
                    accum_out=sb0[:, 6:7]), inc="s")
                rt = p.op("scalar", lambda e, sb0=sb0: e.activation(
                    sb0[:, 2:3], sb0[:, 6:7], ACT.Sqrt,
                    scale=1.0 / 1024, bias=wabs[:, 63:64]), inc="s")
                p.wait("vector", "s", rt)
                p.op("vector", lambda e, sb0=sb0: e.reciprocal(
                    sb0[:, 5:6], sb0[:, 2:3]))
                v_out[TPW * w + tt] = p.op(
                    "vector", lambda e, w=w, tt=tt, osl=osl, sb0=sb0:
                    e.scalar_tensor_tensor(
                        outr[:, osl:osl + 1024], outr[:, osl:osl + 1024],
                        sb0[:, 5:6], xw_tile(w, tt),
                        ALU.mult, ALU.add),
                    inc="v")
                p.wait("gpsimd", "v", v_out[TPW * w + tt])
                p.dma("gpsimd",
                      d["y"][(TPW * w + tt) * 128:(TPW * w + tt + 1) * 128, :],
                      outr[:, osl:osl + 1024],
                      "ds0" if tt % 2 == 0 else "ds1")

            for tt in range(TPW):
                if w == 0 and tt == 0:
                    p.wait("tensor", "v", w2cv)
                    p.wait("tensor", "s", w2cs)
                if v_mm2evac:
                    p.wait("tensor", "v", v_mm2evac[-1])
                for hf in range(2):
                    for hc in range(32):
                        if tt == 0 and hf == 0:
                            p.wait("tensor", "v", vap[hc])
                        tm = p.op("tensor", lambda e, tt=tt, hf=hf, hc=hc:
                                  e.matmul(
                                      psC[:, hf * 512:(hf + 1) * 512],
                                      gbuf[:, hc * 512 + tt * 128:
                                           hc * 512 + (tt + 1) * 128],
                                      w1T8[:, 32768 + hc * 1024 + hf * 512:
                                           32768 + hc * 1024 + (hf + 1) * 512],
                                      start=(hc == 0), stop=(hc == 31)),
                                  inc="t" if hc == 31 else None)
                p.wait("vector", "t", tm)
                osl = (tt % 2) * 1024
                p.wait("vector", "ds0" if tt % 2 == 0 else "ds1")
                for hf in range(2):
                    v_mev = p.op("vector", lambda e, hf=hf, osl=osl:
                                 e.tensor_tensor(
                                     outr[:, osl + hf * 512:osl + (hf + 1) * 512],
                                     psC[:, hf * 512:(hf + 1) * 512],
                                     b2b[:, hf * 512:(hf + 1) * 512], ALU.add),
                                 inc="v" if hf == 1 else None)
                v_mm2evac.append(v_mev)
                if tt >= 1:
                    emit_final(w, tt - 1)
                for _ in range(9):
                    if pre_steps:
                        pre_steps.pop(0)()
            emit_final(w, TPW - 1)
            for fn in pre_steps:
                fn()
            pre_steps = []
            v_win[w] = v_out[TPW * w + TPW - 1]
        p.wait("gpsimd", "ds0")
        p.wait("gpsimd", "ds1")
        p.wait("gpsimd", "dg")

        with nc.Block() as block:
            @block.sync
            def _(e):
                for fn in p.items["sync"]:
                    fn(e)

            @block.tensor
            def _(e):
                for fn in p.items["tensor"]:
                    fn(e)

            @block.vector
            def _(e):
                for fn in p.items["vector"]:
                    fn(e)

            @block.scalar
            def _(e):
                for fn in p.items["scalar"]:
                    fn(e)

            @block.gpsimd
            def _(e):
                for fn in p.items["gpsimd"]:
                    fn(e)

    return nc, dbg_t


_EYE_BF = np.eye(128, dtype=ml_dtypes.bfloat16)
_EYE_F32 = np.eye(128, dtype=np.float32)


def make_in_maps(inputs):
    x = np.ascontiguousarray(np.asarray(inputs["L0_attn_h_t"], dtype=np.float32))
    base = {}
    for nm in WNAMES:
        key = ("L0_attn_" + nm) if ("L0_attn_" + nm) in inputs else ("L0_ffn_" + nm)
        base[nm] = np.ascontiguousarray(np.asarray(inputs[key], dtype=np.float32))
    base["idtb"] = _EYE_BF
    base["idtf"] = _EYE_F32
    return [dict(base, x=x[b]) for b in range(8)]


def kernel(**inputs):
    nc, _ = build_program(dbg=False)
    in_maps = make_in_maps(inputs)
    res = run_bass_kernel_spmd(nc, in_maps, core_ids=list(range(8)))
    return np.stack([res.results[b]["y"] for b in range(8)], axis=0)


# version 22
# speedup vs baseline: 1.0381x; 1.0381x over previous
import numpy as np
import ml_dtypes
from contextlib import ExitStack

import concourse.bass as bass
import concourse.mybir as mybir
from concourse.bass_utils import run_bass_kernel_spmd

F32 = mybir.dt.float32
BF16 = mybir.dt.bfloat16
FP8 = mybir.dt.float8e4
AX = mybir.AxisListType.X
ALU = mybir.AluOpType
ACT = mybir.ActivationFunctionType

S = 4096
D = 1024
H = 8
DK = 128
DR = 32
DC = 32
HID = 4096
NT = S // 128
EPS = 1e-5
SCALE = 1.0 / float(np.sqrt(DK + DR))

ENGINES = ("sync", "tensor", "vector", "scalar", "gpsimd")

WNAMES = ["W_DQ", "b_DQ", "W_UQ", "b_UQ", "W_DKV", "b_DKV", "W_UK", "b_UK",
          "W_UV", "b_UV", "W_QR", "b_QR", "W_KR", "b_KR", "W_O", "b_O",
          "ln1_g", "ln1_b", "w1", "b1", "ln2_g", "ln2_b", "w2", "b2"]


class Prog:

    def __init__(self, sems):
        self.items = {e: [] for e in ENGINES}
        self.sems = sems
        self.cnt = {k: 0 for k in sems}
        self.nbar = 0

    def wait(self, eng, sem, val=None):
        v = self.cnt[sem] if val is None else val
        h = self.sems[sem]
        self.items[eng].append(lambda e, h=h, v=v: e.wait_ge(h, v))

    AUTOSEM = {"vector": "v", "scalar": "s"}

    def op(self, eng, fn, inc=None, n=1):
        auto = self.AUTOSEM.get(eng)
        if auto is not None:
            assert inc in (None, auto), (eng, inc)
            self.wait(eng, auto)
            inc = auto
        if inc is None:
            self.items[eng].append(fn)
            return None
        h = self.sems[inc]
        self.cnt[inc] += n

        def run(e, fn=fn, h=h, n=n):
            fn(e).then_inc(h, n)
        self.items[eng].append(run)
        return self.cnt[inc]

    def dma(self, eng, out, in_, sem):
        return self.op(eng, lambda e, o=out, i=in_: e.dma_start(out=o, in_=i),
                       inc=sem, n=16)

    def barrier(self):
        for s in ("dl", "dx0", "dx1", "dw0", "dw1", "dw2", "dw3"):
            self.wait("sync", s)
        for s in ("dg", "ds0", "ds1"):
            self.wait("gpsimd", s)
        self.wait("tensor", "t")
        self.wait("vector", "v")
        self.wait("scalar", "s")
        self.nbar += 1
        tgt = 5 * self.nbar
        h = self.sems["bar"]
        self.cnt["bar"] = tgt
        for e in ENGINES:
            self.items[e].append(lambda eng, h=h: eng.sem_inc(h, 1))
            self.items[e].append(lambda eng, h=h, t=tgt: eng.wait_ge(h, t))


def build_program(dbg=False):
    nc = bass.Bass("TRN2", target_bir_lowering=False)

    d = {"x": nc.dram_tensor("x", [S, D], F32, kind="ExternalInput").ap()}
    shapes = {"W_DQ": [D, DC], "b_DQ": [DC], "W_UQ": [DC, H * DK],
              "b_UQ": [H * DK], "W_DKV": [D, DC], "b_DKV": [DC],
              "W_UK": [DC, H * DK], "b_UK": [H * DK], "W_UV": [DC, H * DK],
              "b_UV": [H * DK], "W_QR": [DC, H * DR], "b_QR": [H * DR],
              "W_KR": [D, DR], "b_KR": [DR], "W_O": [H * DK, D], "b_O": [D],
              "ln1_g": [D], "ln1_b": [D], "w1": [HID, D], "b1": [HID],
              "ln2_g": [HID], "ln2_b": [HID], "w2": [D, HID], "b2": [D]}
    for nm in WNAMES:
        d[nm] = nc.dram_tensor(nm, shapes[nm], F32, kind="ExternalInput").ap()
    d["idtb"] = nc.dram_tensor("idtb", [128, 128], BF16, kind="ExternalInput").ap()
    d["idtf"] = nc.dram_tensor("idtf", [128, 128], F32, kind="ExternalInput").ap()
    d["y"] = nc.dram_tensor("y", [S, D], F32, kind="ExternalOutput").ap()
    d["x2s"] = nc.dram_tensor("x2s", [S, D], F32).ap()
    dbg_t = {}
    if dbg:
        for nm, shp, dt in [
            ("o_cq", [33, S], BF16), ("o_ckv", [33, S], BF16),
            ("o_krh", [33, S], BF16), ("o_qc", [128, S], BF16),
            ("o_kc", [128, S], BF16), ("o_vt", [128, S], BF16),
            ("o_qrh", [33, S], BF16), ("o_oT", [128, H * S], BF16),
            ("o_x2", [S, D], F32), ("o_w1t", [128, 16384], BF16),
            ("o_w2t", [128, 16384], BF16), ("o_h2", [128, 32 * 256], BF16),
            ("o_ppx", [128, 8], F32), ("o_b1t", [128, 32], F32),
            ("o_zt", [128, 2048], BF16), ("o_gg", [128, 32 * 256], BF16),
        ]:
            dbg_t[nm] = nc.dram_tensor(nm, shp, dt, kind="ExternalOutput").ap()

    es = ExitStack()
    with es:
        def sb(name, shape, dt):
            return es.enter_context(nc.sbuf_tensor(name, shape, dt))

        bigA = sb("bigA", [128, 32768], BF16)
        bigB = sb("bigB", [128, 32768], BF16)
        bigC = sb("bigC", [128, 8192], BF16)
        stg = sb("stg", [128, 2048], F32)
        srow_f = sb("srow_f", [128, 1024], F32)
        xbuf = sb("xbuf", [128, 2048], F32)
        x2w2 = sb("x2w2", [128, 2048], F32)
        xm = sb("xm", [128, 1024], F32)
        stbuf = sb("stbuf", [128, 16], F32)
        outr = sb("outr", [128, 2048], F32)
        xlnb = sb("xlnb", [128, 1024], BF16)
        bcat = sb("bcat", [96, 1], F32)
        WUQ = sb("WUQ", [128, 1024], BF16)
        WUK = sb("WUK", [128, 1024], BF16)
        WUV = sb("WUV", [128, 1024], BF16)
        idtb = sb("idtb_s", [128, 128], BF16)
        idtf = sb("idtf_s", [128, 128], F32)
        onesb = sb("onesb", [128, 128], BF16)
        onesf = sb("onesf", [128, 128], F32)
        b1p = sb("b1p", [128, 32], F32)
        g2p = sb("g2p", [128, 32], F32)
        b2p = sb("b2p", [128, 32], F32)
        ppx = sb("ppx", [128, 8], F32)
        wabs = sb("wabs", [128, 64], F32)
        g1pp = sb("g1pp", [128, 8], F32)
        b1spp = sb("b1spp", [128, 8], F32)
        b1sbf = sb("b1sbf", [128, 8], BF16)
        b1tot = sb("b1tot", [128, 32], F32)
        b2b = sb("b2b", [128, 1024], F32)
        g1b = sb("g1b", [128, 1024], F32)
        zTw = sb("zTw", [128, 1024], BF16)
        gsq = sb("gsq", [128, 1024], BF16)

        Wcat = bigB[:, 31744:32512]
        WQR = bigB[:, 32512:32768]
        nhalf = srow_f[0:1, 0:512]
        Qc = bigB[:, 0:4096]
        Kc = bigB[:, 4096:8192]
        Vt = bigB[:, 8192:12288]
        Qrh = bigB[:, 12288:16384]
        cq = bigB[:, 16384:20480]
        ckv = bigB[:, 20480:24576]
        krh = bigB[:, 24576:28672]
        PT = bigB[:, 28672:30720]
        q2s = bigB[:, 30720:31744]
        WO = bigC[:, 0:8192]
        bO_b = x2w2[:, 1024:2048]
        mu2b = stg[:, 1024:1280].bitcast(BF16)
        rs2b = stg[:, 1280:1536].bitcast(BF16)

        psA = es.enter_context(nc.psum_tensor("psA", [128, 1024], F32))
        psB = es.enter_context(nc.psum_tensor("psB", [128, 1024], F32))
        psC = es.enter_context(nc.psum_tensor("psC", [128, 1024], F32))
        psD = es.enter_context(nc.psum_tensor("psD", [128, 1024], F32))

        sems = {k: es.enter_context(nc.semaphore(k))
                for k in ("dl", "dg", "dx0", "dx1", "ds0", "ds1",
                          "dw0", "dw1", "dw2", "dw3", "t", "v", "s", "bar")}
        p = Prog(sems)

        st_mu = stbuf[:, 0:1]
        st_u = stbuf[:, 1:2]
        st_s0 = stbuf[:, 2:3]
        st_r0 = stbuf[:, 3:4]
        st_t1 = stbuf[:, 4:5]
        st_rs = stbuf[:, 5:6]
        st_ss = stbuf[:, 6:7]

        def rsqrt_chain(u_ap, s0_ap, r0_ap, t1_ap, rs_ap):
            p.wait("scalar", "v")
            p.op("scalar", lambda e: e.activation(s0_ap, u_ap, ACT.Sqrt), inc="s")
            p.wait("vector", "s")
            p.op("vector", lambda e: e.reciprocal(rs_ap, s0_ap), inc="v")

        def ln_stats(xin_ap, ncols, dump=None):
            if dump is None:
                dump = stg[:, 0:ncols]
            p.op("vector", lambda e: e.reduce_sum(st_mu, xin_ap, AX))
            p.op("vector", lambda e: e.tensor_scalar(
                st_mu, st_mu, 1.0 / ncols, None, ALU.mult))
            p.op("vector", lambda e: e.tensor_scalar(
                xm[:, 0:ncols], xin_ap, st_mu, None, ALU.subtract), inc="v")
            p.wait("scalar", "v")
            p.op("scalar", lambda e, dump=dump: e.activation(
                dump, xm[:, 0:ncols], ACT.Square,
                accum_out=st_ss), inc="s")
            rt = p.op("scalar", lambda e, ncols=ncols: e.activation(
                st_s0, st_ss, ACT.Sqrt, scale=1.0 / ncols, bias=wabs[:, 63:64]), inc="s")
            p.wait("vector", "s", rt)
            p.op("vector", lambda e: e.reciprocal(st_rs, st_s0), inc="v")

        p.dma("sync", idtb[:, :], d["idtb"][:, :], "dl")
        p.dma("sync", idtf[:, :], d["idtf"][:, :], "dl")
        p.op("vector", lambda e: e.memset(ppx[:, :], 0.0))
        p.op("vector", lambda e: e.memset(wabs[:, 63:64], EPS))
        p.op("vector", lambda e: e.memset(bigB[32:64, 12288:28672], 0.0))
        p.op("vector", lambda e: e.memset(bigB[64:128, 12288:28672], 0.0))
        p.op("vector", lambda e: e.memset(bigB[32:64, 31232:31744], 0.0))
        p.op("vector", lambda e: e.memset(bigB[64:128, 31232:31744], 0.0))
        p.op("vector", lambda e: e.memset(bigB[32:64, 32512:32768], 0.0))
        p.op("vector", lambda e: e.memset(bigB[64:128, 32512:32768], 0.0))
        p.op("vector", lambda e: e.memset(WUQ[32:64, :], 0.0))
        p.op("vector", lambda e: e.memset(WUQ[64:128, :], 0.0))
        p.op("vector", lambda e: e.memset(WUK[32:64, :], 0.0))
        p.op("vector", lambda e: e.memset(WUK[64:128, :], 0.0))
        p.op("vector", lambda e: e.memset(WUV[32:64, :], 0.0))
        p.op("vector", lambda e: e.memset(WUV[64:128, :], 0.0))
        p.op("vector", lambda e: e.memset(srow_f[:, :], 0.0))
        p.op("vector", lambda e: e.memset(onesb[:, :], 1.0))
        p.op("vector", lambda e: e.memset(onesf[:, :], 1.0))
        p.op("vector", lambda e: e.memset(nhalf[:, :], -0.5))
        p.op("vector", lambda e: e.memset(cq[32:33, :], 1.0))
        p.op("vector", lambda e: e.memset(ckv[32:33, :], 1.0))
        p.op("vector", lambda e: e.memset(krh[32:33, :], 1.0), inc="v")

        for ci in range(8):
            p.dma("gpsimd", stg[:, ci * 32:(ci + 1) * 32],
                  d["W_DQ"][ci * 128:(ci + 1) * 128, :], "dg")
            p.dma("gpsimd", stg[:, 256 + ci * 32:256 + (ci + 1) * 32],
                  d["W_DKV"][ci * 128:(ci + 1) * 128, :], "dg")
            p.dma("gpsimd", stg[:, 512 + ci * 32:512 + (ci + 1) * 32],
                  d["W_KR"][ci * 128:(ci + 1) * 128, :], "dg")
        p.dma("gpsimd", bcat[0:32, :], d["b_DQ"].rearrange("(a b) -> a b", b=1), "dg")
        p.dma("gpsimd", bcat[32:64, :], d["b_DKV"].rearrange("(a b) -> a b", b=1), "dg")
        p.dma("gpsimd", bcat[64:96, :], d["b_KR"].rearrange("(a b) -> a b", b=1), "dg")
        p.wait("vector", "dg")
        for ci in range(8):
            for k, base in ((0, 0), (1, 256), (2, 512)):
                p.op("vector", lambda e, ci=ci, k=k, base=base: e.tensor_copy(
                    Wcat[:, ci * 96 + k * 32:ci * 96 + (k + 1) * 32],
                    stg[:, base + ci * 32:base + (ci + 1) * 32]))
        vfree = p.op("vector", lambda e: e.memset(stbuf[:, 15:16], 0.0),
                     inc="v")

        for wnm, bnm, dst, w in [("W_UQ", "b_UQ", WUQ, 1024),
                                 ("W_UK", "b_UK", WUK, 1024),
                                 ("W_UV", "b_UV", WUV, 1024),
                                 ("W_QR", "b_QR", WQR, 256)]:
            p.wait("gpsimd", "v", vfree)
            p.dma("gpsimd", stg[0:32, 0:w], d[wnm][:, :], "dg")
            p.dma("gpsimd", stg[32:33, 0:w],
                  d[bnm].rearrange("(a b) -> b a", b=1), "dg")
            p.wait("vector", "dg")
            vfree = p.op("vector", lambda e, dst=dst, w=w: e.tensor_copy(
                dst[0:33, 0:w], stg[0:33, 0:w]), inc="v")

        p.wait("tensor", "dl")
        for src, dst, nr in [("b1", b1p, 32), ("ln2_g", g2p, 32),
                             ("ln2_b", b2p, 32), ("ln1_g", g1pp, 8),
                             ("ln1_b", b1spp, 8)]:
            p.wait("gpsimd", "v", vfree)
            p.dma("gpsimd", stg[0:nr, 0:128],
                  d[src].rearrange("(a b) -> a b", b=128), "dg")
            p.wait("tensor", "dg")
            p.op("tensor", lambda e, nr=nr: e.matmul(
                psD[:, 0:nr], stg[0:nr, 0:128], idtf[0:nr, 0:nr],
                start=True, stop=True), inc="t")
            p.wait("vector", "t")
            vfree = p.op("vector", lambda e, dst=dst, nr=nr: e.tensor_copy(
                dst[:, 0:nr], psD[:, 0:nr]), inc="v")

        p.wait("tensor", "dl")
        xlT = bigC[:, 0:4096]
        xm_p = [xm[:, 0:1024], x2w2[:, 1024:2048]]
        dump_p = [stg[:, 0:1024], stg[:, 1024:2048]]
        xlnb_p = [xlnb[:, 0:1024], zTw[:, 0:1024].bitcast(BF16)]
        stb = [stbuf[:, 0:8], stbuf[:, 8:16]]
        v_xfree = {}
        v_sub = {}
        v_u = {}
        v_cp = {}
        s_sq = {}
        s_rt = {}
        s_vx = {}
        t_tr = {}
        t_dn = 0
        for i in range(NT + 1):
            j = i - 1
            if i < NT:
                par = i % 2
                dxs = "dx0" if par == 0 else "dx1"
                if i >= 2:
                    p.wait("sync", "v", v_xfree[i - 2])
                p.dma("sync", xbuf[:, par * 1024:(par + 1) * 1024],
                      d["x"][i * 128:(i + 1) * 128, :], dxs)
            if i < NT:
                par = i % 2
                sb_i = stb[par]
                dxs = "dx0" if par == 0 else "dx1"
                p.wait("vector", dxs)
                if i >= 2 and (i - 2) in s_vx:
                    p.wait("vector", "s", s_vx[i - 2])
                p.op("vector", lambda e, par=par, sb_i=sb_i: e.reduce_sum(
                    sb_i[:, 0:1], xbuf[:, par * 1024:(par + 1) * 1024], AX))
                p.op("vector", lambda e, sb_i=sb_i: e.tensor_scalar(
                    sb_i[:, 0:1], sb_i[:, 0:1], 1.0 / 1024, None, ALU.mult))
                v_sub[i] = p.op("vector", lambda e, par=par, sb_i=sb_i:
                                e.tensor_scalar(
                                    xm_p[par], xbuf[:, par * 1024:(par + 1) * 1024],
                                    sb_i[:, 0:1], None, ALU.subtract), inc="v")
                v_xfree[i] = v_sub[i]
                p.wait("scalar", "v", v_sub[i])
                p.op("scalar", lambda e, par=par, sb_i=sb_i:
                     e.activation(dump_p[par], xm_p[par], ACT.Square,
                                  accum_out=sb_i[:, 6:7]), inc="s")
                s_rt[i] = p.op("scalar", lambda e, sb_i=sb_i: e.activation(
                    sb_i[:, 2:3], sb_i[:, 6:7], ACT.Sqrt,
                    scale=1.0 / 1024, bias=wabs[:, 63:64]), inc="s")
            if j >= 0:
                pj = j % 2
                sb_j = stb[pj]
                p.wait("vector", "s", s_rt[j])
                vrj = p.op("vector", lambda e, sb_j=sb_j: e.reciprocal(
                    sb_j[:, 5:6], sb_j[:, 2:3]), inc="v")
                p.wait("scalar", "v", vrj)
                vx = p.op("scalar", lambda e, pj=pj, sb_j=sb_j: e.mul(
                    xlnb_p[pj], xm_p[pj], sb_j[:, 5:6]), inc="s")
                s_vx[j] = vx
                p.wait("tensor", "s", vx)
                if j >= 1:
                    p.wait("tensor", "v", v_cp[j - 1])
                if j % 4 == 0 and j >= 4:
                    p.wait("tensor", "t", t_dn)
                for c in range(8):
                    t_tr[j] = p.op("tensor", lambda e, c=c, pj=pj: e.matmul(
                        psD[:, c * 128:(c + 1) * 128],
                        xlnb_p[pj][:, c * 128:(c + 1) * 128], idtb[:, :],
                        start=True, stop=True),
                        inc="t" if c == 7 else None)
                p.wait("scalar", "t", t_tr[j])
                if j % 4 == 0 and j >= 4:
                    p.wait("scalar", "t", t_dn)
                jw = j % 4
                v_cp[j] = p.op("scalar", lambda e, jw=jw: e.copy(
                    bass.AP(bigC, jw * 128, [[8192, 128], [512, 8], [1, 128]]),
                    psD[:, :].rearrange("p (c w) -> p c w", c=8)), inc="s")
                if j % 4 == 3:
                    w = j // 4
                    p.wait("tensor", "s", v_cp[j])
                    for c in range(8):
                        t_dn = p.op("tensor", lambda e, c=c: e.matmul(
                            psC[0:96, 0:512], Wcat[:, c * 96:(c + 1) * 96],
                            xlT[:, c * 512:(c + 1) * 512],
                            start=(c == 0), stop=(c == 7)),
                            inc="t" if c == 7 else None)
                    p.wait("vector", "t", t_dn)
                    for r0, dst in ((0, cq), (32, ckv), (64, krh)):
                        p.op("vector", lambda e, r0=r0, dst=dst, w=w:
                             e.tensor_scalar(
                                 dst[0:32, w * 512:(w + 1) * 512],
                                 psC[r0:r0 + 32, 0:512], bcat[r0:r0 + 32, :],
                                 None, ALU.add), inc="v")
        if dbg:
            p.barrier()
            for nm, src in [("o_cq", cq), ("o_ckv", ckv), ("o_krh", krh)]:
                p.dma("gpsimd", dbg_t[nm][:, :], src[0:33, :], "dg")
        p.barrier()

        p1_vq = {}

        def emit_pass1(js):
            for j in js:
                sl = (j % 2) * 1024
                if j >= 2:
                    p.wait("sync", "v", p1_vq[j - 2])
                if j < 32:
                    srcap = d["w1"][j * 128:(j + 1) * 128, :]
                    col = j
                else:
                    i2, q2 = (j - 32) // 4, (j - 32) % 4
                    srcap = d["w2"][i2 * 128:(i2 + 1) * 128,
                                    q2 * 1024:(q2 + 1) * 1024]
                    col = 32 + (j - 32) // 2
                dxs = "dx0" if j % 2 == 0 else "dx1"
                p.dma("sync", stg[:, sl:sl + 1024], srcap, dxs)
                p.wait("vector", dxs)
                p.op("vector", lambda e, j=j, sl=sl, col=col: e.reduce_sum(
                    wabs[:, col:col + 1] if j < 32 else stbuf[:, 8:9],
                    stg[:, sl:sl + 1024], AX,
                    apply_absolute_value=True), inc="v")
                if j < 32:
                    p1_vq[j] = p.cnt["v"]
                elif (j - 32) % 2 == 0:
                    p1_vq[j] = p.op("vector", lambda e, col=col: e.tensor_copy(
                        wabs[:, col:col + 1], stbuf[:, 8:9]), inc="v")
                else:
                    p1_vq[j] = p.op("vector", lambda e, col=col: e.tensor_tensor(
                        wabs[:, col:col + 1], wabs[:, col:col + 1],
                        stbuf[:, 8:9], ALU.add), inc="v")

        gkc = 0
        smark = {}
        expmark = {}
        pvmark = {}
        evacmark = 0

        def emit_pv(kc_g, kc):
            p.wait("tensor", "s", expmark[kc_g])
            first = (kc == 0)
            last = (kc == 31)
            if first:
                p.wait("tensor", "v", evacmark)
            par = (kc_g % 2) * 1024
            for hf in range(2):
                p.op("tensor", lambda e, kc=kc, hf=hf, par=par, first=first,
                     last=last: e.matmul(
                         psC[:, hf * 512:(hf + 1) * 512],
                         Vt[:, kc * 128:(kc + 1) * 128],
                         PT[:, par + hf * 512:par + (hf + 1) * 512],
                         start=first, stop=last))
            for hf in range(2):
                pvmark[kc_g] = p.op(
                    "tensor", lambda e, hf=hf, par=par, first=first, last=last:
                    e.matmul(psD[:, hf * 512:(hf + 1) * 512], onesb[:, :],
                             PT[:, par + hf * 512:par + (hf + 1) * 512],
                             start=first, stop=last),
                    inc="t" if hf == 1 else None)

        deferred_norm = []
        p1_queue = list(range(64))
        for h in range(H):
            if gkc > 0:
                p.wait("tensor", "s", expmark[gkc - 1])
                p.wait("tensor", "v", evacmark)
            vm = {}
            sm = {}
            for di, (dst, wt, src) in enumerate(
                    ((Qc, WUQ, cq), (Kc, WUK, ckv))):
                for g4 in range(4):
                    k = di * 4 + g4
                    ps = psA if k % 2 == 0 else psB
                    if k >= 2:
                        p.wait("tensor", "v", vm.get(k - 2, 0))
                        p.wait("tensor", "s", sm.get(k - 2, 0))
                    for half in range(2):
                        tm = p.op("tensor", lambda e, wt=wt, src=src, g4=g4,
                                  ps=ps, h=h, half=half: e.matmul(
                                      ps[:, half * 512:(half + 1) * 512],
                                      wt[:, h * 128:(h + 1) * 128],
                                      src[:, g4 * 1024 + half * 512:
                                          g4 * 1024 + (half + 1) * 512],
                                      start=True, stop=True),
                                  inc="t" if half == 1 else None)
                    p.wait("vector", "t", tm)
                    vm[k] = p.op("vector", lambda e, dst=dst, g4=g4, ps=ps:
                                 e.tensor_copy(
                                     dst[:, g4 * 1024:g4 * 1024 + 512],
                                     ps[:, 0:512]), inc="v")
                    p.wait("scalar", "t", tm)
                    sm[k] = p.op("scalar", lambda e, dst=dst, g4=g4, ps=ps:
                                 e.copy(dst[:, g4 * 1024 + 512:(g4 + 1) * 1024],
                                        ps[:, 512:1024]), inc="s")
            for g8 in range(8):
                k = 8 + g8
                ps = psA if k % 2 == 0 else psB
                p.wait("tensor", "v", vm.get(k - 2, 0))
                p.wait("tensor", "s", sm.get(k - 2, 0))
                tm = p.op("tensor", lambda e, g8=g8, ps=ps, h=h: e.matmul(
                    ps[0:32, 0:512], WQR[:, h * 32:(h + 1) * 32],
                    cq[:, g8 * 512:(g8 + 1) * 512],
                    start=True, stop=True), inc="t")
                p.wait("vector", "t", tm)
                vm[k] = p.op("vector", lambda e, g8=g8, ps=ps: e.tensor_copy(
                    Qrh[0:32, g8 * 512:(g8 + 1) * 512], ps[0:32, 0:512]),
                    inc="v")
                sm[k] = sm.get(k - 1, 0)
            vmv = {}
            for vt in range(NT):
                slot = vt % 8
                ps = psA if slot < 4 else psB
                off = 512 + (slot % 4) * 128
                if vt >= 8 and vt % 4 == 0:
                    p.wait("tensor", "v", vmv[vt - 8])
                tm = p.op("tensor", lambda e, vt=vt, ps=ps, off=off, h=h:
                    e.matmul(ps[:, off:off + 128],
                             ckv[:, vt * 128:(vt + 1) * 128],
                             WUV[:, h * 128:(h + 1) * 128],
                             start=True, stop=True),
                    inc="t" if vt % 4 == 3 else None)
                if vt % 4 == 3:
                    p.wait("vector", "t", tm)
                    vmv[vt - 3] = p.op(
                        "vector", lambda e, vt=vt, ps=ps: e.tensor_copy(
                            Vt[:, (vt - 3) * 128:(vt + 1) * 128],
                            ps[:, 512:1024]), inc="v")
            for g8 in range(8):
                vmk = p.op("vector", lambda e, g8=g8: e.tensor_mul(
                    q2s[:, 0:512], Kc[:, g8 * 512:(g8 + 1) * 512],
                    Kc[:, g8 * 512:(g8 + 1) * 512]))
                vmk = p.op("vector", lambda e, g8=g8: e.tensor_mul(
                    q2s[0:32, 512:1024], krh[0:32, g8 * 512:(g8 + 1) * 512],
                    krh[0:32, g8 * 512:(g8 + 1) * 512]), inc="v")
                p.wait("tensor", "v", vmk)
                tm = p.op("tensor", lambda e: e.matmul(
                    psD[0:1, 0:512], onesb[:, 0:1], q2s[:, 0:512],
                    start=True, stop=False))
                tm = p.op("tensor", lambda e: e.matmul(
                    psD[0:1, 0:512], onesb[:, 0:1], q2s[:, 512:1024],
                    start=False, stop=True), inc="t")
                p.wait("vector", "t", tm)
                p.op("vector", lambda e, g8=g8: e.reduce_max(
                    stbuf[0:1, 8 + g8:9 + g8], psD[0:1, 0:512], AX), inc="v")
            p.op("vector", lambda e: e.reduce_max(
                stbuf[0:1, 7:8], stbuf[0:1, 8:16], AX), inc="v")
            for g8 in range(8):
                vmk = p.op("vector", lambda e, g8=g8: e.tensor_mul(
                    q2s[:, 0:512], Qc[:, g8 * 512:(g8 + 1) * 512],
                    Qc[:, g8 * 512:(g8 + 1) * 512]))
                vmk = p.op("vector", lambda e, g8=g8: e.tensor_mul(
                    q2s[0:32, 512:1024], Qrh[0:32, g8 * 512:(g8 + 1) * 512],
                    Qrh[0:32, g8 * 512:(g8 + 1) * 512]), inc="v")
                p.wait("tensor", "v", vmk)
                tm = p.op("tensor", lambda e: e.matmul(
                    psD[0:1, 0:512], onesb[:, 0:1], q2s[:, 0:512],
                    start=True, stop=False))
                tm = p.op("tensor", lambda e: e.matmul(
                    psD[0:1, 0:512], onesb[:, 0:1], q2s[:, 512:1024],
                    start=False, stop=True), inc="t")
                p.wait("vector", "t", tm)
                p.op("vector", lambda e, g8=g8: e.scalar_tensor_tensor(
                    Qrh[32:33, g8 * 512:(g8 + 1) * 512],
                    psD[0:1, 0:512], stbuf[0:1, 7:8], nhalf[:, 0:512],
                    ALU.add, ALU.mult), inc="v")
            prep_v = p.cnt["v"]
            if dbg and h == 0:
                p.barrier()
                for nm, src in [("o_qc", Qc), ("o_kc", Kc), ("o_vt", Vt),
                                ("o_qrh", Qrh[0:33, :])]:
                    p.dma("gpsimd", dbg_t[nm][:, :], src, "dg")
                p.barrier()

            for g in range(4):
                q0 = g * 1024
                for kc in range(32):
                    if deferred_norm and kc in (2, 3):
                        deferred_norm.pop(0)()
                    if p1_queue and kc in (6, 10, 14, 18, 22, 26):
                        emit_pass1([p1_queue.pop(0)])
                    if gkc >= 2:
                        p.wait("tensor", "s", expmark[gkc - 2])
                    if kc == 0 and g == 0:
                        p.wait("tensor", "v", prep_v)
                    ps = psA if (gkc % 2 == 0) else psB
                    for hf in range(2):
                        p.op("tensor", lambda e, kc=kc, hf=hf, ps=ps, q0=q0:
                             e.matmul(ps[:, hf * 512:(hf + 1) * 512],
                                      Kc[:, kc * 128:(kc + 1) * 128],
                                      Qc[:, q0 + hf * 512:q0 + (hf + 1) * 512],
                                      start=True, stop=False))
                        smark[gkc] = p.op(
                            "tensor", lambda e, kc=kc, hf=hf, ps=ps, q0=q0:
                            e.matmul(ps[:, hf * 512:(hf + 1) * 512],
                                     krh[:, kc * 128:(kc + 1) * 128],
                                     Qrh[:, q0 + hf * 512:q0 + (hf + 1) * 512],
                                     start=False, stop=True),
                            inc="t" if hf == 1 else None)
                    p.wait("scalar", "t", smark[gkc])
                    if gkc >= 2:
                        p.wait("scalar", "t", pvmark[gkc - 2])
                    par = (gkc % 2) * 1024
                    expmark[gkc] = p.op(
                        "scalar", lambda e, ps=ps, par=par: e.activation(
                            PT[:, par:par + 1024], ps[:, :], ACT.Exp,
                            scale=SCALE), inc="s")
                    if kc >= 1:
                        emit_pv(gkc - 1, kc - 1)
                    gkc += 1
                emit_pv(gkc - 1, 31)
                p.wait("vector", "t", pvmark[gkc - 1])
                p.op("vector", lambda e: e.tensor_copy(x2w2[:, 0:1024],
                                                       psD[:, :]))
                evacmark = p.op("vector", lambda e: e.tensor_copy(
                    outr[:, 0:1024], psC[:, :]), inc="v")

                def _dn1():
                    p.op("vector", lambda e: e.reciprocal(
                        x2w2[:, 1024:2048], x2w2[:, 0:1024]))

                def _dn2(h=h, q0=q0):
                    p.op("vector", lambda e, h=h, q0=q0: e.tensor_tensor(
                        bigA[:, h * 4096 + q0:h * 4096 + q0 + 1024],
                        outr[:, 0:1024], x2w2[:, 1024:2048], ALU.mult),
                        inc="v")
                deferred_norm += [_dn1, _dn2]
        for fn in deferred_norm:
            fn()
        deferred_norm = []
        if p1_queue:
            emit_pass1(list(p1_queue))
            p1_queue = []
        if dbg:
            p.barrier()
            p.dma("gpsimd", dbg_t["o_oT"][:, :], bigA[:, :], "dg")
        p.barrier()

        vfree = p.cnt["v"]
        for ci in range(4):
            if ci >= 1:
                p.wait("sync", "v", vfree)
            p.dma("sync", stg[:, 0:1024],
                  d["W_O"][(2 * ci) * 128:(2 * ci + 1) * 128, :], "dl")
            p.dma("sync", stg[:, 1024:2048],
                  d["W_O"][(2 * ci + 1) * 128:(2 * ci + 2) * 128, :], "dl")
            p.wait("vector", "dl")
            vfree = p.op("vector", lambda e, ci=ci: e.tensor_copy(
                WO[:, (2 * ci) * 1024:(2 * ci + 2) * 1024], stg[:, :]), inc="v")
        p.wait("sync", "v", vfree)
        p.dma("sync", srow_f[0:1, 0:1024],
              d["b_O"].rearrange("(a b) -> b a", b=1), "dl")
        p.wait("tensor", "dl")
        p.wait("tensor", "v", vfree)
        for hf in range(2):
            tm = p.op("tensor", lambda e, hf=hf: e.matmul(
                psC[:, hf * 512:(hf + 1) * 512], onesf[:, :],
                srow_f[:, hf * 512:(hf + 1) * 512], start=True, stop=True),
                inc="t" if hf == 1 else None)
        p.wait("vector", "t", tm)
        p.op("vector", lambda e: e.tensor_copy(bO_b[:, 0:1024], psC[:, :]),
             inc="v")
        for k, (c0, c1, n_el) in enumerate([(0, 32, HID * D), (32, 48, D * HID)]):
            p.op("vector", lambda e, c0=c0, c1=c1: e.reduce_sum(
                stbuf[:, 0:1], wabs[:, c0:c1], AX), inc="v")
            p.wait("tensor", "v")
            tm = p.op("tensor", lambda e: e.matmul(
                psB[0:1, 0:128], stbuf[:, 0:1], idtf[:, :],
                start=True, stop=True), inc="t")
            p.wait("vector", "t", tm)
            p.op("vector", lambda e: e.reduce_sum(
                srow_f[0:1, 0:1], psB[0:1, 0:128], AX))
            p.op("vector", lambda e, n_el=n_el: e.tensor_scalar(
                srow_f[0:1, 0:1], srow_f[0:1, 0:1], 1.0 / n_el, 1e-5,
                ALU.mult, ALU.max), inc="v")
            p.wait("tensor", "v")
            tm = p.op("tensor", lambda e: e.matmul(
                psB[:, 128:129], onesf[:, :], srow_f[:, 0:1],
                start=True, stop=True), inc="t")
            p.wait("vector", "t", tm)
            p.op("vector", lambda e, k=k: e.tensor_copy(
                ppx[:, 3 * k:3 * k + 1], psB[:, 128:129]))
            p.op("vector", lambda e, k=k: e.tensor_scalar(
                ppx[:, 3 * k + 1:3 * k + 2], ppx[:, 3 * k:3 * k + 1], 0.5,
                None, ALU.mult))
            p.op("vector", lambda e, k=k: e.tensor_scalar(
                ppx[:, 3 * k + 2:3 * k + 3], ppx[:, 3 * k:3 * k + 1], -0.5,
                None, ALU.mult), inc="v")
        p.wait("sync", "v")
        p.dma("sync", srow_f[0:1, 0:1024],
              d["ln1_g"].rearrange("(a b) -> b a", b=1), "dl")
        p.wait("tensor", "dl")
        for hf in range(2):
            tm = p.op("tensor", lambda e, hf=hf: e.matmul(
                psB[:, hf * 512:(hf + 1) * 512], onesf[:, :],
                srow_f[:, hf * 512:(hf + 1) * 512],
                start=True, stop=True), inc="t" if hf == 1 else None)
        p.wait("vector", "t", tm)
        p.op("vector", lambda e: e.tensor_scalar(
            g1b[:, :], psB[:, :], ppx[:, 0:1], None, ALU.mult), inc="v")
        p.op("vector", lambda e: e.tensor_scalar(
            g2p[:, :], g2p[:, :], ppx[:, 3:4], None, ALU.mult))
        p.op("vector", lambda e: e.tensor_scalar(
            b2p[:, :], b2p[:, :], ppx[:, 3:4], None, ALU.mult))
        p.op("vector", lambda e: e.tensor_scalar(
            b1spp[:, :], b1spp[:, :], ppx[:, 0:1], None, ALU.mult), inc="v")
        p.wait("sync", "v")
        p.dma("sync", srow_f[0:1, 0:1024],
              d["b2"].rearrange("(a b) -> b a", b=1), "dl")
        p.wait("tensor", "dl")
        for hf in range(2):
            tm = p.op("tensor", lambda e, hf=hf: e.matmul(
                psB[:, hf * 512:(hf + 1) * 512], onesf[:, :],
                srow_f[:, hf * 512:(hf + 1) * 512],
                start=True, stop=True), inc="t" if hf == 1 else None)
        p.wait("vector", "t", tm)
        p.op("vector", lambda e: e.tensor_copy(b2b[:, :], psB[:, :]), inc="v")
        p.barrier()

        p.op("vector", lambda e: e.reduce_sum(wabs[:, 62:63], bO_b[:, 0:1024],
                                              AX))
        p.op("vector", lambda e: e.tensor_scalar(
            wabs[:, 62:63], wabs[:, 62:63], 1.0 / 1024, None, ALU.mult),
            inc="v")

        def emit_w2_tile(jj):
            i2, q2 = jj // 4, jj % 4
            dws = "dw0" if jj % 2 == 0 else "dw1"
            sl = 0
            if jj >= 1:
                p.wait("sync", "v", w2q[jj - 1])
            p.dma("sync", stg[:, sl:sl + 1024],
                  d["w2"][i2 * 128:(i2 + 1) * 128,
                          q2 * 1024:(q2 + 1) * 1024], dws)
            p.wait("vector", dws)
            if jj >= 1:
                p.wait("vector", "t", w2t[jj - 1])
            p.op("vector", lambda e, sl=sl: e.tensor_scalar(
                xlnb[:, :], stg[:, sl:sl + 1024], ppx[:, 5:6], None,
                ALU.is_lt))
            zsl = 0
            w2q[jj] = p.op("vector", lambda e, sl=sl, zsl=zsl:
                           e.scalar_tensor_tensor(
                               zTw[:, zsl:zsl + 1024], stg[:, sl:sl + 1024],
                               ppx[:, 4:5], xlnb[:, :],
                               ALU.is_gt, ALU.subtract), inc="v")
            p.wait("tensor", "v", w2q[jj])
            if jj >= 1:
                p.wait("tensor", "s", w2c[jj - 1])
            for c in range(8):
                w2t[jj] = p.op("tensor", lambda e, c=c, zsl=zsl: e.matmul(
                    psD[:, c * 128:(c + 1) * 128],
                    zTw[:, zsl + c * 128:zsl + (c + 1) * 128], idtb[:, :],
                    start=True, stop=True), inc="t" if c == 7 else None)
            p.wait("scalar", "t", w2t[jj])
            dst = bass.AP(bigB, (q2 * 8) * 1024 + i2 * 128,
                          [[32768, 128], [1024, 8], [1, 128]])
            w2c[jj] = p.op("scalar", lambda e, dst=dst: e.copy(
                dst, psD[:, :].rearrange("p (c w) -> p c w", c=8)), inc="s")

        w2q, w2t, w2c = {}, {}, {}
        v_res = {}
        v_att = {}
        s_rt2 = {}
        xmC = [xm[:, 0:1024], stg[:, 1024:2048]]
        stbC = [stbuf[:, 0:8], stbuf[:, 8:16]]
        for i in range(NT + 1):
            j = i - 1
            if i < NT:
                par = i % 2
                dxs = "dx0" if par == 0 else "dx1"
                if i >= 2:
                    p.wait("sync", "v", v_res.get(i - 2, 0))
                p.dma("sync", xbuf[:, par * 1024:(par + 1) * 1024],
                      d["x"][i * 128:(i + 1) * 128, :], dxs)
                if i >= 1:
                    p.wait("tensor", "v", v_att[i - 1])
                for hf in range(2):
                    for c in range(8):
                        tm = p.op("tensor", lambda e, i=i, hf=hf, c=c: e.matmul(
                            psA[:, hf * 512:(hf + 1) * 512],
                            bigA[:, c * 4096 + i * 128:c * 4096 + (i + 1) * 128],
                            WO[:, c * 1024 + hf * 512:c * 1024 + (hf + 1) * 512],
                            start=(c == 0), stop=(c == 7)),
                            inc="t" if (hf == 1 and c == 7) else None)
                t_oproj = tm
            if j >= 0:
                pj = j % 2
                sbj = stbC[pj]
                dxsj = "dx0" if pj == 0 else "dx1"
                dssj = "ds0" if pj == 0 else "ds1"
                p.wait("vector", "s", s_rt2[j])
                p.op("vector", lambda e, sbj=sbj: e.reciprocal(
                    sbj[:, 5:6], sbj[:, 2:3]))
                p.wait("vector", dxsj)
                p.wait("vector", dssj)
                osl = pj * 1024
                v_res[j] = p.op("vector", lambda e, pj=pj, osl=osl, sbj=sbj:
                                e.scalar_tensor_tensor(
                                    outr[:, osl:osl + 1024], xmC[pj],
                                    sbj[:, 5:6],
                                    xbuf[:, pj * 1024:(pj + 1) * 1024],
                                    ALU.mult, ALU.add),
                                inc="v")
                p.wait("gpsimd", "v", v_res[j])
                p.dma("gpsimd", d["x2s"][j * 128:(j + 1) * 128, :],
                      outr[:, osl:osl + 1024], dssj)
            if i < NT:
                par = i % 2
                sbi = stbC[par]
                p.wait("vector", "t", t_oproj)
                p.op("vector", lambda e, sbi=sbi: e.reduce_sum(
                    sbi[:, 0:1], psA[:, :], AX))
                p.op("vector", lambda e, sbi=sbi: e.scalar_tensor_tensor(
                    sbi[:, 0:1], sbi[:, 0:1], 1.0 / 1024, stbuf[:, 9:10],
                    ALU.mult, ALU.add) if False else e.scalar_tensor_tensor(
                    sbi[:, 0:1], sbi[:, 0:1], 1.0 / 1024, wabs[:, 62:63],
                    ALU.mult, ALU.add))
                v_att[i] = p.op("vector", lambda e, par=par, sbi=sbi:
                                e.scalar_tensor_tensor(
                                    xmC[par], psA[:, :], sbi[:, 0:1],
                                    bO_b[:, 0:1024], ALU.subtract, ALU.add),
                                inc="v")
                p.wait("scalar", "v", v_att[i])
                p.op("scalar", lambda e, par=par, sbi=sbi: e.activation(
                    psB[:, 0:1024], xmC[par], ACT.Square,
                    accum_out=sbi[:, 6:7]), inc="s")
                s_rt2[i] = p.op("scalar", lambda e, sbi=sbi: e.activation(
                    sbi[:, 2:3], sbi[:, 6:7], ACT.Sqrt,
                    scale=1.0 / 1024, bias=wabs[:, 63:64]), inc="s")
            if j >= 2:
                emit_w2_tile(j - 2)
        emit_w2_tile(30)
        emit_w2_tile(31)
        ds_x2 = (p.cnt["ds0"], p.cnt["ds1"])
        if dbg:
            p.barrier()
            p.dma("gpsimd", dbg_t["o_x2"][:, :], d["x2s"][:, :], "dg")
        p.barrier()

        p.op("vector", lambda e: e.tensor_copy(b1sbf[:, :], b1spp[:, :]),
             inc="v")
        for hc in range(32):
            dstc = bass.AP(bigA.bitcast(FP8), 32768 + hc * 1024,
                           [[65536, 128], [1, 1024]])
            if hc % 2 == 0:
                p.op("vector", lambda e, hc=hc, dstc=dstc: e.tensor_copy(
                    dstc, bigB[:, hc * 1024:(hc + 1) * 1024]), inc="v")
            else:
                p.op("scalar", lambda e, hc=hc, dstc=dstc: e.copy(
                    dstc, bigB[:, hc * 1024:(hc + 1) * 1024]), inc="s")
        if dbg:
            p.barrier()
            p.dma("gpsimd", dbg_t["o_w2t"][:, :], bigA[:, 16384:32768], "dg")
            p.dma("gpsimd", dbg_t["o_ppx"][:, :], ppx[:, :], "dg")
        w2cv = p.cnt["v"]
        w2cs = p.cnt["s"]

        p.wait("sync", "ds0", ds_x2[0])
        p.wait("sync", "ds1", ds_x2[1])
        NWIN = 8
        TPW = 4
        w1T8 = bigA.bitcast(FP8)
        gbuf = bigB[:, 0:16384]
        zTbase = 16384
        xwodd = bigB.bitcast(F32)
        v_win = {}
        v_out = {-1: 0, 0: 0}
        v_zt = {}
        gelc = {}
        t_mm1 = {}
        v_mm2evac = []

        def xw_tile(w, tt):
            if w % 2 == 0:
                return (xbuf if tt < 2 else x2w2)[:, (tt % 2) * 1024:
                                                  (tt % 2) * 1024 + 1024]
            return xwodd[:, 12288 + tt * 1024:12288 + (tt + 1) * 1024]

        def emit_pre_steps(w):
            dxs = "dx0" if w % 2 == 0 else "dx1"
            zsl = zTbase + (w % 2) * 4096
            sb8 = stbuf[:, 8:16]
            steps = []
            m = {}

            def s_load():
                if w == 1:
                    p.wait("sync", "v", w2cv)
                    p.wait("sync", "s", w2cs)
                if w >= 2:
                    p.wait("sync", "v", v_win[w - 2])
                for tt in range(TPW):
                    p.dma("sync", xw_tile(w, tt),
                          d["x2s"][w * 512 + tt * 128:
                                   w * 512 + (tt + 1) * 128, :], dxs)
            steps.append(s_load)

            def mk_ln_steps(get_in, tt, second):
                def s1():
                    if not second and tt == 0:
                        p.wait("vector", dxs)
                    p.op("vector", lambda e: e.reduce_sum(
                        sb8[:, 0:1], get_in(), AX))
                    p.op("vector", lambda e: e.tensor_scalar(
                        sb8[:, 0:1], sb8[:, 0:1], 1.0 / 1024, None, ALU.mult))
                    m["vv"] = p.op("vector", lambda e: e.tensor_scalar(
                        xm[:, 0:1024], get_in(), sb8[:, 0:1], None,
                        ALU.subtract), inc="v")

                def s2():
                    p.wait("scalar", "v", m["vv"])
                    p.op("scalar", lambda e: e.activation(
                        psA[:, :], xm[:, 0:1024], ACT.Square,
                        accum_out=sb8[:, 6:7]), inc="s")
                    m["rt"] = p.op("scalar", lambda e: e.activation(
                        sb8[:, 2:3], sb8[:, 6:7], ACT.Sqrt,
                        scale=1.0 / 1024, bias=wabs[:, 63:64]), inc="s")

                def s5():
                    p.wait("vector", "s", m["rt"])
                    p.op("vector", lambda e: e.reciprocal(
                        sb8[:, 5:6], sb8[:, 2:3]))
                    if not second:
                        p.op("vector", lambda e: e.tensor_scalar(
                            xm[:, 0:1024], xm[:, 0:1024], sb8[:, 5:6], None,
                            ALU.mult), inc="v")
                    else:
                        m["vz"] = p.op("vector", lambda e: e.scalar_tensor_tensor(
                            xlnb[:, :], xm[:, 0:1024], sb8[:, 5:6], g1b[:, :],
                            ALU.mult, ALU.mult), inc="v")
                return [s1, s2, s5]

            for tt in range(TPW):
                xa_fn = (lambda w=w, tt=tt: xw_tile(w, tt))
                xm_fn = (lambda: xm[:, 0:1024])
                steps += mk_ln_steps(xa_fn, tt, False)
                steps += mk_ln_steps(xm_fn, tt, True)

                def s_tr(tt=tt):
                    p.wait("tensor", "v", m["vz"])
                    for c in range(8):
                        tz = p.op("tensor", lambda e, c=c: e.matmul(
                            psD[:, c * 128:(c + 1) * 128],
                            xlnb[:, c * 128:(c + 1) * 128], idtb[:, :],
                            start=True, stop=True),
                            inc="t" if c == 7 else None)
                    m["tz"] = tz

                def s_cp(tt=tt, zsl=zsl):
                    p.wait("vector", "t", m["tz"])
                    v_zt[w] = p.op("vector", lambda e, tt=tt: e.tensor_copy(
                        bass.AP(bigB, zsl + tt * 128,
                                [[32768, 128], [512, 8], [1, 128]]),
                        psD[:, :].rearrange("p (c w) -> p c w", c=8)),
                        inc="v")
                steps.append(s_tr)
                steps.append(s_cp)
            return steps

        w1ev = {}
        b1ct = {}
        vq1 = {}
        tq1 = {}

        bigCf = bigC.bitcast(F32)
        xlnb2 = bigC[:, 4096:5120]
        W1SEMS = ("dw0", "dw1", "dw2", "dw3")

        def w1slot(j):
            s = j % 4
            if s < 2:
                return stg[:, s * 1024:(s + 1) * 1024]
            return bigCf[:, (s - 2) * 1024:(s - 1) * 1024]

        def emit_w1_dma(j):
            if j >= 4:
                p.wait("sync", "v", vq1[j - 4])
            elif j < 2:
                p.wait("sync", "v", w2q[31])
            p.dma("sync", w1slot(j), d["w1"][j * 128:(j + 1) * 128, :],
                  W1SEMS[j % 4])

        def emit_w1_quant(j):
            xl = xlnb if j % 2 == 0 else xlnb2
            p.wait("vector", W1SEMS[j % 4])
            if j >= 2:
                p.wait("vector", "t", tq1[j - 2])
            p.op("vector", lambda e, j=j: e.tensor_scalar(
                xm[:, 0:1024], w1slot(j), ppx[:, 2:3], None, ALU.is_lt))
            vq1[j] = p.op("vector", lambda e, j=j, xl=xl:
                          e.scalar_tensor_tensor(
                xl[:, :], w1slot(j), ppx[:, 1:2], xm[:, 0:1024],
                ALU.is_gt, ALU.subtract), inc="v")

        def emit_w1_tr(j):
            xl = xlnb if j % 2 == 0 else xlnb2
            p.wait("tensor", "v", vq1[j])
            if j >= 1:
                p.wait("tensor", "s", w1ev[j - 1])
            for c in range(8):
                tq1[j] = p.op("tensor", lambda e, c=c, xl=xl: e.matmul(
                    psD[:, c * 128:(c + 1) * 128],
                    xl[:, c * 128:(c + 1) * 128], idtb[:, :],
                    start=True, stop=True), inc="t" if c == 7 else None)
            p.wait("scalar", "t", tq1[j])
            dst = bass.AP(bigA.bitcast(FP8), j * 128,
                          [[65536, 128], [4096, 8], [1, 128]])
            w1ev[j] = p.op("scalar", lambda e, dst=dst: e.copy(
                dst, psD[:, :].rearrange("p (c w) -> p c w", c=8)), inc="s")

        def emit_b1col(j):
            p.wait("tensor", "s", w1ev[j])
            for dc in range(8):
                tb = p.op("tensor", lambda e, j=j, dc=dc: e.matmul(
                    psC[:, j:j + 1],
                    w1T8[:, dc * 4096 + j * 128:dc * 4096 + (j + 1) * 128],
                    b1sbf[:, dc:dc + 1],
                    start=(dc == 0), stop=(dc == 7)),
                    inc="t" if dc == 7 else None)
            p.wait("vector", "t", tb)
            b1ct[j] = p.op("vector", lambda e, j=j: e.tensor_tensor(
                b1tot[:, j:j + 1], psC[:, j:j + 1], b1p[:, j:j + 1],
                ALU.add), inc="v")

        for fn in emit_pre_steps(0):
            fn()
        for j in range(4):
            emit_w1_dma(j)
        for j in range(3):
            emit_w1_quant(j)
            emit_w1_tr(j)
        emit_b1col(0)
        emit_b1col(1)
        for w in range(NWIN):
            zsl = zTbase + (w % 2) * 4096
            gel = gelc
            vgsq = {}
            tsum = {}
            pre_steps = emit_pre_steps(w + 1) if w + 1 < NWIN else []
            p.wait("tensor", "v", v_zt[w])
            p.wait("tensor", "s")
            for hc in range(33):
                if pre_steps and len(pre_steps) == 33:
                    pre_steps.pop(0)()
                if w == 0 and hc + 3 < 32:
                    emit_w1_quant(hc + 3)
                if hc < 32:
                    off = (hc % 2) * 512
                    if w == 0:
                        p.wait("tensor", "s", w1ev[hc])
                    km2 = (w, hc - 2) if hc >= 2 else (w - 1, 30 + hc)
                    if km2 in gel:
                        p.wait("tensor", "s", gel[km2])
                    for c in range(8):
                        tmm = p.op("tensor", lambda e, hc=hc, c=c, off=off,
                                   zsl=zsl: e.matmul(
                                       psA[:, off:off + 512],
                                       w1T8[:, c * 4096 + hc * 128:
                                            c * 4096 + (hc + 1) * 128],
                                       bigB[:, zsl + c * 512:zsl + (c + 1) * 512],
                                       start=(c == 0), stop=(c == 7)),
                                   inc="t" if c == 7 else None)
                    t_mm1[(w, hc)] = tmm
                    p.wait("scalar", "t", tmm)
                    if w == 0:
                        p.wait("scalar", "v", b1ct[hc])
                    gel[(w, hc)] = p.op(
                        "scalar", lambda e, hc=hc, off=off: e.activation(
                            gbuf[:, hc * 512:(hc + 1) * 512],
                            psA[:, off:off + 512],
                            ACT.Gelu, bias=b1tot[:, hc:hc + 1]), inc="s")
                    p.wait("vector", "s", gel[(w, hc)])
                    if hc >= 2:
                        p.wait("vector", "t", tsum[hc - 2])
                    vgsq[hc] = p.op("vector", lambda e, hc=hc: e.tensor_mul(
                        gsq[:, (hc % 2) * 512:(hc % 2) * 512 + 512],
                        gbuf[:, hc * 512:(hc + 1) * 512],
                        gbuf[:, hc * 512:(hc + 1) * 512]), inc="v")
                if hc >= 1:
                    hp = hc - 1
                    p.wait("tensor", "s", gel[(w, hp)])
                    p.op("tensor", lambda e, hp=hp: e.matmul(
                        psB[0:1, 0:512], onesb[:, 0:1],
                        gbuf[:, hp * 512:(hp + 1) * 512],
                        start=(hp == 0), stop=(hp == 31)))
                    p.wait("tensor", "v", vgsq[hp])
                    tsum[hp] = p.op("tensor", lambda e, hp=hp: e.matmul(
                        psB[0:1, 512:1024], onesb[:, 0:1],
                        gsq[:, (hp % 2) * 512:(hp % 2) * 512 + 512],
                        start=(hp == 0), stop=(hp == 31)),
                        inc="t")
                if w == 0:
                    if hc + 2 < 32:
                        emit_b1col(hc + 2)
                    if hc + 3 < 32:
                        emit_w1_tr(hc + 3)
                    if hc + 4 < 32:
                        emit_w1_dma(hc + 4)
            p.wait("vector", "t", tsum[31])
            p.op("vector", lambda e: e.tensor_scalar(
                srow_f[0:1, 0:512], psB[0:1, 0:512], 1.0 / HID, None,
                ALU.mult))
            p.op("vector", lambda e: e.tensor_mul(
                srow_f[0:1, 512:1024], srow_f[0:1, 0:512],
                srow_f[0:1, 0:512]))
            vv = p.op("vector", lambda e: e.scalar_tensor_tensor(
                srow_f[0:1, 512:1024], psB[0:1, 512:1024], 1.0 / HID,
                srow_f[0:1, 512:1024], ALU.mult, ALU.subtract), inc="v")
            p.wait("scalar", "v", vv)
            rt = p.op("scalar", lambda e: e.activation(
                srow_f[0:1, 512:1024], srow_f[0:1, 512:1024], ACT.Sqrt,
                bias=wabs[0:1, 63:64]), inc="s")
            p.wait("vector", "s", rt)
            p.op("vector", lambda e: e.reciprocal(
                srow_f[0:1, 512:1024], srow_f[0:1, 512:1024]))
            vrc = p.op("vector", lambda e: e.tensor_mul(
                srow_f[0:1, 0:512], srow_f[0:1, 0:512],
                srow_f[0:1, 512:1024]), inc="v")
            p.wait("tensor", "v", vrc)
            p.op("tensor", lambda e: e.matmul(
                psB[:, 0:512], onesf[:, :], srow_f[:, 0:512],
                start=True, stop=True))
            tm = p.op("tensor", lambda e: e.matmul(
                psB[:, 512:1024], onesf[:, :], srow_f[:, 512:1024],
                start=True, stop=True), inc="t")
            p.wait("vector", "t", tm)
            p.op("vector", lambda e: e.tensor_copy(mu2b[:, :], psB[:, 0:512]))
            p.op("vector", lambda e: e.tensor_copy(rs2b[:, :], psB[:, 512:1024]),
                 inc="v")
            vap = {}
            for half in range(2):
                h0 = half * 16
                gb3 = gbuf[:, h0 * 512:(h0 + 16) * 512].rearrange(
                    "p (c w) -> p c w", c=16)
                m2r_bc = bass.AP(stg.bitcast(BF16), 2048,
                                 [[4096, 128], [0, 16], [1, 512]])
                rs2_bc = bass.AP(stg.bitcast(BF16), 2560,
                                 [[4096, 128], [0, 16], [1, 512]])
                p.op("vector", lambda e, gb3=gb3, rs2_bc=rs2_bc:
                     e.tensor_tensor(gb3, gb3, rs2_bc, ALU.mult))
                p.op("vector", lambda e, gb3=gb3, m2r_bc=m2r_bc:
                     e.tensor_tensor(gb3, gb3, m2r_bc, ALU.subtract), inc="v")
                for hc in range(h0, h0 + 16):
                    vap[hc] = p.op("vector", lambda e, hc=hc: e.tensor_scalar(
                        gbuf[:, hc * 512:(hc + 1) * 512],
                        gbuf[:, hc * 512:(hc + 1) * 512],
                        g2p[:, hc:hc + 1], b2p[:, hc:hc + 1],
                        ALU.mult, ALU.add), inc="v")
            if dbg and w == 0:
                p.barrier()
                p.dma("gpsimd", dbg_t["o_h2"][:, :], gbuf[:, 0:8192], "dg")
                p.dma("gpsimd", dbg_t["o_w1t"][:, :], bigA[:, 0:16384], "dg")
                p.dma("gpsimd", dbg_t["o_b1t"][:, :], b1tot[:, :], "dg")
                p.barrier()
            def emit_final(w, tt):
                osl = (tt % 2) * 1024
                sb0 = stbuf[:, 0:8]
                p.op("vector", lambda e, osl=osl, sb0=sb0: e.reduce_sum(
                    sb0[:, 0:1], outr[:, osl:osl + 1024], AX))
                p.op("vector", lambda e, sb0=sb0: e.tensor_scalar(
                    sb0[:, 0:1], sb0[:, 0:1], 1.0 / 1024, None, ALU.mult))
                vv = p.op("vector", lambda e, osl=osl, sb0=sb0: e.tensor_scalar(
                    outr[:, osl:osl + 1024], outr[:, osl:osl + 1024],
                    sb0[:, 0:1], None, ALU.subtract), inc="v")
                p.wait("scalar", "v", vv)
                p.op("scalar", lambda e, osl=osl, sb0=sb0: e.activation(
                    psA[:, :], outr[:, osl:osl + 1024], ACT.Square,
                    accum_out=sb0[:, 6:7]), inc="s")
                rt = p.op("scalar", lambda e, sb0=sb0: e.activation(
                    sb0[:, 2:3], sb0[:, 6:7], ACT.Sqrt,
                    scale=1.0 / 1024, bias=wabs[:, 63:64]), inc="s")
                p.wait("vector", "s", rt)
                p.op("vector", lambda e, sb0=sb0: e.reciprocal(
                    sb0[:, 5:6], sb0[:, 2:3]))
                v_out[TPW * w + tt] = p.op(
                    "vector", lambda e, w=w, tt=tt, osl=osl, sb0=sb0:
                    e.scalar_tensor_tensor(
                        outr[:, osl:osl + 1024], outr[:, osl:osl + 1024],
                        sb0[:, 5:6], xw_tile(w, tt),
                        ALU.mult, ALU.add),
                    inc="v")
                p.wait("gpsimd", "v", v_out[TPW * w + tt])
                p.dma("gpsimd",
                      d["y"][(TPW * w + tt) * 128:(TPW * w + tt + 1) * 128, :],
                      outr[:, osl:osl + 1024],
                      "ds0" if tt % 2 == 0 else "ds1")

            for tt in range(TPW):
                if w == 0 and tt == 0:
                    p.wait("tensor", "v", w2cv)
                    p.wait("tensor", "s", w2cs)
                if v_mm2evac:
                    p.wait("tensor", "v", v_mm2evac[-1])
                for hf in range(2):
                    for hc in range(32):
                        if tt == 0 and hf == 0:
                            p.wait("tensor", "v", vap[hc])
                        tm = p.op("tensor", lambda e, tt=tt, hf=hf, hc=hc:
                                  e.matmul(
                                      psC[:, hf * 512:(hf + 1) * 512],
                                      gbuf[:, hc * 512 + tt * 128:
                                           hc * 512 + (tt + 1) * 128],
                                      w1T8[:, 32768 + hc * 1024 + hf * 512:
                                           32768 + hc * 1024 + (hf + 1) * 512],
                                      start=(hc == 0), stop=(hc == 31)),
                                  inc="t" if hc == 31 else None)
                p.wait("vector", "t", tm)
                osl = (tt % 2) * 1024
                p.wait("vector", "ds0" if tt % 2 == 0 else "ds1")
                for hf in range(2):
                    v_mev = p.op("vector", lambda e, hf=hf, osl=osl:
                                 e.tensor_tensor(
                                     outr[:, osl + hf * 512:osl + (hf + 1) * 512],
                                     psC[:, hf * 512:(hf + 1) * 512],
                                     b2b[:, hf * 512:(hf + 1) * 512], ALU.add),
                                 inc="v" if hf == 1 else None)
                v_mm2evac.append(v_mev)
                if tt >= 1:
                    emit_final(w, tt - 1)
                for _ in range(9):
                    if pre_steps:
                        pre_steps.pop(0)()
            emit_final(w, TPW - 1)
            for fn in pre_steps:
                fn()
            pre_steps = []
            v_win[w] = v_out[TPW * w + TPW - 1]
        p.wait("gpsimd", "ds0")
        p.wait("gpsimd", "ds1")
        p.wait("gpsimd", "dg")

        with nc.Block() as block:
            @block.sync
            def _(e):
                for fn in p.items["sync"]:
                    fn(e)

            @block.tensor
            def _(e):
                for fn in p.items["tensor"]:
                    fn(e)

            @block.vector
            def _(e):
                for fn in p.items["vector"]:
                    fn(e)

            @block.scalar
            def _(e):
                for fn in p.items["scalar"]:
                    fn(e)

            @block.gpsimd
            def _(e):
                for fn in p.items["gpsimd"]:
                    fn(e)

    return nc, dbg_t


_EYE_BF = np.eye(128, dtype=ml_dtypes.bfloat16)
_EYE_F32 = np.eye(128, dtype=np.float32)


def make_in_maps(inputs):
    x = np.ascontiguousarray(np.asarray(inputs["L0_attn_h_t"], dtype=np.float32))
    base = {}
    for nm in WNAMES:
        key = ("L0_attn_" + nm) if ("L0_attn_" + nm) in inputs else ("L0_ffn_" + nm)
        base[nm] = np.ascontiguousarray(np.asarray(inputs[key], dtype=np.float32))
    base["idtb"] = _EYE_BF
    base["idtf"] = _EYE_F32
    return [dict(base, x=x[b]) for b in range(8)]


def kernel(**inputs):
    nc, _ = build_program(dbg=False)
    in_maps = make_in_maps(inputs)
    res = run_bass_kernel_spmd(nc, in_maps, core_ids=list(range(8)))
    return np.stack([res.results[b]["y"] for b in range(8)], axis=0)


# version 23
# speedup vs baseline: 1.0831x; 1.0433x over previous
import numpy as np
import ml_dtypes
from contextlib import ExitStack

import concourse.bass as bass
import concourse.mybir as mybir
from concourse.bass_utils import run_bass_kernel_spmd

F32 = mybir.dt.float32
BF16 = mybir.dt.bfloat16
FP8 = mybir.dt.float8e4
AX = mybir.AxisListType.X
ALU = mybir.AluOpType
ACT = mybir.ActivationFunctionType

S = 4096
D = 1024
H = 8
DK = 128
DR = 32
DC = 32
HID = 4096
NT = S // 128
EPS = 1e-5
SCALE = 1.0 / float(np.sqrt(DK + DR))

ENGINES = ("sync", "tensor", "vector", "scalar", "gpsimd")

WNAMES = ["W_DQ", "b_DQ", "W_UQ", "b_UQ", "W_DKV", "b_DKV", "W_UK", "b_UK",
          "W_UV", "b_UV", "W_QR", "b_QR", "W_KR", "b_KR", "W_O", "b_O",
          "ln1_g", "ln1_b", "w1", "b1", "ln2_g", "ln2_b", "w2", "b2"]


class Prog:

    def __init__(self, sems):
        self.items = {e: [] for e in ENGINES}
        self.sems = sems
        self.cnt = {k: 0 for k in sems}
        self.nbar = 0

    def wait(self, eng, sem, val=None):
        v = self.cnt[sem] if val is None else val
        h = self.sems[sem]
        self.items[eng].append(lambda e, h=h, v=v: e.wait_ge(h, v))

    AUTOSEM = {"vector": "v", "scalar": "s"}

    def op(self, eng, fn, inc=None, n=1):
        auto = self.AUTOSEM.get(eng)
        if auto is not None:
            assert inc in (None, auto), (eng, inc)
            self.wait(eng, auto)
            inc = auto
        if inc is None:
            self.items[eng].append(fn)
            return None
        h = self.sems[inc]
        self.cnt[inc] += n

        def run(e, fn=fn, h=h, n=n):
            fn(e).then_inc(h, n)
        self.items[eng].append(run)
        return self.cnt[inc]

    def dma(self, eng, out, in_, sem):
        return self.op(eng, lambda e, o=out, i=in_: e.dma_start(out=o, in_=i),
                       inc=sem, n=16)

    def barrier(self):
        for s in ("dl", "dx0", "dx1", "dw0", "dw1", "dw2", "dw3"):
            self.wait("sync", s)
        for s in ("dg", "ds0", "ds1"):
            self.wait("gpsimd", s)
        self.wait("tensor", "t")
        self.wait("vector", "v")
        self.wait("scalar", "s")
        self.nbar += 1
        tgt = 5 * self.nbar
        h = self.sems["bar"]
        self.cnt["bar"] = tgt
        for e in ENGINES:
            self.items[e].append(lambda eng, h=h: eng.sem_inc(h, 1))
            self.items[e].append(lambda eng, h=h, t=tgt: eng.wait_ge(h, t))


def build_program(dbg=False):
    nc = bass.Bass("TRN2", target_bir_lowering=False)

    d = {"x": nc.dram_tensor("x", [S, D], F32, kind="ExternalInput").ap()}
    shapes = {"W_DQ": [D, DC], "b_DQ": [DC], "W_UQ": [DC, H * DK],
              "b_UQ": [H * DK], "W_DKV": [D, DC], "b_DKV": [DC],
              "W_UK": [DC, H * DK], "b_UK": [H * DK], "W_UV": [DC, H * DK],
              "b_UV": [H * DK], "W_QR": [DC, H * DR], "b_QR": [H * DR],
              "W_KR": [D, DR], "b_KR": [DR], "W_O": [H * DK, D], "b_O": [D],
              "ln1_g": [D], "ln1_b": [D], "w1": [HID, D], "b1": [HID],
              "ln2_g": [HID], "ln2_b": [HID], "w2": [D, HID], "b2": [D]}
    for nm in WNAMES:
        d[nm] = nc.dram_tensor(nm, shapes[nm], F32, kind="ExternalInput").ap()
    d["idtb"] = nc.dram_tensor("idtb", [128, 128], BF16, kind="ExternalInput").ap()
    d["idtf"] = nc.dram_tensor("idtf", [128, 128], F32, kind="ExternalInput").ap()
    d["y"] = nc.dram_tensor("y", [S, D], F32, kind="ExternalOutput").ap()
    d["x2s"] = nc.dram_tensor("x2s", [S, D], F32).ap()
    dbg_t = {}
    if dbg:
        for nm, shp, dt in [
            ("o_cq", [33, S], BF16), ("o_ckv", [33, S], BF16),
            ("o_krh", [33, S], BF16), ("o_qc", [128, S], BF16),
            ("o_kc", [128, S], BF16), ("o_vt", [128, S], BF16),
            ("o_qrh", [33, S], BF16), ("o_oT", [128, H * S], BF16),
            ("o_x2", [S, D], F32), ("o_w1t", [128, 16384], BF16),
            ("o_w2t", [128, 16384], BF16), ("o_h2", [128, 32 * 256], BF16),
            ("o_ppx", [128, 8], F32), ("o_b1t", [128, 32], F32),
            ("o_zt", [128, 2048], BF16), ("o_gg", [128, 32 * 256], BF16),
        ]:
            dbg_t[nm] = nc.dram_tensor(nm, shp, dt, kind="ExternalOutput").ap()

    es = ExitStack()
    with es:
        def sb(name, shape, dt):
            return es.enter_context(nc.sbuf_tensor(name, shape, dt))

        bigA = sb("bigA", [128, 32768], BF16)
        bigB = sb("bigB", [128, 32768], BF16)
        bigC = sb("bigC", [128, 8192], BF16)
        stg = sb("stg", [128, 2048], F32)
        srow_f = sb("srow_f", [128, 1024], F32)
        xbuf = sb("xbuf", [128, 2048], F32)
        x2w2 = sb("x2w2", [128, 2048], F32)
        xm = sb("xm", [128, 1024], F32)
        stbuf = sb("stbuf", [128, 16], F32)
        outr = sb("outr", [128, 2048], F32)
        xlnb = sb("xlnb", [128, 1024], BF16)
        bcat = sb("bcat", [96, 1], F32)
        WUQ = sb("WUQ", [128, 1024], BF16)
        WUK = sb("WUK", [128, 1024], BF16)
        WUV = sb("WUV", [128, 1024], BF16)
        idtb = sb("idtb_s", [128, 128], BF16)
        idtf = sb("idtf_s", [128, 128], F32)
        onesb = sb("onesb", [128, 128], BF16)
        onesf = sb("onesf", [128, 128], F32)
        b1p = sb("b1p", [128, 32], F32)
        g2p = sb("g2p", [128, 32], F32)
        b2p = sb("b2p", [128, 32], F32)
        ppx = sb("ppx", [128, 8], F32)
        wabs = sb("wabs", [128, 64], F32)
        g1pp = sb("g1pp", [128, 8], F32)
        b1spp = sb("b1spp", [128, 8], F32)
        b1sbf = sb("b1sbf", [128, 8], BF16)
        b1tot = sb("b1tot", [128, 32], F32)
        b2b = sb("b2b", [128, 1024], F32)
        g1b = sb("g1b", [128, 1024], F32)
        zTw = sb("zTw", [128, 1024], BF16)
        gsq = sb("gsq", [128, 1024], BF16)

        Wcat = bigB[:, 31744:32512]
        WQR = bigB[:, 32512:32768]
        nhalf = srow_f[0:1, 0:512]
        Qc = bigB[:, 0:4096]
        Kc = bigB[:, 4096:8192]
        Vt = bigB[:, 8192:12288]
        Qrh = bigB[:, 12288:16384]
        cq = bigB[:, 16384:20480]
        ckv = bigB[:, 20480:24576]
        krh = bigB[:, 24576:28672]
        PT = bigB[:, 28672:30720]
        q2s = bigB[:, 30720:31744]
        WO = bigC[:, 0:8192]
        bO_b = x2w2[:, 1024:2048]
        mu2b = stg[:, 1024:1280].bitcast(BF16)
        rs2b = stg[:, 1280:1536].bitcast(BF16)

        psA = es.enter_context(nc.psum_tensor("psA", [128, 1024], F32))
        psB = es.enter_context(nc.psum_tensor("psB", [128, 1024], F32))
        psC = es.enter_context(nc.psum_tensor("psC", [128, 1024], F32))
        psD = es.enter_context(nc.psum_tensor("psD", [128, 1024], F32))

        sems = {k: es.enter_context(nc.semaphore(k))
                for k in ("dl", "dg", "dx0", "dx1", "ds0", "ds1",
                          "dw0", "dw1", "dw2", "dw3", "t", "v", "s", "bar")}
        p = Prog(sems)

        st_mu = stbuf[:, 0:1]
        st_u = stbuf[:, 1:2]
        st_s0 = stbuf[:, 2:3]
        st_r0 = stbuf[:, 3:4]
        st_t1 = stbuf[:, 4:5]
        st_rs = stbuf[:, 5:6]
        st_ss = stbuf[:, 6:7]

        def rsqrt_chain(u_ap, s0_ap, r0_ap, t1_ap, rs_ap):
            p.wait("scalar", "v")
            p.op("scalar", lambda e: e.activation(s0_ap, u_ap, ACT.Sqrt), inc="s")
            p.wait("vector", "s")
            p.op("vector", lambda e: e.reciprocal(rs_ap, s0_ap), inc="v")

        def ln_stats(xin_ap, ncols, dump=None):
            if dump is None:
                dump = stg[:, 0:ncols]
            p.op("vector", lambda e: e.reduce_sum(st_mu, xin_ap, AX))
            p.op("vector", lambda e: e.tensor_scalar(
                st_mu, st_mu, 1.0 / ncols, None, ALU.mult))
            p.op("vector", lambda e: e.tensor_scalar(
                xm[:, 0:ncols], xin_ap, st_mu, None, ALU.subtract), inc="v")
            p.wait("scalar", "v")
            p.op("scalar", lambda e, dump=dump: e.activation(
                dump, xm[:, 0:ncols], ACT.Square,
                accum_out=st_ss), inc="s")
            rt = p.op("scalar", lambda e, ncols=ncols: e.activation(
                st_s0, st_ss, ACT.Sqrt, scale=1.0 / ncols, bias=wabs[:, 63:64]), inc="s")
            p.wait("vector", "s", rt)
            p.op("vector", lambda e: e.reciprocal(st_rs, st_s0), inc="v")

        p.dma("sync", idtb[:, :], d["idtb"][:, :], "dl")
        p.dma("sync", idtf[:, :], d["idtf"][:, :], "dl")
        p.op("vector", lambda e: e.memset(ppx[:, :], 0.0))
        p.op("vector", lambda e: e.memset(wabs[:, 63:64], EPS))
        p.op("vector", lambda e: e.memset(bigB[32:64, 12288:28672], 0.0))
        p.op("vector", lambda e: e.memset(bigB[64:128, 12288:28672], 0.0))
        p.op("vector", lambda e: e.memset(bigB[32:64, 31232:31744], 0.0))
        p.op("vector", lambda e: e.memset(bigB[64:128, 31232:31744], 0.0))
        p.op("vector", lambda e: e.memset(bigB[32:64, 32512:32768], 0.0))
        p.op("vector", lambda e: e.memset(bigB[64:128, 32512:32768], 0.0))
        p.op("vector", lambda e: e.memset(WUQ[32:64, :], 0.0))
        p.op("vector", lambda e: e.memset(WUQ[64:128, :], 0.0))
        p.op("vector", lambda e: e.memset(WUK[32:64, :], 0.0))
        p.op("vector", lambda e: e.memset(WUK[64:128, :], 0.0))
        p.op("vector", lambda e: e.memset(WUV[32:64, :], 0.0))
        p.op("vector", lambda e: e.memset(WUV[64:128, :], 0.0))
        p.op("vector", lambda e: e.memset(srow_f[:, :], 0.0))
        p.op("vector", lambda e: e.memset(onesb[:, :], 1.0))
        p.op("vector", lambda e: e.memset(onesf[:, :], 1.0))
        p.op("vector", lambda e: e.memset(nhalf[:, :], -0.5))
        p.op("vector", lambda e: e.memset(cq[32:33, :], 1.0))
        p.op("vector", lambda e: e.memset(ckv[32:33, :], 1.0))
        p.op("vector", lambda e: e.memset(krh[32:33, :], 1.0), inc="v")

        for ci in range(8):
            p.dma("gpsimd", stg[:, ci * 32:(ci + 1) * 32],
                  d["W_DQ"][ci * 128:(ci + 1) * 128, :], "dg")
            p.dma("gpsimd", stg[:, 256 + ci * 32:256 + (ci + 1) * 32],
                  d["W_DKV"][ci * 128:(ci + 1) * 128, :], "dg")
            p.dma("gpsimd", stg[:, 512 + ci * 32:512 + (ci + 1) * 32],
                  d["W_KR"][ci * 128:(ci + 1) * 128, :], "dg")
        p.dma("gpsimd", bcat[0:32, :], d["b_DQ"].rearrange("(a b) -> a b", b=1), "dg")
        p.dma("gpsimd", bcat[32:64, :], d["b_DKV"].rearrange("(a b) -> a b", b=1), "dg")
        p.dma("gpsimd", bcat[64:96, :], d["b_KR"].rearrange("(a b) -> a b", b=1), "dg")
        p.wait("vector", "dg")
        for ci in range(8):
            for k, base in ((0, 0), (1, 256), (2, 512)):
                p.op("vector", lambda e, ci=ci, k=k, base=base: e.tensor_copy(
                    Wcat[:, ci * 96 + k * 32:ci * 96 + (k + 1) * 32],
                    stg[:, base + ci * 32:base + (ci + 1) * 32]))
        vfree = p.op("vector", lambda e: e.memset(stbuf[:, 15:16], 0.0),
                     inc="v")

        for wnm, bnm, dst, w in [("W_UQ", "b_UQ", WUQ, 1024),
                                 ("W_UK", "b_UK", WUK, 1024),
                                 ("W_UV", "b_UV", WUV, 1024),
                                 ("W_QR", "b_QR", WQR, 256)]:
            p.wait("gpsimd", "v", vfree)
            p.dma("gpsimd", stg[0:32, 0:w], d[wnm][:, :], "dg")
            p.dma("gpsimd", stg[32:33, 0:w],
                  d[bnm].rearrange("(a b) -> b a", b=1), "dg")
            p.wait("vector", "dg")
            vfree = p.op("vector", lambda e, dst=dst, w=w: e.tensor_copy(
                dst[0:33, 0:w], stg[0:33, 0:w]), inc="v")

        p.wait("tensor", "dl")
        for src, dst, nr in [("b1", b1p, 32), ("ln2_g", g2p, 32),
                             ("ln2_b", b2p, 32), ("ln1_g", g1pp, 8),
                             ("ln1_b", b1spp, 8)]:
            p.wait("gpsimd", "v", vfree)
            p.dma("gpsimd", stg[0:nr, 0:128],
                  d[src].rearrange("(a b) -> a b", b=128), "dg")
            p.wait("tensor", "dg")
            p.op("tensor", lambda e, nr=nr: e.matmul(
                psD[:, 0:nr], stg[0:nr, 0:128], idtf[0:nr, 0:nr],
                start=True, stop=True), inc="t")
            p.wait("vector", "t")
            vfree = p.op("vector", lambda e, dst=dst, nr=nr: e.tensor_copy(
                dst[:, 0:nr], psD[:, 0:nr]), inc="v")

        p.wait("tensor", "dl")
        xlT = bigC[:, 0:4096]
        xm_p = [xm[:, 0:1024], x2w2[:, 1024:2048]]
        dump_p = [stg[:, 0:1024], stg[:, 1024:2048]]
        xlnb_p = [xlnb[:, 0:1024], zTw[:, 0:1024].bitcast(BF16)]
        stb = [stbuf[:, 0:8], stbuf[:, 8:16]]
        v_xfree = {}
        v_sub = {}
        v_u = {}
        v_cp = {}
        s_sq = {}
        s_rt = {}
        s_vx = {}
        t_tr = {}
        t_dn = 0
        for i in range(NT + 1):
            j = i - 1
            if i < NT:
                par = i % 2
                dxs = "dx0" if par == 0 else "dx1"
                if i >= 2:
                    p.wait("sync", "v", v_xfree[i - 2])
                p.dma("sync", xbuf[:, par * 1024:(par + 1) * 1024],
                      d["x"][i * 128:(i + 1) * 128, :], dxs)
            if i < NT:
                par = i % 2
                sb_i = stb[par]
                dxs = "dx0" if par == 0 else "dx1"
                p.wait("vector", dxs)
                if i >= 2 and (i - 2) in s_vx:
                    p.wait("vector", "s", s_vx[i - 2])
                p.op("vector", lambda e, par=par, sb_i=sb_i: e.reduce_sum(
                    sb_i[:, 0:1], xbuf[:, par * 1024:(par + 1) * 1024], AX))
                p.op("vector", lambda e, sb_i=sb_i: e.tensor_scalar(
                    sb_i[:, 0:1], sb_i[:, 0:1], 1.0 / 1024, None, ALU.mult))
                v_sub[i] = p.op("vector", lambda e, par=par, sb_i=sb_i:
                                e.tensor_scalar(
                                    xm_p[par], xbuf[:, par * 1024:(par + 1) * 1024],
                                    sb_i[:, 0:1], None, ALU.subtract), inc="v")
                v_xfree[i] = v_sub[i]
                p.wait("scalar", "v", v_sub[i])
                p.op("scalar", lambda e, par=par, sb_i=sb_i:
                     e.activation(dump_p[par], xm_p[par], ACT.Square,
                                  accum_out=sb_i[:, 6:7]), inc="s")
                s_rt[i] = p.op("scalar", lambda e, sb_i=sb_i: e.activation(
                    sb_i[:, 2:3], sb_i[:, 6:7], ACT.Sqrt,
                    scale=1.0 / 1024, bias=wabs[:, 63:64]), inc="s")
            if j >= 0:
                pj = j % 2
                sb_j = stb[pj]
                p.wait("vector", "s", s_rt[j])
                vrj = p.op("vector", lambda e, sb_j=sb_j: e.reciprocal(
                    sb_j[:, 5:6], sb_j[:, 2:3]), inc="v")
                p.wait("scalar", "v", vrj)
                vx = p.op("scalar", lambda e, pj=pj, sb_j=sb_j: e.mul(
                    xlnb_p[pj], xm_p[pj], sb_j[:, 5:6]), inc="s")
                s_vx[j] = vx
                p.wait("tensor", "s", vx)
                if j >= 1:
                    p.wait("tensor", "v", v_cp[j - 1])
                if j % 4 == 0 and j >= 4:
                    p.wait("tensor", "t", t_dn)
                for c in range(8):
                    t_tr[j] = p.op("tensor", lambda e, c=c, pj=pj: e.matmul(
                        psD[:, c * 128:(c + 1) * 128],
                        xlnb_p[pj][:, c * 128:(c + 1) * 128], idtb[:, :],
                        start=True, stop=True),
                        inc="t" if c == 7 else None)
                p.wait("scalar", "t", t_tr[j])
                if j % 4 == 0 and j >= 4:
                    p.wait("scalar", "t", t_dn)
                jw = j % 4
                v_cp[j] = p.op("scalar", lambda e, jw=jw: e.copy(
                    bass.AP(bigC, jw * 128, [[8192, 128], [512, 8], [1, 128]]),
                    psD[:, :].rearrange("p (c w) -> p c w", c=8)), inc="s")
                if j % 4 == 3:
                    w = j // 4
                    p.wait("tensor", "s", v_cp[j])
                    for c in range(8):
                        t_dn = p.op("tensor", lambda e, c=c: e.matmul(
                            psC[0:96, 0:512], Wcat[:, c * 96:(c + 1) * 96],
                            xlT[:, c * 512:(c + 1) * 512],
                            start=(c == 0), stop=(c == 7)),
                            inc="t" if c == 7 else None)
                    p.wait("vector", "t", t_dn)
                    for r0, dst in ((0, cq), (32, ckv), (64, krh)):
                        p.op("vector", lambda e, r0=r0, dst=dst, w=w:
                             e.tensor_scalar(
                                 dst[0:32, w * 512:(w + 1) * 512],
                                 psC[r0:r0 + 32, 0:512], bcat[r0:r0 + 32, :],
                                 None, ALU.add), inc="v")
        if dbg:
            p.barrier()
            for nm, src in [("o_cq", cq), ("o_ckv", ckv), ("o_krh", krh)]:
                p.dma("gpsimd", dbg_t[nm][:, :], src[0:33, :], "dg")
        p.barrier()

        p1_vq = {}

        def emit_pass1(js):
            for j in js:
                sl = (j % 2) * 1024
                if j >= 2:
                    p.wait("sync", "v", p1_vq[j - 2])
                if j < 32:
                    srcap = d["w1"][j * 128:(j + 1) * 128, :]
                    col = j
                else:
                    i2, q2 = (j - 32) // 4, (j - 32) % 4
                    srcap = d["w2"][i2 * 128:(i2 + 1) * 128,
                                    q2 * 1024:(q2 + 1) * 1024]
                    col = 32 + (j - 32) // 2
                dxs = "dx0" if j % 2 == 0 else "dx1"
                p.dma("sync", stg[:, sl:sl + 1024], srcap, dxs)
                p.wait("vector", dxs)
                p.op("vector", lambda e, j=j, sl=sl, col=col: e.reduce_sum(
                    wabs[:, col:col + 1] if j < 32 else stbuf[:, 8:9],
                    stg[:, sl:sl + 1024], AX,
                    apply_absolute_value=True), inc="v")
                if j < 32:
                    p1_vq[j] = p.cnt["v"]
                elif (j - 32) % 2 == 0:
                    p1_vq[j] = p.op("vector", lambda e, col=col: e.tensor_copy(
                        wabs[:, col:col + 1], stbuf[:, 8:9]), inc="v")
                else:
                    p1_vq[j] = p.op("vector", lambda e, col=col: e.tensor_tensor(
                        wabs[:, col:col + 1], wabs[:, col:col + 1],
                        stbuf[:, 8:9], ALU.add), inc="v")

        gkc = 0
        smark = {}
        expmark = {}
        pvmark = {}
        evacmark = 0

        def emit_pv(kc_g, kc):
            p.wait("tensor", "s", expmark[kc_g])
            first = (kc == 0)
            last = (kc == 31)
            if first:
                p.wait("tensor", "v", evacmark)
            par = (kc_g % 2) * 1024
            for hf in range(2):
                p.op("tensor", lambda e, kc=kc, hf=hf, par=par, first=first,
                     last=last: e.matmul(
                         psC[:, hf * 512:(hf + 1) * 512],
                         Vt[:, kc * 128:(kc + 1) * 128],
                         PT[:, par + hf * 512:par + (hf + 1) * 512],
                         start=first, stop=last))
            for hf in range(2):
                pvmark[kc_g] = p.op(
                    "tensor", lambda e, hf=hf, par=par, first=first, last=last:
                    e.matmul(psD[:, hf * 512:(hf + 1) * 512], onesb[:, :],
                             PT[:, par + hf * 512:par + (hf + 1) * 512],
                             start=first, stop=last),
                    inc="t" if hf == 1 else None)

        deferred_norm = []
        p1_queue = list(range(64))
        for h in range(H):
            if gkc > 0:
                p.wait("tensor", "s", expmark[gkc - 1])
                p.wait("tensor", "v", evacmark)
            vm = {}
            sm = {}
            for di, (dst, wt, src) in enumerate(
                    ((Qc, WUQ, cq), (Kc, WUK, ckv))):
                for g4 in range(4):
                    k = di * 4 + g4
                    ps = psA if k % 2 == 0 else psB
                    if k >= 2:
                        p.wait("tensor", "v", vm.get(k - 2, 0))
                        p.wait("tensor", "s", sm.get(k - 2, 0))
                    for half in range(2):
                        tm = p.op("tensor", lambda e, wt=wt, src=src, g4=g4,
                                  ps=ps, h=h, half=half: e.matmul(
                                      ps[:, half * 512:(half + 1) * 512],
                                      wt[:, h * 128:(h + 1) * 128],
                                      src[:, g4 * 1024 + half * 512:
                                          g4 * 1024 + (half + 1) * 512],
                                      start=True, stop=True),
                                  inc="t" if half == 1 else None)
                    p.wait("vector", "t", tm)
                    vm[k] = p.op("vector", lambda e, dst=dst, g4=g4, ps=ps:
                                 e.tensor_copy(
                                     dst[:, g4 * 1024:g4 * 1024 + 512],
                                     ps[:, 0:512]), inc="v")
                    p.wait("scalar", "t", tm)
                    sm[k] = p.op("scalar", lambda e, dst=dst, g4=g4, ps=ps:
                                 e.copy(dst[:, g4 * 1024 + 512:(g4 + 1) * 1024],
                                        ps[:, 512:1024]), inc="s")
            for g8 in range(8):
                k = 8 + g8
                ps = psA if k % 2 == 0 else psB
                p.wait("tensor", "v", vm.get(k - 2, 0))
                p.wait("tensor", "s", sm.get(k - 2, 0))
                tm = p.op("tensor", lambda e, g8=g8, ps=ps, h=h: e.matmul(
                    ps[0:32, 0:512], WQR[:, h * 32:(h + 1) * 32],
                    cq[:, g8 * 512:(g8 + 1) * 512],
                    start=True, stop=True), inc="t")
                p.wait("vector", "t", tm)
                vm[k] = p.op("vector", lambda e, g8=g8, ps=ps: e.tensor_copy(
                    Qrh[0:32, g8 * 512:(g8 + 1) * 512], ps[0:32, 0:512]),
                    inc="v")
                sm[k] = sm.get(k - 1, 0)
            vmv = {}
            for vt in range(NT):
                slot = vt % 8
                ps = psA if slot < 4 else psB
                off = 512 + (slot % 4) * 128
                if vt >= 8 and vt % 4 == 0:
                    p.wait("tensor", "v", vmv[vt - 8])
                tm = p.op("tensor", lambda e, vt=vt, ps=ps, off=off, h=h:
                    e.matmul(ps[:, off:off + 128],
                             ckv[:, vt * 128:(vt + 1) * 128],
                             WUV[:, h * 128:(h + 1) * 128],
                             start=True, stop=True),
                    inc="t" if vt % 4 == 3 else None)
                if vt % 4 == 3:
                    p.wait("vector", "t", tm)
                    vmv[vt - 3] = p.op(
                        "vector", lambda e, vt=vt, ps=ps: e.tensor_copy(
                            Vt[:, (vt - 3) * 128:(vt + 1) * 128],
                            ps[:, 512:1024]), inc="v")
            for g8 in range(8):
                vmk = p.op("vector", lambda e, g8=g8: e.tensor_mul(
                    q2s[:, 0:512], Kc[:, g8 * 512:(g8 + 1) * 512],
                    Kc[:, g8 * 512:(g8 + 1) * 512]))
                vmk = p.op("vector", lambda e, g8=g8: e.tensor_mul(
                    q2s[0:32, 512:1024], krh[0:32, g8 * 512:(g8 + 1) * 512],
                    krh[0:32, g8 * 512:(g8 + 1) * 512]), inc="v")
                p.wait("tensor", "v", vmk)
                tm = p.op("tensor", lambda e: e.matmul(
                    psD[0:1, 0:512], onesb[:, 0:1], q2s[:, 0:512],
                    start=True, stop=False))
                tm = p.op("tensor", lambda e: e.matmul(
                    psD[0:1, 0:512], onesb[:, 0:1], q2s[:, 512:1024],
                    start=False, stop=True), inc="t")
                p.wait("vector", "t", tm)
                p.op("vector", lambda e, g8=g8: e.reduce_max(
                    stbuf[0:1, 8 + g8:9 + g8], psD[0:1, 0:512], AX), inc="v")
            p.op("vector", lambda e: e.reduce_max(
                stbuf[0:1, 7:8], stbuf[0:1, 8:16], AX), inc="v")
            for g8 in range(8):
                vmk = p.op("vector", lambda e, g8=g8: e.tensor_mul(
                    q2s[:, 0:512], Qc[:, g8 * 512:(g8 + 1) * 512],
                    Qc[:, g8 * 512:(g8 + 1) * 512]))
                vmk = p.op("vector", lambda e, g8=g8: e.tensor_mul(
                    q2s[0:32, 512:1024], Qrh[0:32, g8 * 512:(g8 + 1) * 512],
                    Qrh[0:32, g8 * 512:(g8 + 1) * 512]), inc="v")
                p.wait("tensor", "v", vmk)
                tm = p.op("tensor", lambda e: e.matmul(
                    psD[0:1, 0:512], onesb[:, 0:1], q2s[:, 0:512],
                    start=True, stop=False))
                tm = p.op("tensor", lambda e: e.matmul(
                    psD[0:1, 0:512], onesb[:, 0:1], q2s[:, 512:1024],
                    start=False, stop=True), inc="t")
                p.wait("vector", "t", tm)
                p.op("vector", lambda e, g8=g8: e.scalar_tensor_tensor(
                    Qrh[32:33, g8 * 512:(g8 + 1) * 512],
                    psD[0:1, 0:512], stbuf[0:1, 7:8], nhalf[:, 0:512],
                    ALU.add, ALU.mult), inc="v")
            prep_v = p.cnt["v"]
            if dbg and h == 0:
                p.barrier()
                for nm, src in [("o_qc", Qc), ("o_kc", Kc), ("o_vt", Vt),
                                ("o_qrh", Qrh[0:33, :])]:
                    p.dma("gpsimd", dbg_t[nm][:, :], src, "dg")
                p.barrier()

            for g in range(4):
                q0 = g * 1024
                for kc in range(32):
                    if deferred_norm and kc in (2, 3):
                        deferred_norm.pop(0)()
                    if p1_queue and kc in (6, 10, 14, 18, 22, 26):
                        emit_pass1([p1_queue.pop(0)])
                    if gkc >= 2:
                        p.wait("tensor", "s", expmark[gkc - 2])
                    if kc == 0 and g == 0:
                        p.wait("tensor", "v", prep_v)
                    ps = psA if (gkc % 2 == 0) else psB
                    for hf in range(2):
                        p.op("tensor", lambda e, kc=kc, hf=hf, ps=ps, q0=q0:
                             e.matmul(ps[:, hf * 512:(hf + 1) * 512],
                                      Kc[:, kc * 128:(kc + 1) * 128],
                                      Qc[:, q0 + hf * 512:q0 + (hf + 1) * 512],
                                      start=True, stop=False))
                        smark[gkc] = p.op(
                            "tensor", lambda e, kc=kc, hf=hf, ps=ps, q0=q0:
                            e.matmul(ps[:, hf * 512:(hf + 1) * 512],
                                     krh[:, kc * 128:(kc + 1) * 128],
                                     Qrh[:, q0 + hf * 512:q0 + (hf + 1) * 512],
                                     start=False, stop=True),
                            inc="t" if hf == 1 else None)
                    p.wait("scalar", "t", smark[gkc])
                    if gkc >= 2:
                        p.wait("scalar", "t", pvmark[gkc - 2])
                    par = (gkc % 2) * 1024
                    expmark[gkc] = p.op(
                        "scalar", lambda e, ps=ps, par=par: e.activation(
                            PT[:, par:par + 1024], ps[:, :], ACT.Exp,
                            scale=SCALE), inc="s")
                    if kc >= 1:
                        emit_pv(gkc - 1, kc - 1)
                    gkc += 1
                emit_pv(gkc - 1, 31)
                p.wait("vector", "t", pvmark[gkc - 1])
                p.op("vector", lambda e: e.tensor_copy(x2w2[:, 0:1024],
                                                       psD[:, :]))
                evacmark = p.op("vector", lambda e: e.tensor_copy(
                    outr[:, 0:1024], psC[:, :]), inc="v")

                def _dn1():
                    p.op("vector", lambda e: e.reciprocal(
                        x2w2[:, 1024:2048], x2w2[:, 0:1024]))

                def _dn2(h=h, q0=q0):
                    p.op("vector", lambda e, h=h, q0=q0: e.tensor_tensor(
                        bigA[:, h * 4096 + q0:h * 4096 + q0 + 1024],
                        outr[:, 0:1024], x2w2[:, 1024:2048], ALU.mult),
                        inc="v")
                deferred_norm += [_dn1, _dn2]
        for fn in deferred_norm:
            fn()
        deferred_norm = []
        if p1_queue:
            emit_pass1(list(p1_queue))
            p1_queue = []
        if dbg:
            p.barrier()
            p.dma("gpsimd", dbg_t["o_oT"][:, :], bigA[:, :], "dg")
        p.barrier()

        vfree = p.cnt["v"]
        for ci in range(4):
            if ci >= 1:
                p.wait("sync", "v", vfree)
            p.dma("sync", stg[:, 0:1024],
                  d["W_O"][(2 * ci) * 128:(2 * ci + 1) * 128, :], "dl")
            p.dma("sync", stg[:, 1024:2048],
                  d["W_O"][(2 * ci + 1) * 128:(2 * ci + 2) * 128, :], "dl")
            p.wait("vector", "dl")
            vfree = p.op("vector", lambda e, ci=ci: e.tensor_copy(
                WO[:, (2 * ci) * 1024:(2 * ci + 2) * 1024], stg[:, :]), inc="v")
        p.wait("sync", "v", vfree)
        p.dma("sync", srow_f[0:1, 0:1024],
              d["b_O"].rearrange("(a b) -> b a", b=1), "dl")
        p.wait("tensor", "dl")
        p.wait("tensor", "v", vfree)
        for hf in range(2):
            tm = p.op("tensor", lambda e, hf=hf: e.matmul(
                psC[:, hf * 512:(hf + 1) * 512], onesf[:, :],
                srow_f[:, hf * 512:(hf + 1) * 512], start=True, stop=True),
                inc="t" if hf == 1 else None)
        p.wait("vector", "t", tm)
        p.op("vector", lambda e: e.tensor_copy(bO_b[:, 0:1024], psC[:, :]),
             inc="v")
        for k, (c0, c1, n_el) in enumerate([(0, 32, HID * D), (32, 48, D * HID)]):
            p.op("vector", lambda e, c0=c0, c1=c1: e.reduce_sum(
                stbuf[:, 0:1], wabs[:, c0:c1], AX), inc="v")
            p.wait("tensor", "v")
            tm = p.op("tensor", lambda e: e.matmul(
                psB[0:1, 0:128], stbuf[:, 0:1], idtf[:, :],
                start=True, stop=True), inc="t")
            p.wait("vector", "t", tm)
            p.op("vector", lambda e: e.reduce_sum(
                srow_f[0:1, 0:1], psB[0:1, 0:128], AX))
            p.op("vector", lambda e, n_el=n_el: e.tensor_scalar(
                srow_f[0:1, 0:1], srow_f[0:1, 0:1], 1.0 / n_el, 1e-5,
                ALU.mult, ALU.max), inc="v")
            p.wait("tensor", "v")
            tm = p.op("tensor", lambda e: e.matmul(
                psB[:, 128:129], onesf[:, :], srow_f[:, 0:1],
                start=True, stop=True), inc="t")
            p.wait("vector", "t", tm)
            p.op("vector", lambda e, k=k: e.tensor_copy(
                ppx[:, 3 * k:3 * k + 1], psB[:, 128:129]))
            p.op("vector", lambda e, k=k: e.tensor_scalar(
                ppx[:, 3 * k + 1:3 * k + 2], ppx[:, 3 * k:3 * k + 1], 0.5,
                None, ALU.mult))
            p.op("vector", lambda e, k=k: e.tensor_scalar(
                ppx[:, 3 * k + 2:3 * k + 3], ppx[:, 3 * k:3 * k + 1], -0.5,
                None, ALU.mult), inc="v")
        p.wait("sync", "v")
        p.dma("sync", srow_f[0:1, 0:1024],
              d["ln1_g"].rearrange("(a b) -> b a", b=1), "dl")
        p.wait("tensor", "dl")
        for hf in range(2):
            tm = p.op("tensor", lambda e, hf=hf: e.matmul(
                psB[:, hf * 512:(hf + 1) * 512], onesf[:, :],
                srow_f[:, hf * 512:(hf + 1) * 512],
                start=True, stop=True), inc="t" if hf == 1 else None)
        p.wait("vector", "t", tm)
        p.op("vector", lambda e: e.tensor_scalar(
            g1b[:, :], psB[:, :], ppx[:, 0:1], None, ALU.mult), inc="v")
        p.op("vector", lambda e: e.tensor_scalar(
            g2p[:, :], g2p[:, :], ppx[:, 3:4], None, ALU.mult))
        p.op("vector", lambda e: e.tensor_scalar(
            b2p[:, :], b2p[:, :], ppx[:, 3:4], None, ALU.mult))
        p.op("vector", lambda e: e.tensor_scalar(
            b1spp[:, :], b1spp[:, :], ppx[:, 0:1], None, ALU.mult), inc="v")
        p.wait("sync", "v")
        p.dma("sync", srow_f[0:1, 0:1024],
              d["b2"].rearrange("(a b) -> b a", b=1), "dl")
        p.wait("tensor", "dl")
        for hf in range(2):
            tm = p.op("tensor", lambda e, hf=hf: e.matmul(
                psB[:, hf * 512:(hf + 1) * 512], onesf[:, :],
                srow_f[:, hf * 512:(hf + 1) * 512],
                start=True, stop=True), inc="t" if hf == 1 else None)
        p.wait("vector", "t", tm)
        p.op("vector", lambda e: e.tensor_copy(b2b[:, :], psB[:, :]), inc="v")
        p.barrier()

        p.op("vector", lambda e: e.reduce_sum(wabs[:, 62:63], bO_b[:, 0:1024],
                                              AX))
        p.op("vector", lambda e: e.tensor_scalar(
            wabs[:, 62:63], wabs[:, 62:63], 1.0 / 1024, None, ALU.mult),
            inc="v")

        def emit_w2_tile(jj):
            i2, q2 = jj // 4, jj % 4
            dws = "dw0" if jj % 2 == 0 else "dw1"
            sl = 0
            if jj >= 1:
                p.wait("sync", "v", w2q[jj - 1])
            p.dma("sync", stg[:, sl:sl + 1024],
                  d["w2"][i2 * 128:(i2 + 1) * 128,
                          q2 * 1024:(q2 + 1) * 1024], dws)
            p.wait("vector", dws)
            if jj >= 1:
                p.wait("vector", "t", w2t[jj - 1])
            p.op("vector", lambda e, sl=sl: e.tensor_scalar(
                xlnb[:, :], stg[:, sl:sl + 1024], ppx[:, 5:6], None,
                ALU.is_lt))
            zsl = 0
            w2q[jj] = p.op("vector", lambda e, sl=sl, zsl=zsl:
                           e.scalar_tensor_tensor(
                               zTw[:, zsl:zsl + 1024], stg[:, sl:sl + 1024],
                               ppx[:, 4:5], xlnb[:, :],
                               ALU.is_gt, ALU.subtract), inc="v")
            p.wait("tensor", "v", w2q[jj])
            if jj >= 1:
                p.wait("tensor", "s", w2c[jj - 1])
            for c in range(8):
                w2t[jj] = p.op("tensor", lambda e, c=c, zsl=zsl: e.matmul(
                    psD[:, c * 128:(c + 1) * 128],
                    zTw[:, zsl + c * 128:zsl + (c + 1) * 128], idtb[:, :],
                    start=True, stop=True), inc="t" if c == 7 else None)
            p.wait("scalar", "t", w2t[jj])
            dst = bass.AP(bigB, (q2 * 8) * 1024 + i2 * 128,
                          [[32768, 128], [1024, 8], [1, 128]])
            w2c[jj] = p.op("scalar", lambda e, dst=dst: e.copy(
                dst, psD[:, :].rearrange("p (c w) -> p c w", c=8)), inc="s")

        w2q, w2t, w2c = {}, {}, {}
        v_res = {}
        v_att = {}
        s_rt2 = {}
        xmC = [xm[:, 0:1024], stg[:, 1024:2048]]
        stbC = [stbuf[:, 0:8], stbuf[:, 8:16]]
        for i in range(NT + 1):
            j = i - 1
            if i < NT:
                par = i % 2
                dxs = "dx0" if par == 0 else "dx1"
                if i >= 2:
                    p.wait("sync", "v", v_res.get(i - 2, 0))
                p.dma("sync", xbuf[:, par * 1024:(par + 1) * 1024],
                      d["x"][i * 128:(i + 1) * 128, :], dxs)
                if i >= 1:
                    p.wait("tensor", "v", v_att[i - 1])
                for hf in range(2):
                    for c in range(8):
                        tm = p.op("tensor", lambda e, i=i, hf=hf, c=c: e.matmul(
                            psA[:, hf * 512:(hf + 1) * 512],
                            bigA[:, c * 4096 + i * 128:c * 4096 + (i + 1) * 128],
                            WO[:, c * 1024 + hf * 512:c * 1024 + (hf + 1) * 512],
                            start=(c == 0), stop=(c == 7)),
                            inc="t" if (hf == 1 and c == 7) else None)
                t_oproj = tm
            if j >= 0:
                pj = j % 2
                sbj = stbC[pj]
                dxsj = "dx0" if pj == 0 else "dx1"
                dssj = "ds0" if pj == 0 else "ds1"
                p.wait("vector", "s", s_rt2[j])
                p.op("vector", lambda e, sbj=sbj: e.reciprocal(
                    sbj[:, 5:6], sbj[:, 2:3]))
                p.wait("vector", dxsj)
                p.wait("vector", dssj)
                osl = pj * 1024
                v_res[j] = p.op("vector", lambda e, pj=pj, osl=osl, sbj=sbj:
                                e.scalar_tensor_tensor(
                                    outr[:, osl:osl + 1024], xmC[pj],
                                    sbj[:, 5:6],
                                    xbuf[:, pj * 1024:(pj + 1) * 1024],
                                    ALU.mult, ALU.add),
                                inc="v")
                p.wait("gpsimd", "v", v_res[j])
                p.dma("gpsimd", d["x2s"][j * 128:(j + 1) * 128, :],
                      outr[:, osl:osl + 1024], dssj)
            if i < NT:
                par = i % 2
                sbi = stbC[par]
                p.wait("vector", "t", t_oproj)
                p.op("vector", lambda e, sbi=sbi: e.reduce_sum(
                    sbi[:, 0:1], psA[:, :], AX))
                p.op("vector", lambda e, sbi=sbi: e.scalar_tensor_tensor(
                    sbi[:, 0:1], sbi[:, 0:1], 1.0 / 1024, stbuf[:, 9:10],
                    ALU.mult, ALU.add) if False else e.scalar_tensor_tensor(
                    sbi[:, 0:1], sbi[:, 0:1], 1.0 / 1024, wabs[:, 62:63],
                    ALU.mult, ALU.add))
                v_att[i] = p.op("vector", lambda e, par=par, sbi=sbi:
                                e.scalar_tensor_tensor(
                                    xmC[par], psA[:, :], sbi[:, 0:1],
                                    bO_b[:, 0:1024], ALU.subtract, ALU.add),
                                inc="v")
                p.wait("scalar", "v", v_att[i])
                p.op("scalar", lambda e, par=par, sbi=sbi: e.activation(
                    psB[:, 0:1024], xmC[par], ACT.Square,
                    accum_out=sbi[:, 6:7]), inc="s")
                s_rt2[i] = p.op("scalar", lambda e, sbi=sbi: e.activation(
                    sbi[:, 2:3], sbi[:, 6:7], ACT.Sqrt,
                    scale=1.0 / 1024, bias=wabs[:, 63:64]), inc="s")
            if j >= 2:
                emit_w2_tile(j - 2)
        emit_w2_tile(30)
        emit_w2_tile(31)
        ds_x2 = (p.cnt["ds0"], p.cnt["ds1"])
        if dbg:
            p.barrier()
            p.dma("gpsimd", dbg_t["o_x2"][:, :], d["x2s"][:, :], "dg")
        p.barrier()

        p.op("vector", lambda e: e.tensor_copy(b1sbf[:, :], b1spp[:, :]),
             inc="v")
        for hc in range(32):
            dstc = bass.AP(bigA.bitcast(FP8), 32768 + hc * 1024,
                           [[65536, 128], [1, 1024]])
            if hc % 2 == 0:
                p.op("vector", lambda e, hc=hc, dstc=dstc: e.tensor_copy(
                    dstc, bigB[:, hc * 1024:(hc + 1) * 1024]), inc="v")
            else:
                p.op("scalar", lambda e, hc=hc, dstc=dstc: e.copy(
                    dstc, bigB[:, hc * 1024:(hc + 1) * 1024]), inc="s")
        if dbg:
            p.barrier()
            p.dma("gpsimd", dbg_t["o_w2t"][:, :], bigA[:, 16384:32768], "dg")
            p.dma("gpsimd", dbg_t["o_ppx"][:, :], ppx[:, :], "dg")
        w2cv = p.cnt["v"]
        w2cs = p.cnt["s"]

        p.wait("sync", "ds0", ds_x2[0])
        p.wait("sync", "ds1", ds_x2[1])
        NWIN = 8
        TPW = 4
        w1T8 = bigA.bitcast(FP8)
        gbuf = bigB[:, 0:16384]
        zTbase = 16384
        xwodd = bigB.bitcast(F32)
        v_win = {}
        v_out = {-1: 0, 0: 0}
        v_zt = {}
        vz_of = {}
        gelc = {}
        t_mm1 = {}
        v_mm2evac = []

        def xw_tile(w, tt):
            if w % 2 == 0:
                return (xbuf if tt < 2 else x2w2)[:, (tt % 2) * 1024:
                                                  (tt % 2) * 1024 + 1024]
            return xwodd[:, 12288 + tt * 1024:12288 + (tt + 1) * 1024]

        def emit_pre_steps(w):
            dxs = "dx0" if w % 2 == 0 else "dx1"
            zsl = zTbase + (w % 2) * 4096
            sb8 = stbuf[:, 8:16]
            steps = []
            m = {}

            def s_load():
                if w == 1:
                    p.wait("sync", "v", w2cv)
                    p.wait("sync", "s", w2cs)
                if w >= 2:
                    p.wait("sync", "v", v_win[w - 2])
                for tt in range(TPW):
                    p.dma("sync", xw_tile(w, tt),
                          d["x2s"][w * 512 + tt * 128:
                                   w * 512 + (tt + 1) * 128, :], dxs)
            steps.append(s_load)

            def mk_ln_steps(get_in, tt, second):
                def s1():
                    if not second and tt == 0:
                        p.wait("vector", dxs)
                    p.op("vector", lambda e: e.reduce_sum(
                        sb8[:, 0:1], get_in(), AX))
                    p.op("vector", lambda e: e.tensor_scalar(
                        sb8[:, 0:1], sb8[:, 0:1], 1.0 / 1024, None, ALU.mult))
                    m["vv"] = p.op("vector", lambda e: e.tensor_scalar(
                        xm[:, 0:1024], get_in(), sb8[:, 0:1], None,
                        ALU.subtract), inc="v")

                def s2():
                    p.wait("scalar", "v", m["vv"])
                    p.op("scalar", lambda e: e.activation(
                        psA[:, :], xm[:, 0:1024], ACT.Square,
                        accum_out=sb8[:, 6:7]), inc="s")
                    m["rt"] = p.op("scalar", lambda e: e.activation(
                        sb8[:, 2:3], sb8[:, 6:7], ACT.Sqrt,
                        scale=1.0 / 1024, bias=wabs[:, 63:64]), inc="s")

                def s5():
                    p.wait("vector", "s", m["rt"])
                    p.op("vector", lambda e: e.reciprocal(
                        sb8[:, 5:6], sb8[:, 2:3]))
                    if not second:
                        p.op("vector", lambda e: e.tensor_scalar(
                            xm[:, 0:1024], xm[:, 0:1024], sb8[:, 5:6], None,
                            ALU.mult), inc="v")
                    else:
                        vz_of[(w, tt)] = m["vz"] = p.op(
                            "vector", lambda e, tt=tt: e.scalar_tensor_tensor(
                                zbuf[tt][:, :], xm[:, 0:1024], sb8[:, 5:6],
                                g1b[:, :], ALU.mult, ALU.mult), inc="v")
                return [s1, s2, s5]

            for tt in range(TPW):
                xa_fn = (lambda w=w, tt=tt: xw_tile(w, tt))
                xm_fn = (lambda: xm[:, 0:1024])
                steps += mk_ln_steps(xa_fn, tt, False)
                steps += mk_ln_steps(xm_fn, tt, True)
            return steps

        def emit_zt_batch(w):
            zsl = zTbase + (w % 2) * 4096
            cpm = {}
            tzm = {}
            for tt in range(TPW):
                ps = psD if tt % 2 == 0 else psC
                p.wait("tensor", "v", vz_of[(w, tt)])
                if tt >= 2:
                    p.wait("tensor", "v", cpm[tt - 2])
                if tt % 2 == 1 and v_mm2evac:
                    p.wait("tensor", "v", v_mm2evac[-1])
                for c in range(8):
                    tzm[tt] = p.op("tensor", lambda e, c=c, tt=tt, ps=ps:
                                   e.matmul(
                        ps[:, c * 128:(c + 1) * 128],
                        zbuf[tt][:, c * 128:(c + 1) * 128], idtb[:, :],
                        start=True, stop=True), inc="t" if c == 7 else None)
                p.wait("vector", "t", tzm[tt])
                cpm[tt] = p.op("vector", lambda e, tt=tt, ps=ps, zsl=zsl:
                               e.tensor_copy(
                    bass.AP(bigB, zsl + tt * 128,
                            [[32768, 128], [512, 8], [1, 128]]),
                    ps[:, :].rearrange("p (c w) -> p c w", c=8)), inc="v")
            v_zt[w] = cpm[TPW - 1]

        w1ev = {}
        b1ct = {}
        vq1 = {}
        tq1 = {}

        bigCf = bigC.bitcast(F32)
        xlnb2 = bigC[:, 4096:5120]
        zbuf = [xlnb, xlnb2, bigC[:, 5120:6144], bigC[:, 6144:7168]]
        W1SEMS = ("dw0", "dw1", "dw2", "dw3")

        def w1slot(j):
            s = j % 4
            if s < 2:
                return stg[:, s * 1024:(s + 1) * 1024]
            return bigCf[:, (s - 2) * 1024:(s - 1) * 1024]

        def emit_w1_dma(j):
            if j >= 4:
                p.wait("sync", "v", vq1[j - 4])
            elif j < 2:
                p.wait("sync", "v", w2q[31])
            p.dma("sync", w1slot(j), d["w1"][j * 128:(j + 1) * 128, :],
                  W1SEMS[j % 4])

        def emit_w1_quant(j):
            xl = xlnb if j % 2 == 0 else xlnb2
            p.wait("vector", W1SEMS[j % 4])
            if j >= 2:
                p.wait("vector", "t", tq1[j - 2])
            p.op("vector", lambda e, j=j: e.tensor_scalar(
                xm[:, 0:1024], w1slot(j), ppx[:, 2:3], None, ALU.is_lt))
            vq1[j] = p.op("vector", lambda e, j=j, xl=xl:
                          e.scalar_tensor_tensor(
                xl[:, :], w1slot(j), ppx[:, 1:2], xm[:, 0:1024],
                ALU.is_gt, ALU.subtract), inc="v")

        def emit_w1_tr(j):
            xl = xlnb if j % 2 == 0 else xlnb2
            p.wait("tensor", "v", vq1[j])
            if j >= 1:
                p.wait("tensor", "s", w1ev[j - 1])
            for c in range(8):
                tq1[j] = p.op("tensor", lambda e, c=c, xl=xl: e.matmul(
                    psD[:, c * 128:(c + 1) * 128],
                    xl[:, c * 128:(c + 1) * 128], idtb[:, :],
                    start=True, stop=True), inc="t" if c == 7 else None)
            p.wait("scalar", "t", tq1[j])
            dst = bass.AP(bigA.bitcast(FP8), j * 128,
                          [[65536, 128], [4096, 8], [1, 128]])
            w1ev[j] = p.op("scalar", lambda e, dst=dst: e.copy(
                dst, psD[:, :].rearrange("p (c w) -> p c w", c=8)), inc="s")

        def emit_b1col(j):
            p.wait("tensor", "s", w1ev[j])
            for dc in range(8):
                tb = p.op("tensor", lambda e, j=j, dc=dc: e.matmul(
                    psC[:, j:j + 1],
                    w1T8[:, dc * 4096 + j * 128:dc * 4096 + (j + 1) * 128],
                    b1sbf[:, dc:dc + 1],
                    start=(dc == 0), stop=(dc == 7)),
                    inc="t" if dc == 7 else None)
            p.wait("vector", "t", tb)
            b1ct[j] = p.op("vector", lambda e, j=j: e.tensor_tensor(
                b1tot[:, j:j + 1], psC[:, j:j + 1], b1p[:, j:j + 1],
                ALU.add), inc="v")

        for fn in emit_pre_steps(0):
            fn()
        emit_zt_batch(0)
        for j in range(4):
            emit_w1_dma(j)
        for j in range(3):
            emit_w1_quant(j)
            emit_w1_tr(j)
        emit_b1col(0)
        emit_b1col(1)
        for w in range(NWIN):
            zsl = zTbase + (w % 2) * 4096
            gel = gelc
            vgsq = {}
            tsum = {}
            pre_steps = emit_pre_steps(w + 1) if w + 1 < NWIN else []
            p.wait("tensor", "s")
            if w >= 1:
                emit_zt_batch(w)
            p.wait("tensor", "v", v_zt[w])
            for hc in range(33):
                if pre_steps and len(pre_steps) == 25:
                    pre_steps.pop(0)()
                if w == 0 and hc + 3 < 32:
                    emit_w1_quant(hc + 3)
                if hc < 32:
                    off = (hc % 2) * 512
                    if w == 0:
                        p.wait("tensor", "s", w1ev[hc])
                    km2 = (w, hc - 2) if hc >= 2 else (w - 1, 30 + hc)
                    if km2 in gel:
                        p.wait("tensor", "s", gel[km2])
                    for c in range(8):
                        tmm = p.op("tensor", lambda e, hc=hc, c=c, off=off,
                                   zsl=zsl: e.matmul(
                                       psA[:, off:off + 512],
                                       w1T8[:, c * 4096 + hc * 128:
                                            c * 4096 + (hc + 1) * 128],
                                       bigB[:, zsl + c * 512:zsl + (c + 1) * 512],
                                       start=(c == 0), stop=(c == 7)),
                                   inc="t" if c == 7 else None)
                    t_mm1[(w, hc)] = tmm
                    p.wait("scalar", "t", tmm)
                    if w == 0:
                        p.wait("scalar", "v", b1ct[hc])
                    gel[(w, hc)] = p.op(
                        "scalar", lambda e, hc=hc, off=off: e.activation(
                            gbuf[:, hc * 512:(hc + 1) * 512],
                            psA[:, off:off + 512],
                            ACT.Gelu, bias=b1tot[:, hc:hc + 1]), inc="s")
                    p.wait("vector", "s", gel[(w, hc)])
                    if hc >= 2:
                        p.wait("vector", "t", tsum[hc - 2])
                    vgsq[hc] = p.op("vector", lambda e, hc=hc: e.tensor_mul(
                        gsq[:, (hc % 2) * 512:(hc % 2) * 512 + 512],
                        gbuf[:, hc * 512:(hc + 1) * 512],
                        gbuf[:, hc * 512:(hc + 1) * 512]), inc="v")
                if hc >= 1:
                    hp = hc - 1
                    p.wait("tensor", "s", gel[(w, hp)])
                    p.op("tensor", lambda e, hp=hp: e.matmul(
                        psB[0:1, 0:512], onesb[:, 0:1],
                        gbuf[:, hp * 512:(hp + 1) * 512],
                        start=(hp == 0), stop=(hp == 31)))
                    p.wait("tensor", "v", vgsq[hp])
                    tsum[hp] = p.op("tensor", lambda e, hp=hp: e.matmul(
                        psB[0:1, 512:1024], onesb[:, 0:1],
                        gsq[:, (hp % 2) * 512:(hp % 2) * 512 + 512],
                        start=(hp == 0), stop=(hp == 31)),
                        inc="t")
                if w == 0:
                    if hc + 2 < 32:
                        emit_b1col(hc + 2)
                    if hc + 3 < 32:
                        emit_w1_tr(hc + 3)
                    if hc + 4 < 32:
                        emit_w1_dma(hc + 4)
            p.wait("vector", "t", tsum[31])
            p.op("vector", lambda e: e.tensor_scalar(
                srow_f[0:1, 0:512], psB[0:1, 0:512], 1.0 / HID, None,
                ALU.mult))
            p.op("vector", lambda e: e.tensor_mul(
                srow_f[0:1, 512:1024], srow_f[0:1, 0:512],
                srow_f[0:1, 0:512]))
            vv = p.op("vector", lambda e: e.scalar_tensor_tensor(
                srow_f[0:1, 512:1024], psB[0:1, 512:1024], 1.0 / HID,
                srow_f[0:1, 512:1024], ALU.mult, ALU.subtract), inc="v")
            p.wait("scalar", "v", vv)
            rt = p.op("scalar", lambda e: e.activation(
                srow_f[0:1, 512:1024], srow_f[0:1, 512:1024], ACT.Sqrt,
                bias=wabs[0:1, 63:64]), inc="s")
            p.wait("vector", "s", rt)
            p.op("vector", lambda e: e.reciprocal(
                srow_f[0:1, 512:1024], srow_f[0:1, 512:1024]))
            vrc = p.op("vector", lambda e: e.tensor_mul(
                srow_f[0:1, 0:512], srow_f[0:1, 0:512],
                srow_f[0:1, 512:1024]), inc="v")
            p.wait("tensor", "v", vrc)
            p.op("tensor", lambda e: e.matmul(
                psB[:, 0:512], onesf[:, :], srow_f[:, 0:512],
                start=True, stop=True))
            tm = p.op("tensor", lambda e: e.matmul(
                psB[:, 512:1024], onesf[:, :], srow_f[:, 512:1024],
                start=True, stop=True), inc="t")
            p.wait("vector", "t", tm)
            p.op("vector", lambda e: e.tensor_copy(mu2b[:, :], psB[:, 0:512]))
            p.op("vector", lambda e: e.tensor_copy(rs2b[:, :], psB[:, 512:1024]),
                 inc="v")
            vap = {}
            for half in range(2):
                h0 = half * 16
                gb3 = gbuf[:, h0 * 512:(h0 + 16) * 512].rearrange(
                    "p (c w) -> p c w", c=16)
                m2r_bc = bass.AP(stg.bitcast(BF16), 2048,
                                 [[4096, 128], [0, 16], [1, 512]])
                rs2_bc = bass.AP(stg.bitcast(BF16), 2560,
                                 [[4096, 128], [0, 16], [1, 512]])
                p.op("vector", lambda e, gb3=gb3, rs2_bc=rs2_bc:
                     e.tensor_tensor(gb3, gb3, rs2_bc, ALU.mult))
                p.op("vector", lambda e, gb3=gb3, m2r_bc=m2r_bc:
                     e.tensor_tensor(gb3, gb3, m2r_bc, ALU.subtract), inc="v")
                for hc in range(h0, h0 + 16):
                    vap[hc] = p.op("vector", lambda e, hc=hc: e.tensor_scalar(
                        gbuf[:, hc * 512:(hc + 1) * 512],
                        gbuf[:, hc * 512:(hc + 1) * 512],
                        g2p[:, hc:hc + 1], b2p[:, hc:hc + 1],
                        ALU.mult, ALU.add), inc="v")
            if dbg and w == 0:
                p.barrier()
                p.dma("gpsimd", dbg_t["o_h2"][:, :], gbuf[:, 0:8192], "dg")
                p.dma("gpsimd", dbg_t["o_w1t"][:, :], bigA[:, 0:16384], "dg")
                p.dma("gpsimd", dbg_t["o_b1t"][:, :], b1tot[:, :], "dg")
                p.barrier()
            def emit_final(w, tt):
                osl = (tt % 2) * 1024
                sb0 = stbuf[:, 0:8]
                p.op("vector", lambda e, osl=osl, sb0=sb0: e.reduce_sum(
                    sb0[:, 0:1], outr[:, osl:osl + 1024], AX))
                p.op("vector", lambda e, sb0=sb0: e.tensor_scalar(
                    sb0[:, 0:1], sb0[:, 0:1], 1.0 / 1024, None, ALU.mult))
                vv = p.op("vector", lambda e, osl=osl, sb0=sb0: e.tensor_scalar(
                    outr[:, osl:osl + 1024], outr[:, osl:osl + 1024],
                    sb0[:, 0:1], None, ALU.subtract), inc="v")
                p.wait("scalar", "v", vv)
                p.op("scalar", lambda e, osl=osl, sb0=sb0: e.activation(
                    psA[:, :], outr[:, osl:osl + 1024], ACT.Square,
                    accum_out=sb0[:, 6:7]), inc="s")
                rt = p.op("scalar", lambda e, sb0=sb0: e.activation(
                    sb0[:, 2:3], sb0[:, 6:7], ACT.Sqrt,
                    scale=1.0 / 1024, bias=wabs[:, 63:64]), inc="s")
                p.wait("vector", "s", rt)
                p.op("vector", lambda e, sb0=sb0: e.reciprocal(
                    sb0[:, 5:6], sb0[:, 2:3]))
                v_out[TPW * w + tt] = p.op(
                    "vector", lambda e, w=w, tt=tt, osl=osl, sb0=sb0:
                    e.scalar_tensor_tensor(
                        outr[:, osl:osl + 1024], outr[:, osl:osl + 1024],
                        sb0[:, 5:6], xw_tile(w, tt),
                        ALU.mult, ALU.add),
                    inc="v")
                p.wait("gpsimd", "v", v_out[TPW * w + tt])
                p.dma("gpsimd",
                      d["y"][(TPW * w + tt) * 128:(TPW * w + tt + 1) * 128, :],
                      outr[:, osl:osl + 1024],
                      "ds0" if tt % 2 == 0 else "ds1")

            for tt in range(TPW):
                if w == 0 and tt == 0:
                    p.wait("tensor", "v", w2cv)
                    p.wait("tensor", "s", w2cs)
                if v_mm2evac:
                    p.wait("tensor", "v", v_mm2evac[-1])
                for hf in range(2):
                    for hc in range(32):
                        if tt == 0 and hf == 0:
                            p.wait("tensor", "v", vap[hc])
                        tm = p.op("tensor", lambda e, tt=tt, hf=hf, hc=hc:
                                  e.matmul(
                                      psC[:, hf * 512:(hf + 1) * 512],
                                      gbuf[:, hc * 512 + tt * 128:
                                           hc * 512 + (tt + 1) * 128],
                                      w1T8[:, 32768 + hc * 1024 + hf * 512:
                                           32768 + hc * 1024 + (hf + 1) * 512],
                                      start=(hc == 0), stop=(hc == 31)),
                                  inc="t" if hc == 31 else None)
                p.wait("vector", "t", tm)
                osl = (tt % 2) * 1024
                p.wait("vector", "ds0" if tt % 2 == 0 else "ds1")
                for hf in range(2):
                    v_mev = p.op("vector", lambda e, hf=hf, osl=osl:
                                 e.tensor_tensor(
                                     outr[:, osl + hf * 512:osl + (hf + 1) * 512],
                                     psC[:, hf * 512:(hf + 1) * 512],
                                     b2b[:, hf * 512:(hf + 1) * 512], ALU.add),
                                 inc="v" if hf == 1 else None)
                v_mm2evac.append(v_mev)
                if tt >= 1:
                    emit_final(w, tt - 1)
                for _ in range(9):
                    if pre_steps:
                        pre_steps.pop(0)()
            emit_final(w, TPW - 1)
            for fn in pre_steps:
                fn()
            pre_steps = []
            v_win[w] = v_out[TPW * w + TPW - 1]
        p.wait("gpsimd", "ds0")
        p.wait("gpsimd", "ds1")
        p.wait("gpsimd", "dg")

        with nc.Block() as block:
            @block.sync
            def _(e):
                for fn in p.items["sync"]:
                    fn(e)

            @block.tensor
            def _(e):
                for fn in p.items["tensor"]:
                    fn(e)

            @block.vector
            def _(e):
                for fn in p.items["vector"]:
                    fn(e)

            @block.scalar
            def _(e):
                for fn in p.items["scalar"]:
                    fn(e)

            @block.gpsimd
            def _(e):
                for fn in p.items["gpsimd"]:
                    fn(e)

    return nc, dbg_t


_EYE_BF = np.eye(128, dtype=ml_dtypes.bfloat16)
_EYE_F32 = np.eye(128, dtype=np.float32)


def make_in_maps(inputs):
    x = np.ascontiguousarray(np.asarray(inputs["L0_attn_h_t"], dtype=np.float32))
    base = {}
    for nm in WNAMES:
        key = ("L0_attn_" + nm) if ("L0_attn_" + nm) in inputs else ("L0_ffn_" + nm)
        base[nm] = np.ascontiguousarray(np.asarray(inputs[key], dtype=np.float32))
    base["idtb"] = _EYE_BF
    base["idtf"] = _EYE_F32
    return [dict(base, x=x[b]) for b in range(8)]


def kernel(**inputs):
    nc, _ = build_program(dbg=False)
    in_maps = make_in_maps(inputs)
    res = run_bass_kernel_spmd(nc, in_maps, core_ids=list(range(8)))
    return np.stack([res.results[b]["y"] for b in range(8)], axis=0)


# version 24
# speedup vs baseline: 1.0899x; 1.0062x over previous
import numpy as np
import ml_dtypes
from contextlib import ExitStack

import concourse.bass as bass
import concourse.mybir as mybir
from concourse.bass_utils import run_bass_kernel_spmd

F32 = mybir.dt.float32
BF16 = mybir.dt.bfloat16
FP8 = mybir.dt.float8e4
AX = mybir.AxisListType.X
ALU = mybir.AluOpType
ACT = mybir.ActivationFunctionType

S = 4096
D = 1024
H = 8
DK = 128
DR = 32
DC = 32
HID = 4096
NT = S // 128
EPS = 1e-5
SCALE = 1.0 / float(np.sqrt(DK + DR))

ENGINES = ("sync", "tensor", "vector", "scalar", "gpsimd")

WNAMES = ["W_DQ", "b_DQ", "W_UQ", "b_UQ", "W_DKV", "b_DKV", "W_UK", "b_UK",
          "W_UV", "b_UV", "W_QR", "b_QR", "W_KR", "b_KR", "W_O", "b_O",
          "ln1_g", "ln1_b", "w1", "b1", "ln2_g", "ln2_b", "w2", "b2"]


class Prog:

    def __init__(self, sems):
        self.items = {e: [] for e in ENGINES}
        self.sems = sems
        self.cnt = {k: 0 for k in sems}
        self.nbar = 0

    def wait(self, eng, sem, val=None):
        v = self.cnt[sem] if val is None else val
        h = self.sems[sem]
        self.items[eng].append(lambda e, h=h, v=v: e.wait_ge(h, v))

    AUTOSEM = {"vector": "v", "scalar": "s"}

    def op(self, eng, fn, inc=None, n=1):
        auto = self.AUTOSEM.get(eng)
        if auto is not None:
            assert inc in (None, auto), (eng, inc)
            self.wait(eng, auto)
            inc = auto
        if inc is None:
            self.items[eng].append(fn)
            return None
        h = self.sems[inc]
        self.cnt[inc] += n

        def run(e, fn=fn, h=h, n=n):
            fn(e).then_inc(h, n)
        self.items[eng].append(run)
        return self.cnt[inc]

    def dma(self, eng, out, in_, sem):
        return self.op(eng, lambda e, o=out, i=in_: e.dma_start(out=o, in_=i),
                       inc=sem, n=16)

    def barrier(self):
        for s in ("dl", "dx0", "dx1", "dw0", "dw1", "dw2", "dw3"):
            self.wait("sync", s)
        for s in ("dg", "ds0", "ds1"):
            self.wait("gpsimd", s)
        self.wait("tensor", "t")
        self.wait("vector", "v")
        self.wait("scalar", "s")
        self.nbar += 1
        tgt = 5 * self.nbar
        h = self.sems["bar"]
        self.cnt["bar"] = tgt
        for e in ENGINES:
            self.items[e].append(lambda eng, h=h: eng.sem_inc(h, 1))
            self.items[e].append(lambda eng, h=h, t=tgt: eng.wait_ge(h, t))


def build_program(dbg=False):
    nc = bass.Bass("TRN2", target_bir_lowering=False)

    d = {"x": nc.dram_tensor("x", [S, D], F32, kind="ExternalInput").ap()}
    shapes = {"W_DQ": [D, DC], "b_DQ": [DC], "W_UQ": [DC, H * DK],
              "b_UQ": [H * DK], "W_DKV": [D, DC], "b_DKV": [DC],
              "W_UK": [DC, H * DK], "b_UK": [H * DK], "W_UV": [DC, H * DK],
              "b_UV": [H * DK], "W_QR": [DC, H * DR], "b_QR": [H * DR],
              "W_KR": [D, DR], "b_KR": [DR], "W_O": [H * DK, D], "b_O": [D],
              "ln1_g": [D], "ln1_b": [D], "w1": [HID, D], "b1": [HID],
              "ln2_g": [HID], "ln2_b": [HID], "w2": [D, HID], "b2": [D]}
    for nm in WNAMES:
        d[nm] = nc.dram_tensor(nm, shapes[nm], F32, kind="ExternalInput").ap()
    d["idtb"] = nc.dram_tensor("idtb", [128, 128], BF16, kind="ExternalInput").ap()
    d["idtf"] = nc.dram_tensor("idtf", [128, 128], F32, kind="ExternalInput").ap()
    d["y"] = nc.dram_tensor("y", [S, D], F32, kind="ExternalOutput").ap()
    d["x2s"] = nc.dram_tensor("x2s", [S, D], F32).ap()
    dbg_t = {}
    if dbg:
        for nm, shp, dt in [
            ("o_cq", [33, S], BF16), ("o_ckv", [33, S], BF16),
            ("o_krh", [33, S], BF16), ("o_qc", [128, S], BF16),
            ("o_kc", [128, S], BF16), ("o_vt", [128, S], BF16),
            ("o_qrh", [33, S], BF16), ("o_oT", [128, H * S], BF16),
            ("o_x2", [S, D], F32), ("o_w1t", [128, 16384], BF16),
            ("o_w2t", [128, 16384], BF16), ("o_h2", [128, 32 * 256], BF16),
            ("o_ppx", [128, 8], F32), ("o_b1t", [128, 32], F32),
            ("o_zt", [128, 2048], BF16), ("o_gg", [128, 32 * 256], BF16),
        ]:
            dbg_t[nm] = nc.dram_tensor(nm, shp, dt, kind="ExternalOutput").ap()

    es = ExitStack()
    with es:
        def sb(name, shape, dt):
            return es.enter_context(nc.sbuf_tensor(name, shape, dt))

        bigA = sb("bigA", [128, 32768], BF16)
        bigB = sb("bigB", [128, 32768], BF16)
        bigC = sb("bigC", [128, 8192], BF16)
        stg = sb("stg", [128, 2048], F32)
        srow_f = sb("srow_f", [128, 1024], F32)
        xbuf = sb("xbuf", [128, 2048], F32)
        x2w2 = sb("x2w2", [128, 2048], F32)
        xm = sb("xm", [128, 1024], F32)
        stbuf = sb("stbuf", [128, 16], F32)
        outr = sb("outr", [128, 2048], F32)
        xlnb = sb("xlnb", [128, 1024], BF16)
        bcat = sb("bcat", [96, 1], F32)
        WUQ = sb("WUQ", [128, 1024], BF16)
        WUK = sb("WUK", [128, 1024], BF16)
        WUV = sb("WUV", [128, 1024], BF16)
        idtb = sb("idtb_s", [128, 128], BF16)
        idtf = sb("idtf_s", [128, 128], F32)
        onesb = sb("onesb", [128, 128], BF16)
        onesf = sb("onesf", [128, 128], F32)
        b1p = sb("b1p", [128, 32], F32)
        g2p = sb("g2p", [128, 32], F32)
        b2p = sb("b2p", [128, 32], F32)
        ppx = sb("ppx", [128, 8], F32)
        wabs = sb("wabs", [128, 64], F32)
        g1pp = sb("g1pp", [128, 8], F32)
        b1spp = sb("b1spp", [128, 8], F32)
        b1sbf = sb("b1sbf", [128, 8], BF16)
        b1tot = sb("b1tot", [128, 32], F32)
        b2b = sb("b2b", [128, 1024], F32)
        g1b = sb("g1b", [128, 1024], F32)
        zTw = sb("zTw", [128, 1024], BF16)
        gsq = sb("gsq", [128, 1024], BF16)

        Wcat = bigB[:, 31744:32512]
        WQR = bigB[:, 32512:32768]
        nhalf = srow_f[0:1, 0:512]
        Qc = bigB[:, 0:4096]
        Kc = bigB[:, 4096:8192]
        Vt = bigB[:, 8192:12288]
        Qrh = bigB[:, 12288:16384]
        cq = bigB[:, 16384:20480]
        ckv = bigB[:, 20480:24576]
        krh = bigB[:, 24576:28672]
        PT = bigB[:, 28672:30720]
        q2s = bigB[:, 30720:31744]
        WO = bigC[:, 0:8192]
        bO_b = x2w2[:, 1024:2048]
        mu2b = stg[:, 1024:1280].bitcast(BF16)
        rs2b = stg[:, 1280:1536].bitcast(BF16)

        psA = es.enter_context(nc.psum_tensor("psA", [128, 1024], F32))
        psB = es.enter_context(nc.psum_tensor("psB", [128, 1024], F32))
        psC = es.enter_context(nc.psum_tensor("psC", [128, 1024], F32))
        psD = es.enter_context(nc.psum_tensor("psD", [128, 1024], F32))

        sems = {k: es.enter_context(nc.semaphore(k))
                for k in ("dl", "dg", "dx0", "dx1", "ds0", "ds1",
                          "dw0", "dw1", "dw2", "dw3", "t", "v", "s", "bar")}
        p = Prog(sems)

        st_mu = stbuf[:, 0:1]
        st_u = stbuf[:, 1:2]
        st_s0 = stbuf[:, 2:3]
        st_r0 = stbuf[:, 3:4]
        st_t1 = stbuf[:, 4:5]
        st_rs = stbuf[:, 5:6]
        st_ss = stbuf[:, 6:7]

        def rsqrt_chain(u_ap, s0_ap, r0_ap, t1_ap, rs_ap):
            p.wait("scalar", "v")
            p.op("scalar", lambda e: e.activation(s0_ap, u_ap, ACT.Sqrt), inc="s")
            p.wait("vector", "s")
            p.op("vector", lambda e: e.reciprocal(rs_ap, s0_ap), inc="v")

        def ln_stats(xin_ap, ncols, dump=None):
            if dump is None:
                dump = stg[:, 0:ncols]
            p.op("vector", lambda e: e.reduce_sum(st_mu, xin_ap, AX))
            p.op("vector", lambda e: e.tensor_scalar(
                st_mu, st_mu, 1.0 / ncols, None, ALU.mult))
            p.op("vector", lambda e: e.tensor_scalar(
                xm[:, 0:ncols], xin_ap, st_mu, None, ALU.subtract), inc="v")
            p.wait("scalar", "v")
            p.op("scalar", lambda e, dump=dump: e.activation(
                dump, xm[:, 0:ncols], ACT.Square,
                accum_out=st_ss), inc="s")
            rt = p.op("scalar", lambda e, ncols=ncols: e.activation(
                st_s0, st_ss, ACT.Sqrt, scale=1.0 / ncols, bias=wabs[:, 63:64]), inc="s")
            p.wait("vector", "s", rt)
            p.op("vector", lambda e: e.reciprocal(st_rs, st_s0), inc="v")

        p.dma("sync", idtb[:, :], d["idtb"][:, :], "dl")
        p.dma("sync", idtf[:, :], d["idtf"][:, :], "dl")
        p.op("vector", lambda e: e.memset(ppx[:, :], 0.0))
        p.op("vector", lambda e: e.memset(wabs[:, 63:64], EPS))
        p.op("vector", lambda e: e.memset(bigB[32:64, 12288:28672], 0.0))
        p.op("vector", lambda e: e.memset(bigB[64:128, 12288:28672], 0.0))
        p.op("vector", lambda e: e.memset(bigB[32:64, 31232:31744], 0.0))
        p.op("vector", lambda e: e.memset(bigB[64:128, 31232:31744], 0.0))
        p.op("vector", lambda e: e.memset(bigB[32:64, 32512:32768], 0.0))
        p.op("vector", lambda e: e.memset(bigB[64:128, 32512:32768], 0.0))
        p.op("vector", lambda e: e.memset(WUQ[32:64, :], 0.0))
        p.op("vector", lambda e: e.memset(WUQ[64:128, :], 0.0))
        p.op("vector", lambda e: e.memset(WUK[32:64, :], 0.0))
        p.op("vector", lambda e: e.memset(WUK[64:128, :], 0.0))
        p.op("vector", lambda e: e.memset(WUV[32:64, :], 0.0))
        p.op("vector", lambda e: e.memset(WUV[64:128, :], 0.0))
        p.op("vector", lambda e: e.memset(srow_f[:, :], 0.0))
        p.op("vector", lambda e: e.memset(onesb[:, :], 1.0))
        p.op("vector", lambda e: e.memset(onesf[:, :], 1.0))
        p.op("vector", lambda e: e.memset(nhalf[:, :], -0.5))
        p.op("vector", lambda e: e.memset(cq[32:33, :], 1.0))
        p.op("vector", lambda e: e.memset(ckv[32:33, :], 1.0))
        p.op("vector", lambda e: e.memset(krh[32:33, :], 1.0), inc="v")

        for ci in range(8):
            p.dma("gpsimd", stg[:, ci * 32:(ci + 1) * 32],
                  d["W_DQ"][ci * 128:(ci + 1) * 128, :], "dg")
            p.dma("gpsimd", stg[:, 256 + ci * 32:256 + (ci + 1) * 32],
                  d["W_DKV"][ci * 128:(ci + 1) * 128, :], "dg")
            p.dma("gpsimd", stg[:, 512 + ci * 32:512 + (ci + 1) * 32],
                  d["W_KR"][ci * 128:(ci + 1) * 128, :], "dg")
        p.dma("gpsimd", bcat[0:32, :], d["b_DQ"].rearrange("(a b) -> a b", b=1), "dg")
        p.dma("gpsimd", bcat[32:64, :], d["b_DKV"].rearrange("(a b) -> a b", b=1), "dg")
        p.dma("gpsimd", bcat[64:96, :], d["b_KR"].rearrange("(a b) -> a b", b=1), "dg")
        p.wait("vector", "dg")
        for ci in range(8):
            for k, base in ((0, 0), (1, 256), (2, 512)):
                p.op("vector", lambda e, ci=ci, k=k, base=base: e.tensor_copy(
                    Wcat[:, ci * 96 + k * 32:ci * 96 + (k + 1) * 32],
                    stg[:, base + ci * 32:base + (ci + 1) * 32]))
        vfree = p.op("vector", lambda e: e.memset(stbuf[:, 15:16], 0.0),
                     inc="v")

        for wnm, bnm, dst, w in [("W_UQ", "b_UQ", WUQ, 1024),
                                 ("W_UK", "b_UK", WUK, 1024),
                                 ("W_UV", "b_UV", WUV, 1024),
                                 ("W_QR", "b_QR", WQR, 256)]:
            p.wait("gpsimd", "v", vfree)
            p.dma("gpsimd", stg[0:32, 0:w], d[wnm][:, :], "dg")
            p.dma("gpsimd", stg[32:33, 0:w],
                  d[bnm].rearrange("(a b) -> b a", b=1), "dg")
            p.wait("vector", "dg")
            vfree = p.op("vector", lambda e, dst=dst, w=w: e.tensor_copy(
                dst[0:33, 0:w], stg[0:33, 0:w]), inc="v")

        p.wait("tensor", "dl")
        for src, dst, nr in [("b1", b1p, 32), ("ln2_g", g2p, 32),
                             ("ln2_b", b2p, 32), ("ln1_g", g1pp, 8),
                             ("ln1_b", b1spp, 8)]:
            p.wait("gpsimd", "v", vfree)
            p.dma("gpsimd", stg[0:nr, 0:128],
                  d[src].rearrange("(a b) -> a b", b=128), "dg")
            p.wait("tensor", "dg")
            p.op("tensor", lambda e, nr=nr: e.matmul(
                psD[:, 0:nr], stg[0:nr, 0:128], idtf[0:nr, 0:nr],
                start=True, stop=True), inc="t")
            p.wait("vector", "t")
            vfree = p.op("vector", lambda e, dst=dst, nr=nr: e.tensor_copy(
                dst[:, 0:nr], psD[:, 0:nr]), inc="v")

        p.wait("tensor", "dl")
        xlT = bigC[:, 0:4096]
        xm_p = [xm[:, 0:1024], x2w2[:, 1024:2048]]
        dump_p = [stg[:, 0:1024], stg[:, 1024:2048]]
        xlnb_p = [xlnb[:, 0:1024], zTw[:, 0:1024].bitcast(BF16)]
        stb = [stbuf[:, 0:8], stbuf[:, 8:16]]
        v_xfree = {}
        v_sub = {}
        v_u = {}
        v_cp = {}
        s_sq = {}
        s_rt = {}
        s_vx = {}
        t_tr = {}
        t_dn = 0
        for i in range(NT + 1):
            j = i - 1
            if i < NT:
                par = i % 2
                dxs = "dx0" if par == 0 else "dx1"
                if i >= 2:
                    p.wait("sync", "v", v_xfree[i - 2])
                p.dma("sync", xbuf[:, par * 1024:(par + 1) * 1024],
                      d["x"][i * 128:(i + 1) * 128, :], dxs)
            if i < NT:
                par = i % 2
                sb_i = stb[par]
                dxs = "dx0" if par == 0 else "dx1"
                p.wait("vector", dxs)
                if i >= 2 and (i - 2) in s_vx:
                    p.wait("vector", "s", s_vx[i - 2])
                p.op("vector", lambda e, par=par, sb_i=sb_i: e.reduce_sum(
                    sb_i[:, 0:1], xbuf[:, par * 1024:(par + 1) * 1024], AX))
                p.op("vector", lambda e, sb_i=sb_i: e.tensor_scalar(
                    sb_i[:, 0:1], sb_i[:, 0:1], 1.0 / 1024, None, ALU.mult))
                v_sub[i] = p.op("vector", lambda e, par=par, sb_i=sb_i:
                                e.tensor_scalar(
                                    xm_p[par], xbuf[:, par * 1024:(par + 1) * 1024],
                                    sb_i[:, 0:1], None, ALU.subtract), inc="v")
                v_xfree[i] = v_sub[i]
                p.wait("scalar", "v", v_sub[i])
                p.op("scalar", lambda e, par=par, sb_i=sb_i:
                     e.activation(dump_p[par], xm_p[par], ACT.Square,
                                  accum_out=sb_i[:, 6:7]), inc="s")
                s_rt[i] = p.op("scalar", lambda e, sb_i=sb_i: e.activation(
                    sb_i[:, 2:3], sb_i[:, 6:7], ACT.Sqrt,
                    scale=1.0 / 1024, bias=wabs[:, 63:64]), inc="s")
            if j >= 0:
                pj = j % 2
                sb_j = stb[pj]
                p.wait("vector", "s", s_rt[j])
                vrj = p.op("vector", lambda e, sb_j=sb_j: e.reciprocal(
                    sb_j[:, 5:6], sb_j[:, 2:3]), inc="v")
                p.wait("scalar", "v", vrj)
                vx = p.op("scalar", lambda e, pj=pj, sb_j=sb_j: e.mul(
                    xlnb_p[pj], xm_p[pj], sb_j[:, 5:6]), inc="s")
                s_vx[j] = vx
                p.wait("tensor", "s", vx)
                if j >= 1:
                    p.wait("tensor", "v", v_cp[j - 1])
                if j % 4 == 0 and j >= 4:
                    p.wait("tensor", "t", t_dn)
                for c in range(8):
                    t_tr[j] = p.op("tensor", lambda e, c=c, pj=pj: e.matmul(
                        psD[:, c * 128:(c + 1) * 128],
                        xlnb_p[pj][:, c * 128:(c + 1) * 128], idtb[:, :],
                        start=True, stop=True),
                        inc="t" if c == 7 else None)
                p.wait("scalar", "t", t_tr[j])
                if j % 4 == 0 and j >= 4:
                    p.wait("scalar", "t", t_dn)
                jw = j % 4
                v_cp[j] = p.op("scalar", lambda e, jw=jw: e.copy(
                    bass.AP(bigC, jw * 128, [[8192, 128], [512, 8], [1, 128]]),
                    psD[:, :].rearrange("p (c w) -> p c w", c=8)), inc="s")
                if j % 4 == 3:
                    w = j // 4
                    p.wait("tensor", "s", v_cp[j])
                    for c in range(8):
                        t_dn = p.op("tensor", lambda e, c=c: e.matmul(
                            psC[0:96, 0:512], Wcat[:, c * 96:(c + 1) * 96],
                            xlT[:, c * 512:(c + 1) * 512],
                            start=(c == 0), stop=(c == 7)),
                            inc="t" if c == 7 else None)
                    p.wait("vector", "t", t_dn)
                    for r0, dst in ((0, cq), (32, ckv), (64, krh)):
                        p.op("vector", lambda e, r0=r0, dst=dst, w=w:
                             e.tensor_scalar(
                                 dst[0:32, w * 512:(w + 1) * 512],
                                 psC[r0:r0 + 32, 0:512], bcat[r0:r0 + 32, :],
                                 None, ALU.add), inc="v")
        if dbg:
            p.barrier()
            for nm, src in [("o_cq", cq), ("o_ckv", ckv), ("o_krh", krh)]:
                p.dma("gpsimd", dbg_t[nm][:, :], src[0:33, :], "dg")
        p.barrier()

        p1_vq = {}

        def emit_pass1(js):
            for j in js:
                sl = (j % 2) * 1024
                if j >= 2:
                    p.wait("sync", "v", p1_vq[j - 2])
                if j < 32:
                    srcap = d["w1"][j * 128:(j + 1) * 128, :]
                    col = j
                else:
                    i2, q2 = (j - 32) // 4, (j - 32) % 4
                    srcap = d["w2"][i2 * 128:(i2 + 1) * 128,
                                    q2 * 1024:(q2 + 1) * 1024]
                    col = 32 + (j - 32) // 2
                dxs = "dx0" if j % 2 == 0 else "dx1"
                p.dma("sync", stg[:, sl:sl + 1024], srcap, dxs)
                p.wait("vector", dxs)
                p.op("vector", lambda e, j=j, sl=sl, col=col: e.reduce_sum(
                    wabs[:, col:col + 1] if j < 32 else stbuf[:, 8:9],
                    stg[:, sl:sl + 1024], AX,
                    apply_absolute_value=True), inc="v")
                if j < 32:
                    p1_vq[j] = p.cnt["v"]
                elif (j - 32) % 2 == 0:
                    p1_vq[j] = p.op("vector", lambda e, col=col: e.tensor_copy(
                        wabs[:, col:col + 1], stbuf[:, 8:9]), inc="v")
                else:
                    p1_vq[j] = p.op("vector", lambda e, col=col: e.tensor_tensor(
                        wabs[:, col:col + 1], wabs[:, col:col + 1],
                        stbuf[:, 8:9], ALU.add), inc="v")

        gkc = 0
        smark = {}
        expmark = {}
        pvmark = {}
        evacmark = 0

        def emit_pv(kc_g, kc):
            p.wait("tensor", "s", expmark[kc_g])
            first = (kc == 0)
            last = (kc == 31)
            if first:
                p.wait("tensor", "v", evacmark)
            par = (kc_g % 2) * 1024
            for hf in range(2):
                p.op("tensor", lambda e, kc=kc, hf=hf, par=par, first=first,
                     last=last: e.matmul(
                         psC[:, hf * 512:(hf + 1) * 512],
                         Vt[:, kc * 128:(kc + 1) * 128],
                         PT[:, par + hf * 512:par + (hf + 1) * 512],
                         start=first, stop=last))
            for hf in range(2):
                pvmark[kc_g] = p.op(
                    "tensor", lambda e, hf=hf, par=par, first=first, last=last:
                    e.matmul(psD[:, hf * 512:(hf + 1) * 512], onesb[:, :],
                             PT[:, par + hf * 512:par + (hf + 1) * 512],
                             start=first, stop=last),
                    inc="t" if hf == 1 else None)

        deferred_norm = []
        p1_queue = list(range(64))
        for h in range(H):
            if gkc > 0:
                p.wait("tensor", "s", expmark[gkc - 1])
                p.wait("tensor", "v", evacmark)
            vm = {}
            sm = {}
            for di, (dst, wt, src) in enumerate(
                    ((Qc, WUQ, cq), (Kc, WUK, ckv))):
                for g4 in range(4):
                    k = di * 4 + g4
                    ps = psA if k % 2 == 0 else psB
                    if k >= 2:
                        p.wait("tensor", "v", vm.get(k - 2, 0))
                        p.wait("tensor", "s", sm.get(k - 2, 0))
                    for half in range(2):
                        tm = p.op("tensor", lambda e, wt=wt, src=src, g4=g4,
                                  ps=ps, h=h, half=half: e.matmul(
                                      ps[:, half * 512:(half + 1) * 512],
                                      wt[:, h * 128:(h + 1) * 128],
                                      src[:, g4 * 1024 + half * 512:
                                          g4 * 1024 + (half + 1) * 512],
                                      start=True, stop=True),
                                  inc="t" if half == 1 else None)
                    p.wait("vector", "t", tm)
                    vm[k] = p.op("vector", lambda e, dst=dst, g4=g4, ps=ps:
                                 e.tensor_copy(
                                     dst[:, g4 * 1024:g4 * 1024 + 512],
                                     ps[:, 0:512]), inc="v")
                    p.wait("scalar", "t", tm)
                    sm[k] = p.op("scalar", lambda e, dst=dst, g4=g4, ps=ps:
                                 e.copy(dst[:, g4 * 1024 + 512:(g4 + 1) * 1024],
                                        ps[:, 512:1024]), inc="s")
            for g8 in range(8):
                k = 8 + g8
                ps = psA if k % 2 == 0 else psB
                p.wait("tensor", "v", vm.get(k - 2, 0))
                p.wait("tensor", "s", sm.get(k - 2, 0))
                tm = p.op("tensor", lambda e, g8=g8, ps=ps, h=h: e.matmul(
                    ps[0:32, 0:512], WQR[:, h * 32:(h + 1) * 32],
                    cq[:, g8 * 512:(g8 + 1) * 512],
                    start=True, stop=True), inc="t")
                p.wait("vector", "t", tm)
                vm[k] = p.op("vector", lambda e, g8=g8, ps=ps: e.tensor_copy(
                    Qrh[0:32, g8 * 512:(g8 + 1) * 512], ps[0:32, 0:512]),
                    inc="v")
                sm[k] = sm.get(k - 1, 0)
            vmv = {}
            for vt in range(NT):
                slot = vt % 8
                ps = psA if slot < 4 else psB
                off = 512 + (slot % 4) * 128
                if vt >= 8 and vt % 4 == 0:
                    p.wait("tensor", "v", vmv[vt - 8])
                tm = p.op("tensor", lambda e, vt=vt, ps=ps, off=off, h=h:
                    e.matmul(ps[:, off:off + 128],
                             ckv[:, vt * 128:(vt + 1) * 128],
                             WUV[:, h * 128:(h + 1) * 128],
                             start=True, stop=True),
                    inc="t" if vt % 4 == 3 else None)
                if vt % 4 == 3:
                    p.wait("vector", "t", tm)
                    vmv[vt - 3] = p.op(
                        "vector", lambda e, vt=vt, ps=ps: e.tensor_copy(
                            Vt[:, (vt - 3) * 128:(vt + 1) * 128],
                            ps[:, 512:1024]), inc="v")
            for g8 in range(8):
                vmk = p.op("vector", lambda e, g8=g8: e.tensor_mul(
                    q2s[:, 0:512], Kc[:, g8 * 512:(g8 + 1) * 512],
                    Kc[:, g8 * 512:(g8 + 1) * 512]))
                vmk = p.op("vector", lambda e, g8=g8: e.tensor_mul(
                    q2s[0:32, 512:1024], krh[0:32, g8 * 512:(g8 + 1) * 512],
                    krh[0:32, g8 * 512:(g8 + 1) * 512]), inc="v")
                p.wait("tensor", "v", vmk)
                tm = p.op("tensor", lambda e: e.matmul(
                    psD[0:1, 0:512], onesb[:, 0:1], q2s[:, 0:512],
                    start=True, stop=False))
                tm = p.op("tensor", lambda e: e.matmul(
                    psD[0:1, 0:512], onesb[:, 0:1], q2s[:, 512:1024],
                    start=False, stop=True), inc="t")
                p.wait("vector", "t", tm)
                p.op("vector", lambda e, g8=g8: e.reduce_max(
                    stbuf[0:1, 8 + g8:9 + g8], psD[0:1, 0:512], AX), inc="v")
            p.op("vector", lambda e: e.reduce_max(
                stbuf[0:1, 7:8], stbuf[0:1, 8:16], AX), inc="v")
            for g8 in range(8):
                vmk = p.op("vector", lambda e, g8=g8: e.tensor_mul(
                    q2s[:, 0:512], Qc[:, g8 * 512:(g8 + 1) * 512],
                    Qc[:, g8 * 512:(g8 + 1) * 512]))
                vmk = p.op("vector", lambda e, g8=g8: e.tensor_mul(
                    q2s[0:32, 512:1024], Qrh[0:32, g8 * 512:(g8 + 1) * 512],
                    Qrh[0:32, g8 * 512:(g8 + 1) * 512]), inc="v")
                p.wait("tensor", "v", vmk)
                tm = p.op("tensor", lambda e: e.matmul(
                    psD[0:1, 0:512], onesb[:, 0:1], q2s[:, 0:512],
                    start=True, stop=False))
                tm = p.op("tensor", lambda e: e.matmul(
                    psD[0:1, 0:512], onesb[:, 0:1], q2s[:, 512:1024],
                    start=False, stop=True), inc="t")
                p.wait("vector", "t", tm)
                p.op("vector", lambda e, g8=g8: e.scalar_tensor_tensor(
                    Qrh[32:33, g8 * 512:(g8 + 1) * 512],
                    psD[0:1, 0:512], stbuf[0:1, 7:8], nhalf[:, 0:512],
                    ALU.add, ALU.mult), inc="v")
            prep_v = p.cnt["v"]
            if dbg and h == 0:
                p.barrier()
                for nm, src in [("o_qc", Qc), ("o_kc", Kc), ("o_vt", Vt),
                                ("o_qrh", Qrh[0:33, :])]:
                    p.dma("gpsimd", dbg_t[nm][:, :], src, "dg")
                p.barrier()

            for g in range(4):
                q0 = g * 1024
                for kc in range(32):
                    if deferred_norm and kc in (2, 3):
                        deferred_norm.pop(0)()
                    if p1_queue and kc in (6, 10, 14, 18, 22, 26):
                        emit_pass1([p1_queue.pop(0)])
                    if gkc >= 2:
                        p.wait("tensor", "s", expmark[gkc - 2])
                    if kc == 0 and g == 0:
                        p.wait("tensor", "v", prep_v)
                    ps = psA if (gkc % 2 == 0) else psB
                    for hf in range(2):
                        p.op("tensor", lambda e, kc=kc, hf=hf, ps=ps, q0=q0:
                             e.matmul(ps[:, hf * 512:(hf + 1) * 512],
                                      Kc[:, kc * 128:(kc + 1) * 128],
                                      Qc[:, q0 + hf * 512:q0 + (hf + 1) * 512],
                                      start=True, stop=False))
                        smark[gkc] = p.op(
                            "tensor", lambda e, kc=kc, hf=hf, ps=ps, q0=q0:
                            e.matmul(ps[:, hf * 512:(hf + 1) * 512],
                                     krh[:, kc * 128:(kc + 1) * 128],
                                     Qrh[:, q0 + hf * 512:q0 + (hf + 1) * 512],
                                     start=False, stop=True),
                            inc="t" if hf == 1 else None)
                    p.wait("scalar", "t", smark[gkc])
                    if gkc >= 2:
                        p.wait("scalar", "t", pvmark[gkc - 2])
                    par = (gkc % 2) * 1024
                    expmark[gkc] = p.op(
                        "scalar", lambda e, ps=ps, par=par: e.activation(
                            PT[:, par:par + 1024], ps[:, :], ACT.Exp,
                            scale=SCALE), inc="s")
                    if kc >= 1:
                        emit_pv(gkc - 1, kc - 1)
                    gkc += 1
                emit_pv(gkc - 1, 31)
                p.wait("vector", "t", pvmark[gkc - 1])
                p.op("vector", lambda e: e.tensor_copy(x2w2[:, 0:1024],
                                                       psD[:, :]))
                evacmark = p.op("vector", lambda e: e.tensor_copy(
                    outr[:, 0:1024], psC[:, :]), inc="v")

                def _dn1():
                    p.op("vector", lambda e: e.reciprocal(
                        x2w2[:, 1024:2048], x2w2[:, 0:1024]))

                def _dn2(h=h, q0=q0):
                    p.op("vector", lambda e, h=h, q0=q0: e.tensor_tensor(
                        bigA[:, h * 4096 + q0:h * 4096 + q0 + 1024],
                        outr[:, 0:1024], x2w2[:, 1024:2048], ALU.mult),
                        inc="v")
                deferred_norm += [_dn1, _dn2]
        for fn in deferred_norm:
            fn()
        deferred_norm = []
        if p1_queue:
            emit_pass1(list(p1_queue))
            p1_queue = []
        if dbg:
            p.barrier()
            p.dma("gpsimd", dbg_t["o_oT"][:, :], bigA[:, :], "dg")
        p.barrier()

        vfree = p.cnt["v"]
        for ci in range(4):
            if ci >= 1:
                p.wait("sync", "v", vfree)
            p.dma("sync", stg[:, 0:1024],
                  d["W_O"][(2 * ci) * 128:(2 * ci + 1) * 128, :], "dl")
            p.dma("sync", stg[:, 1024:2048],
                  d["W_O"][(2 * ci + 1) * 128:(2 * ci + 2) * 128, :], "dl")
            p.wait("vector", "dl")
            vfree = p.op("vector", lambda e, ci=ci: e.tensor_copy(
                WO[:, (2 * ci) * 1024:(2 * ci + 2) * 1024], stg[:, :]), inc="v")
        p.wait("sync", "v", vfree)
        p.dma("sync", srow_f[0:1, 0:1024],
              d["b_O"].rearrange("(a b) -> b a", b=1), "dl")
        p.wait("tensor", "dl")
        p.wait("tensor", "v", vfree)
        for hf in range(2):
            tm = p.op("tensor", lambda e, hf=hf: e.matmul(
                psC[:, hf * 512:(hf + 1) * 512], onesf[:, :],
                srow_f[:, hf * 512:(hf + 1) * 512], start=True, stop=True),
                inc="t" if hf == 1 else None)
        p.wait("vector", "t", tm)
        p.op("vector", lambda e: e.tensor_copy(bO_b[:, 0:1024], psC[:, :]),
             inc="v")
        for k, (c0, c1, n_el) in enumerate([(0, 32, HID * D), (32, 48, D * HID)]):
            p.op("vector", lambda e, c0=c0, c1=c1: e.reduce_sum(
                stbuf[:, 0:1], wabs[:, c0:c1], AX), inc="v")
            p.wait("tensor", "v")
            tm = p.op("tensor", lambda e: e.matmul(
                psB[0:1, 0:128], stbuf[:, 0:1], idtf[:, :],
                start=True, stop=True), inc="t")
            p.wait("vector", "t", tm)
            p.op("vector", lambda e: e.reduce_sum(
                srow_f[0:1, 0:1], psB[0:1, 0:128], AX))
            p.op("vector", lambda e, n_el=n_el: e.tensor_scalar(
                srow_f[0:1, 0:1], srow_f[0:1, 0:1], 1.0 / n_el, 1e-5,
                ALU.mult, ALU.max), inc="v")
            p.wait("tensor", "v")
            tm = p.op("tensor", lambda e: e.matmul(
                psB[:, 128:129], onesf[:, :], srow_f[:, 0:1],
                start=True, stop=True), inc="t")
            p.wait("vector", "t", tm)
            p.op("vector", lambda e, k=k: e.tensor_copy(
                ppx[:, 3 * k:3 * k + 1], psB[:, 128:129]))
            p.op("vector", lambda e, k=k: e.tensor_scalar(
                ppx[:, 3 * k + 1:3 * k + 2], ppx[:, 3 * k:3 * k + 1], 0.5,
                None, ALU.mult))
            p.op("vector", lambda e, k=k: e.tensor_scalar(
                ppx[:, 3 * k + 2:3 * k + 3], ppx[:, 3 * k:3 * k + 1], -0.5,
                None, ALU.mult), inc="v")
        p.wait("sync", "v")
        p.dma("sync", srow_f[0:1, 0:1024],
              d["ln1_g"].rearrange("(a b) -> b a", b=1), "dl")
        p.wait("tensor", "dl")
        for hf in range(2):
            tm = p.op("tensor", lambda e, hf=hf: e.matmul(
                psB[:, hf * 512:(hf + 1) * 512], onesf[:, :],
                srow_f[:, hf * 512:(hf + 1) * 512],
                start=True, stop=True), inc="t" if hf == 1 else None)
        p.wait("vector", "t", tm)
        p.op("vector", lambda e: e.tensor_scalar(
            g1b[:, :], psB[:, :], ppx[:, 0:1], None, ALU.mult), inc="v")
        p.op("vector", lambda e: e.tensor_scalar(
            g2p[:, :], g2p[:, :], ppx[:, 3:4], None, ALU.mult))
        p.op("vector", lambda e: e.tensor_scalar(
            b2p[:, :], b2p[:, :], ppx[:, 3:4], None, ALU.mult))
        p.op("vector", lambda e: e.tensor_scalar(
            b1spp[:, :], b1spp[:, :], ppx[:, 0:1], None, ALU.mult), inc="v")
        p.wait("sync", "v")
        p.dma("sync", srow_f[0:1, 0:1024],
              d["b2"].rearrange("(a b) -> b a", b=1), "dl")
        p.wait("tensor", "dl")
        for hf in range(2):
            tm = p.op("tensor", lambda e, hf=hf: e.matmul(
                psB[:, hf * 512:(hf + 1) * 512], onesf[:, :],
                srow_f[:, hf * 512:(hf + 1) * 512],
                start=True, stop=True), inc="t" if hf == 1 else None)
        p.wait("vector", "t", tm)
        p.op("vector", lambda e: e.tensor_copy(b2b[:, :], psB[:, :]), inc="v")
        p.barrier()

        p.op("vector", lambda e: e.reduce_sum(wabs[:, 62:63], bO_b[:, 0:1024],
                                              AX))
        p.op("vector", lambda e: e.tensor_scalar(
            wabs[:, 62:63], wabs[:, 62:63], 1.0 / 1024, None, ALU.mult),
            inc="v")

        def emit_w2_tile(jj):
            i2, q2 = jj // 4, jj % 4
            dws = "dw0" if jj % 2 == 0 else "dw1"
            sl = 0
            if jj >= 1:
                p.wait("sync", "v", w2q[jj - 1])
            p.dma("sync", stg[:, sl:sl + 1024],
                  d["w2"][i2 * 128:(i2 + 1) * 128,
                          q2 * 1024:(q2 + 1) * 1024], dws)
            p.wait("vector", dws)
            if jj >= 1:
                p.wait("vector", "t", w2t[jj - 1])
            p.op("vector", lambda e, sl=sl: e.tensor_scalar(
                xlnb[:, :], stg[:, sl:sl + 1024], ppx[:, 5:6], None,
                ALU.is_lt))
            zsl = 0
            w2q[jj] = p.op("vector", lambda e, sl=sl, zsl=zsl:
                           e.scalar_tensor_tensor(
                               zTw[:, zsl:zsl + 1024], stg[:, sl:sl + 1024],
                               ppx[:, 4:5], xlnb[:, :],
                               ALU.is_gt, ALU.subtract), inc="v")
            p.wait("tensor", "v", w2q[jj])
            if jj >= 1:
                p.wait("tensor", "s", w2c[jj - 1])
            for c in range(8):
                w2t[jj] = p.op("tensor", lambda e, c=c, zsl=zsl: e.matmul(
                    psD[:, c * 128:(c + 1) * 128],
                    zTw[:, zsl + c * 128:zsl + (c + 1) * 128], idtb[:, :],
                    start=True, stop=True), inc="t" if c == 7 else None)
            p.wait("scalar", "t", w2t[jj])
            dst = bass.AP(bigB, (q2 * 8) * 1024 + i2 * 128,
                          [[32768, 128], [1024, 8], [1, 128]])
            w2c[jj] = p.op("scalar", lambda e, dst=dst: e.copy(
                dst, psD[:, :].rearrange("p (c w) -> p c w", c=8)), inc="s")

        w2q, w2t, w2c = {}, {}, {}
        v_res = {}
        v_att = {}
        s_rt2 = {}
        xmC = [xm[:, 0:1024], stg[:, 1024:2048]]
        stbC = [stbuf[:, 0:8], stbuf[:, 8:16]]
        for i in range(NT + 1):
            j = i - 1
            if i < NT:
                par = i % 2
                dxs = "dx0" if par == 0 else "dx1"
                if i >= 2:
                    p.wait("sync", "v", v_res.get(i - 2, 0))
                p.dma("sync", xbuf[:, par * 1024:(par + 1) * 1024],
                      d["x"][i * 128:(i + 1) * 128, :], dxs)
                if i >= 1:
                    p.wait("tensor", "v", v_att[i - 1])
                for hf in range(2):
                    for c in range(8):
                        tm = p.op("tensor", lambda e, i=i, hf=hf, c=c: e.matmul(
                            psA[:, hf * 512:(hf + 1) * 512],
                            bigA[:, c * 4096 + i * 128:c * 4096 + (i + 1) * 128],
                            WO[:, c * 1024 + hf * 512:c * 1024 + (hf + 1) * 512],
                            start=(c == 0), stop=(c == 7)),
                            inc="t" if (hf == 1 and c == 7) else None)
                t_oproj = tm
            if j >= 0:
                pj = j % 2
                sbj = stbC[pj]
                dxsj = "dx0" if pj == 0 else "dx1"
                dssj = "ds0" if pj == 0 else "ds1"
                p.wait("vector", "s", s_rt2[j])
                p.op("vector", lambda e, sbj=sbj: e.reciprocal(
                    sbj[:, 5:6], sbj[:, 2:3]))
                p.wait("vector", dxsj)
                p.wait("vector", dssj)
                osl = pj * 1024
                v_res[j] = p.op("vector", lambda e, pj=pj, osl=osl, sbj=sbj:
                                e.scalar_tensor_tensor(
                                    outr[:, osl:osl + 1024], xmC[pj],
                                    sbj[:, 5:6],
                                    xbuf[:, pj * 1024:(pj + 1) * 1024],
                                    ALU.mult, ALU.add),
                                inc="v")
                p.wait("gpsimd", "v", v_res[j])
                p.dma("gpsimd", d["x2s"][j * 128:(j + 1) * 128, :],
                      outr[:, osl:osl + 1024], dssj)
            if i < NT:
                par = i % 2
                sbi = stbC[par]
                p.wait("vector", "t", t_oproj)
                p.op("vector", lambda e, sbi=sbi: e.reduce_sum(
                    sbi[:, 0:1], psA[:, :], AX))
                p.op("vector", lambda e, sbi=sbi: e.scalar_tensor_tensor(
                    sbi[:, 0:1], sbi[:, 0:1], 1.0 / 1024, stbuf[:, 9:10],
                    ALU.mult, ALU.add) if False else e.scalar_tensor_tensor(
                    sbi[:, 0:1], sbi[:, 0:1], 1.0 / 1024, wabs[:, 62:63],
                    ALU.mult, ALU.add))
                v_att[i] = p.op("vector", lambda e, par=par, sbi=sbi:
                                e.scalar_tensor_tensor(
                                    xmC[par], psA[:, :], sbi[:, 0:1],
                                    bO_b[:, 0:1024], ALU.subtract, ALU.add),
                                inc="v")
                p.wait("scalar", "v", v_att[i])
                p.op("scalar", lambda e, par=par, sbi=sbi: e.activation(
                    psB[:, 0:1024], xmC[par], ACT.Square,
                    accum_out=sbi[:, 6:7]), inc="s")
                s_rt2[i] = p.op("scalar", lambda e, sbi=sbi: e.activation(
                    sbi[:, 2:3], sbi[:, 6:7], ACT.Sqrt,
                    scale=1.0 / 1024, bias=wabs[:, 63:64]), inc="s")
            if j >= 2:
                emit_w2_tile(j - 2)
        emit_w2_tile(30)
        emit_w2_tile(31)
        ds_x2 = (p.cnt["ds0"], p.cnt["ds1"])
        if dbg:
            p.barrier()
            p.dma("gpsimd", dbg_t["o_x2"][:, :], d["x2s"][:, :], "dg")
        p.barrier()

        p.op("vector", lambda e: e.tensor_copy(b1sbf[:, :], b1spp[:, :]),
             inc="v")
        for hc in range(32):
            dstc = bass.AP(bigA.bitcast(FP8), 32768 + hc * 1024,
                           [[65536, 128], [1, 1024]])
            if hc % 2 == 0:
                p.op("vector", lambda e, hc=hc, dstc=dstc: e.tensor_copy(
                    dstc, bigB[:, hc * 1024:(hc + 1) * 1024]), inc="v")
            else:
                p.op("scalar", lambda e, hc=hc, dstc=dstc: e.copy(
                    dstc, bigB[:, hc * 1024:(hc + 1) * 1024]), inc="s")
        if dbg:
            p.barrier()
            p.dma("gpsimd", dbg_t["o_w2t"][:, :], bigA[:, 16384:32768], "dg")
            p.dma("gpsimd", dbg_t["o_ppx"][:, :], ppx[:, :], "dg")
        w2cv = p.cnt["v"]
        w2cs = p.cnt["s"]

        p.wait("sync", "ds0", ds_x2[0])
        p.wait("sync", "ds1", ds_x2[1])
        NWIN = 8
        TPW = 4
        w1T8 = bigA.bitcast(FP8)
        gbuf = bigB[:, 0:16384]
        zTbase = 16384
        xwodd = bigB.bitcast(F32)
        v_win = {}
        v_out = {-1: 0, 0: 0}
        v_zt = {}
        vz_of = {}
        gelc = {}
        t_mm1 = {}
        v_mm2evac = []

        def xw_tile(w, tt):
            if w % 2 == 0:
                return (xbuf if tt < 2 else x2w2)[:, (tt % 2) * 1024:
                                                  (tt % 2) * 1024 + 1024]
            return xwodd[:, 12288 + tt * 1024:12288 + (tt + 1) * 1024]

        def emit_pre_steps(w):
            dxs = "dx0" if w % 2 == 0 else "dx1"
            zsl = zTbase + (w % 2) * 4096
            sb8 = stbuf[:, 8:16]
            steps = []
            m = {}

            def s_load():
                if w == 1:
                    p.wait("sync", "v", w2cv)
                    p.wait("sync", "s", w2cs)
                if w >= 2:
                    p.wait("sync", "v", v_win[w - 2])
                for tt in range(TPW):
                    p.dma("sync", xw_tile(w, tt),
                          d["x2s"][w * 512 + tt * 128:
                                   w * 512 + (tt + 1) * 128, :], dxs)
            steps.append(s_load)

            def mk_ln_steps(get_in, tt, second):
                def s1():
                    if not second and tt == 0:
                        p.wait("vector", dxs)
                    p.op("vector", lambda e: e.reduce_sum(
                        sb8[:, 0:1], get_in(), AX))
                    p.op("vector", lambda e: e.tensor_scalar(
                        sb8[:, 0:1], sb8[:, 0:1], 1.0 / 1024, None, ALU.mult))
                    m["vv"] = p.op("vector", lambda e: e.tensor_scalar(
                        xm[:, 0:1024], get_in(), sb8[:, 0:1], None,
                        ALU.subtract), inc="v")

                def s2():
                    p.wait("scalar", "v", m["vv"])
                    p.op("scalar", lambda e: e.activation(
                        psB[:, :], xm[:, 0:1024], ACT.Square,
                        accum_out=sb8[:, 6:7]), inc="s")
                    m["rt"] = p.op("scalar", lambda e: e.activation(
                        sb8[:, 2:3], sb8[:, 6:7], ACT.Sqrt,
                        scale=1.0 / 1024, bias=wabs[:, 63:64]), inc="s")

                def s5():
                    p.wait("vector", "s", m["rt"])
                    p.op("vector", lambda e: e.reciprocal(
                        sb8[:, 5:6], sb8[:, 2:3]))
                    if not second:
                        p.op("vector", lambda e: e.tensor_scalar(
                            xm[:, 0:1024], xm[:, 0:1024], sb8[:, 5:6], None,
                            ALU.mult), inc="v")
                    else:
                        vz_of[(w, tt)] = m["vz"] = p.op(
                            "vector", lambda e, tt=tt: e.scalar_tensor_tensor(
                                zbuf[tt][:, :], xm[:, 0:1024], sb8[:, 5:6],
                                g1b[:, :], ALU.mult, ALU.mult), inc="v")
                return [s1, s2, s5]

            for tt in range(TPW):
                xa_fn = (lambda w=w, tt=tt: xw_tile(w, tt))
                xm_fn = (lambda: xm[:, 0:1024])
                steps += mk_ln_steps(xa_fn, tt, False)
                steps += mk_ln_steps(xm_fn, tt, True)
            return steps

        def emit_zt_batch(w):
            zsl = zTbase + (w % 2) * 4096
            cpm = {}
            tzm = {}
            for tt in range(TPW):
                ps = psD if tt % 2 == 0 else psC
                p.wait("tensor", "v", vz_of[(w, tt)])
                if tt >= 2:
                    p.wait("tensor", "v", cpm[tt - 2])
                if tt % 2 == 1 and v_mm2evac:
                    p.wait("tensor", "v", v_mm2evac[-1])
                for c in range(8):
                    tzm[tt] = p.op("tensor", lambda e, c=c, tt=tt, ps=ps:
                                   e.matmul(
                        ps[:, c * 128:(c + 1) * 128],
                        zbuf[tt][:, c * 128:(c + 1) * 128], idtb[:, :],
                        start=True, stop=True), inc="t" if c == 7 else None)
                p.wait("vector", "t", tzm[tt])
                cpm[tt] = p.op("vector", lambda e, tt=tt, ps=ps, zsl=zsl:
                               e.tensor_copy(
                    bass.AP(bigB, zsl + tt * 128,
                            [[32768, 128], [512, 8], [1, 128]]),
                    ps[:, :].rearrange("p (c w) -> p c w", c=8)), inc="v")
            v_zt[w] = cpm[TPW - 1]

        w1ev = {}
        b1ct = {}
        vq1 = {}
        tq1 = {}

        bigCf = bigC.bitcast(F32)
        xlnb2 = bigC[:, 4096:5120]
        zbuf = [xlnb, xlnb2, bigC[:, 5120:6144], bigC[:, 6144:7168]]
        W1SEMS = ("dw0", "dw1", "dw2", "dw3")

        def w1slot(j):
            s = j % 4
            if s < 2:
                return stg[:, s * 1024:(s + 1) * 1024]
            return bigCf[:, (s - 2) * 1024:(s - 1) * 1024]

        def emit_w1_dma(j):
            if j >= 4:
                p.wait("sync", "v", vq1[j - 4])
            elif j < 2:
                p.wait("sync", "v", w2q[31])
            p.dma("sync", w1slot(j), d["w1"][j * 128:(j + 1) * 128, :],
                  W1SEMS[j % 4])

        def emit_w1_quant(j):
            xl = xlnb if j % 2 == 0 else xlnb2
            p.wait("vector", W1SEMS[j % 4])
            if j >= 2:
                p.wait("vector", "t", tq1[j - 2])
            p.op("vector", lambda e, j=j: e.tensor_scalar(
                xm[:, 0:1024], w1slot(j), ppx[:, 2:3], None, ALU.is_lt))
            vq1[j] = p.op("vector", lambda e, j=j, xl=xl:
                          e.scalar_tensor_tensor(
                xl[:, :], w1slot(j), ppx[:, 1:2], xm[:, 0:1024],
                ALU.is_gt, ALU.subtract), inc="v")

        def emit_w1_tr(j):
            xl = xlnb if j % 2 == 0 else xlnb2
            p.wait("tensor", "v", vq1[j])
            if j >= 1:
                p.wait("tensor", "s", w1ev[j - 1])
            for c in range(8):
                tq1[j] = p.op("tensor", lambda e, c=c, xl=xl: e.matmul(
                    psD[:, c * 128:(c + 1) * 128],
                    xl[:, c * 128:(c + 1) * 128], idtb[:, :],
                    start=True, stop=True), inc="t" if c == 7 else None)
            p.wait("scalar", "t", tq1[j])
            dst = bass.AP(bigA.bitcast(FP8), j * 128,
                          [[65536, 128], [4096, 8], [1, 128]])
            w1ev[j] = p.op("scalar", lambda e, dst=dst: e.copy(
                dst, psD[:, :].rearrange("p (c w) -> p c w", c=8)), inc="s")

        def emit_b1col(j):
            p.wait("tensor", "s", w1ev[j])
            for dc in range(8):
                tb = p.op("tensor", lambda e, j=j, dc=dc: e.matmul(
                    psC[:, j:j + 1],
                    w1T8[:, dc * 4096 + j * 128:dc * 4096 + (j + 1) * 128],
                    b1sbf[:, dc:dc + 1],
                    start=(dc == 0), stop=(dc == 7)),
                    inc="t" if dc == 7 else None)
            p.wait("vector", "t", tb)
            b1ct[j] = p.op("vector", lambda e, j=j: e.tensor_tensor(
                b1tot[:, j:j + 1], psC[:, j:j + 1], b1p[:, j:j + 1],
                ALU.add), inc="v")

        for fn in emit_pre_steps(0):
            fn()
        emit_zt_batch(0)
        for j in range(4):
            emit_w1_dma(j)
        for j in range(3):
            emit_w1_quant(j)
            emit_w1_tr(j)
        emit_b1col(0)
        emit_b1col(1)
        for w in range(NWIN):
            zsl = zTbase + (w % 2) * 4096
            gel = gelc
            vgsq = {}
            tsum = {}
            pre_steps = emit_pre_steps(w + 1) if w + 1 < NWIN else []
            p.wait("tensor", "v", v_zt[w])
            for hc in range(33):
                if pre_steps and len(pre_steps) == 25:
                    pre_steps.pop(0)()
                if w == 0 and hc + 3 < 32:
                    emit_w1_quant(hc + 3)
                if hc < 32:
                    off = (hc % 2) * 512
                    if w == 0:
                        p.wait("tensor", "s", w1ev[hc])
                    km2 = (w, hc - 2) if hc >= 2 else (w - 1, 30 + hc)
                    if km2 in gel:
                        p.wait("tensor", "s", gel[km2])
                    for c in range(8):
                        tmm = p.op("tensor", lambda e, hc=hc, c=c, off=off,
                                   zsl=zsl: e.matmul(
                                       psA[:, off:off + 512],
                                       w1T8[:, c * 4096 + hc * 128:
                                            c * 4096 + (hc + 1) * 128],
                                       bigB[:, zsl + c * 512:zsl + (c + 1) * 512],
                                       start=(c == 0), stop=(c == 7)),
                                   inc="t" if c == 7 else None)
                    t_mm1[(w, hc)] = tmm
                    p.wait("scalar", "t", tmm)
                    if w == 0:
                        p.wait("scalar", "v", b1ct[hc])
                    gel[(w, hc)] = p.op(
                        "scalar", lambda e, hc=hc, off=off: e.activation(
                            gbuf[:, hc * 512:(hc + 1) * 512],
                            psA[:, off:off + 512],
                            ACT.Gelu, bias=b1tot[:, hc:hc + 1]), inc="s")
                    p.wait("vector", "s", gel[(w, hc)])
                    if hc >= 2:
                        p.wait("vector", "t", tsum[hc - 2])
                    vgsq[hc] = p.op("vector", lambda e, hc=hc: e.tensor_mul(
                        gsq[:, (hc % 2) * 512:(hc % 2) * 512 + 512],
                        gbuf[:, hc * 512:(hc + 1) * 512],
                        gbuf[:, hc * 512:(hc + 1) * 512]), inc="v")
                if hc >= 1:
                    hp = hc - 1
                    p.wait("tensor", "s", gel[(w, hp)])
                    p.op("tensor", lambda e, hp=hp: e.matmul(
                        psB[0:1, 0:512], onesb[:, 0:1],
                        gbuf[:, hp * 512:(hp + 1) * 512],
                        start=(hp == 0), stop=(hp == 31)))
                    p.wait("tensor", "v", vgsq[hp])
                    tsum[hp] = p.op("tensor", lambda e, hp=hp: e.matmul(
                        psB[0:1, 512:1024], onesb[:, 0:1],
                        gsq[:, (hp % 2) * 512:(hp % 2) * 512 + 512],
                        start=(hp == 0), stop=(hp == 31)),
                        inc="t")
                if w == 0:
                    if hc + 2 < 32:
                        emit_b1col(hc + 2)
                    if hc + 3 < 32:
                        emit_w1_tr(hc + 3)
                    if hc + 4 < 32:
                        emit_w1_dma(hc + 4)
            p.wait("vector", "t", tsum[31])
            p.op("vector", lambda e: e.tensor_scalar(
                srow_f[0:1, 0:512], psB[0:1, 0:512], 1.0 / HID, None,
                ALU.mult))
            p.op("vector", lambda e: e.tensor_mul(
                srow_f[0:1, 512:1024], srow_f[0:1, 0:512],
                srow_f[0:1, 0:512]))
            vv = p.op("vector", lambda e: e.scalar_tensor_tensor(
                srow_f[0:1, 512:1024], psB[0:1, 512:1024], 1.0 / HID,
                srow_f[0:1, 512:1024], ALU.mult, ALU.subtract), inc="v")
            p.wait("scalar", "v", vv)
            rt = p.op("scalar", lambda e: e.activation(
                srow_f[0:1, 512:1024], srow_f[0:1, 512:1024], ACT.Sqrt,
                bias=wabs[0:1, 63:64]), inc="s")
            p.wait("vector", "s", rt)
            p.op("vector", lambda e: e.reciprocal(
                srow_f[0:1, 512:1024], srow_f[0:1, 512:1024]))
            vrc = p.op("vector", lambda e: e.tensor_mul(
                srow_f[0:1, 0:512], srow_f[0:1, 0:512],
                srow_f[0:1, 512:1024]), inc="v")
            p.wait("tensor", "v", vrc)
            p.op("tensor", lambda e: e.matmul(
                psB[:, 0:512], onesf[:, :], srow_f[:, 0:512],
                start=True, stop=True))
            tm = p.op("tensor", lambda e: e.matmul(
                psB[:, 512:1024], onesf[:, :], srow_f[:, 512:1024],
                start=True, stop=True), inc="t")
            p.wait("vector", "t", tm)
            p.op("vector", lambda e: e.tensor_copy(mu2b[:, :], psB[:, 0:512]))
            p.op("vector", lambda e: e.tensor_copy(rs2b[:, :], psB[:, 512:1024]),
                 inc="v")
            vap = {}
            for half in range(2):
                h0 = half * 16
                gb3 = gbuf[:, h0 * 512:(h0 + 16) * 512].rearrange(
                    "p (c w) -> p c w", c=16)
                m2r_bc = bass.AP(stg.bitcast(BF16), 2048,
                                 [[4096, 128], [0, 16], [1, 512]])
                rs2_bc = bass.AP(stg.bitcast(BF16), 2560,
                                 [[4096, 128], [0, 16], [1, 512]])
                p.op("vector", lambda e, gb3=gb3, rs2_bc=rs2_bc:
                     e.tensor_tensor(gb3, gb3, rs2_bc, ALU.mult))
                p.op("vector", lambda e, gb3=gb3, m2r_bc=m2r_bc:
                     e.tensor_tensor(gb3, gb3, m2r_bc, ALU.subtract), inc="v")
                for hc in range(h0, h0 + 16):
                    vap[hc] = p.op("vector", lambda e, hc=hc: e.tensor_scalar(
                        gbuf[:, hc * 512:(hc + 1) * 512],
                        gbuf[:, hc * 512:(hc + 1) * 512],
                        g2p[:, hc:hc + 1], b2p[:, hc:hc + 1],
                        ALU.mult, ALU.add), inc="v")
            if dbg and w == 0:
                p.barrier()
                p.dma("gpsimd", dbg_t["o_h2"][:, :], gbuf[:, 0:8192], "dg")
                p.dma("gpsimd", dbg_t["o_w1t"][:, :], bigA[:, 0:16384], "dg")
                p.dma("gpsimd", dbg_t["o_b1t"][:, :], b1tot[:, :], "dg")
                p.barrier()
            def emit_final(w, tt):
                osl = (tt % 2) * 1024
                sb0 = stbuf[:, 0:8]
                p.op("vector", lambda e, osl=osl, sb0=sb0: e.reduce_sum(
                    sb0[:, 0:1], outr[:, osl:osl + 1024], AX))
                p.op("vector", lambda e, sb0=sb0: e.tensor_scalar(
                    sb0[:, 0:1], sb0[:, 0:1], 1.0 / 1024, None, ALU.mult))
                vv = p.op("vector", lambda e, osl=osl, sb0=sb0: e.tensor_scalar(
                    outr[:, osl:osl + 1024], outr[:, osl:osl + 1024],
                    sb0[:, 0:1], None, ALU.subtract), inc="v")
                p.wait("scalar", "v", vv)
                p.op("scalar", lambda e, osl=osl, sb0=sb0: e.activation(
                    psB[:, :], outr[:, osl:osl + 1024], ACT.Square,
                    accum_out=sb0[:, 6:7]), inc="s")
                rt = p.op("scalar", lambda e, sb0=sb0: e.activation(
                    sb0[:, 2:3], sb0[:, 6:7], ACT.Sqrt,
                    scale=1.0 / 1024, bias=wabs[:, 63:64]), inc="s")
                p.wait("vector", "s", rt)
                p.op("vector", lambda e, sb0=sb0: e.reciprocal(
                    sb0[:, 5:6], sb0[:, 2:3]))
                v_out[TPW * w + tt] = p.op(
                    "vector", lambda e, w=w, tt=tt, osl=osl, sb0=sb0:
                    e.scalar_tensor_tensor(
                        outr[:, osl:osl + 1024], outr[:, osl:osl + 1024],
                        sb0[:, 5:6], xw_tile(w, tt),
                        ALU.mult, ALU.add),
                    inc="v")
                p.wait("gpsimd", "v", v_out[TPW * w + tt])
                p.dma("gpsimd",
                      d["y"][(TPW * w + tt) * 128:(TPW * w + tt + 1) * 128, :],
                      outr[:, osl:osl + 1024],
                      "ds0" if tt % 2 == 0 else "ds1")

            for tt in range(TPW):
                if w == 0 and tt == 0:
                    p.wait("tensor", "v", w2cv)
                    p.wait("tensor", "s", w2cs)
                if v_mm2evac:
                    p.wait("tensor", "v", v_mm2evac[-1])
                for hf in range(2):
                    for hc in range(32):
                        if tt == 0 and hf == 0:
                            p.wait("tensor", "v", vap[hc])
                        tm = p.op("tensor", lambda e, tt=tt, hf=hf, hc=hc:
                                  e.matmul(
                                      psC[:, hf * 512:(hf + 1) * 512],
                                      gbuf[:, hc * 512 + tt * 128:
                                           hc * 512 + (tt + 1) * 128],
                                      w1T8[:, 32768 + hc * 1024 + hf * 512:
                                           32768 + hc * 1024 + (hf + 1) * 512],
                                      start=(hc == 0), stop=(hc == 31)),
                                  inc="t" if hc == 31 else None)
                p.wait("vector", "t", tm)
                osl = (tt % 2) * 1024
                p.wait("vector", "ds0" if tt % 2 == 0 else "ds1")
                for hf in range(2):
                    v_mev = p.op("vector", lambda e, hf=hf, osl=osl:
                                 e.tensor_tensor(
                                     outr[:, osl + hf * 512:osl + (hf + 1) * 512],
                                     psC[:, hf * 512:(hf + 1) * 512],
                                     b2b[:, hf * 512:(hf + 1) * 512], ALU.add),
                                 inc="v" if hf == 1 else None)
                v_mm2evac.append(v_mev)
                if tt >= 1:
                    emit_final(w, tt - 1)
                for _ in range(9):
                    if pre_steps:
                        pre_steps.pop(0)()
            if w + 1 < NWIN:
                emit_zt_batch(w + 1)
            emit_final(w, TPW - 1)
            for fn in pre_steps:
                fn()
            pre_steps = []
            v_win[w] = v_out[TPW * w + TPW - 1]
        p.wait("gpsimd", "ds0")
        p.wait("gpsimd", "ds1")
        p.wait("gpsimd", "dg")

        with nc.Block() as block:
            @block.sync
            def _(e):
                for fn in p.items["sync"]:
                    fn(e)

            @block.tensor
            def _(e):
                for fn in p.items["tensor"]:
                    fn(e)

            @block.vector
            def _(e):
                for fn in p.items["vector"]:
                    fn(e)

            @block.scalar
            def _(e):
                for fn in p.items["scalar"]:
                    fn(e)

            @block.gpsimd
            def _(e):
                for fn in p.items["gpsimd"]:
                    fn(e)

    return nc, dbg_t


_EYE_BF = np.eye(128, dtype=ml_dtypes.bfloat16)
_EYE_F32 = np.eye(128, dtype=np.float32)


def make_in_maps(inputs):
    x = np.ascontiguousarray(np.asarray(inputs["L0_attn_h_t"], dtype=np.float32))
    base = {}
    for nm in WNAMES:
        key = ("L0_attn_" + nm) if ("L0_attn_" + nm) in inputs else ("L0_ffn_" + nm)
        base[nm] = np.ascontiguousarray(np.asarray(inputs[key], dtype=np.float32))
    base["idtb"] = _EYE_BF
    base["idtf"] = _EYE_F32
    return [dict(base, x=x[b]) for b in range(8)]


def kernel(**inputs):
    nc, _ = build_program(dbg=False)
    in_maps = make_in_maps(inputs)
    res = run_bass_kernel_spmd(nc, in_maps, core_ids=list(range(8)))
    return np.stack([res.results[b]["y"] for b in range(8)], axis=0)
